# version 08
# baseline (speedup 1.0000x reference)
import math
from contextlib import ExitStack

import numpy as np

import concourse.bass as bass
import concourse.mybir as mybir
from concourse.bass_utils import run_bass_kernel_spmd

F32 = mybir.dt.float32
BF16 = mybir.dt.bfloat16
AF = mybir.ActivationFunctionType
ALU = mybir.AluOpType
AX = mybir.AxisListType

P = 128
H = 2048
NH = 16
LAT = 512
QHW = 1024
S = 2048
NTOK = 1024
NE = 8
FF = 1024
MAGIC = 1.5 * 2 ** 23
INV2PI = float(np.float32(1.0 / (2 * math.pi)))
TWOPI = float(np.float32(2 * math.pi))
SCL = 1.0 / math.sqrt(128.0)

SBUF_BASE = 17024
SBUF_TOP = 229344

P0, P1A, P1BK, P1BQ, P1C, PAA, PAB, PB, PC, PD = range(10)

TEN, VEC, SCA, SYN, GPS = "t", "v", "s", "y", "g"
_TAPS = False
_LAST_TAPS = None


class Alloc:

    def __init__(self, base, top):
        self.base, self.top = base, top
        self.reqs = []
        self.items = []
        self.offsets = {}

    def reserve(self, name, nbytes, birth, death):
        nbytes = (nbytes + 31) // 32 * 32
        self.reqs.append((nbytes, birth, death, name))

    def pin(self, name, rel_off):
        self.pins = getattr(self, "pins", {})
        self.pins[name] = self.base + rel_off

    def _try(self, order):
        pins = getattr(self, "pins", {})
        items, offsets = [], {}
        for nbytes, birth, death, name in order:
            if name in pins:
                start, end = pins[name], pins[name] + nbytes
                assert end <= self.top, f"pin {name} overflows"
                for (s, e, b, d, n) in items:
                    assert not (start < e and s < end and birth <= d and b <= death), \
                        f"pin {name} collides with {n}"
                items.append((start, end, birth, death, name))
                offsets[name] = start
                continue
            cands = sorted({self.base} | {e for (s, e, b, d, n) in items})
            placed = False
            for start in cands:
                end = start + nbytes
                if end > self.top:
                    continue
                if all(not (start < e and s < end and birth <= d and b <= death)
                       for (s, e, b, d, n) in items):
                    items.append((start, end, birth, death, name))
                    offsets[name] = start
                    placed = True
                    break
            if not placed:
                return None
        self.items, self.offsets = items, offsets
        return offsets

    def solve(self):
        pins = getattr(self, "pins", {})
        keys = [
            lambda r: (r[3] not in pins, -(r[2] - r[1]), -r[0]),
            lambda r: (r[3] not in pins, -r[0], r[1]),
            lambda r: (r[3] not in pins, r[1], -(r[2] - r[1]), -r[0]),
            lambda r: (r[3] not in pins, -r[0] * (r[2] - r[1] + 1)),
        ]
        for k in keys:
            if self._try(sorted(self.reqs, key=k)) is not None:
                return
        raise RuntimeError("SBUF overflow: no ordering found a packing")


class Sem:
    def __init__(self, h):
        self.h = h
        self.n = 0


class KB:

    def __init__(self):
        nc = bass.Bass("TRN2", target_bir_lowering=False)
        self.nc = nc
        self.st = ExitStack()
        self.streams = {k: [] for k in (TEN, VEC, SCA, SYN, GPS)}
        self.al = Alloc(SBUF_BASE, SBUF_TOP)
        self._pending = []

        d = {}
        def dram(name, shape, dt=F32, kind="ExternalInput"):
            d[name] = nc.dram_tensor(name, list(shape), dt, kind=kind)
        dram("x_full", [S, H]); dram("x_own", [NTOK, H]); dram("pos0", [P, 1])
        dram("wqdT", [H, LAT]); dram("wkvdT", [H, LAT])
        dram("wqupT", [LAT, QHW]); dram("wkupT", [LAT, QHW]); dram("wvupT", [LAT, H])
        dram("woT", [H, H])
        dram("bqd", [P, 4]); dram("bkvd", [P, 4]); dram("bqup", [P, 8]); dram("bkup", [P, 8])
        dram("bv", [1, H]); dram("bo", [1, H])
        dram("rmsw", [P, 16]); dram("router", [H, NE])
        dram("gate_r", [NE, 8, P, 16, P]); dram("up_r", [NE, 8, P, 16, P])
        dram("down_w", [NE, FF, H])
        dram("swg_r", [8, P, 16, P]); dram("swu_r", [8, P, 16, P]); dram("swd_w", [FF, H])
        dram("esel", [8, NE * P])
        dram("out", [NTOK, H], F32, kind="ExternalOutput")
        self.tap_tensors = {}
        d["aosp"] = nc.dram_tensor("aosp", [8, P, H], BF16, kind="Internal")
        self.d = d

        self.PS0 = self.st.enter_context(nc.psum_tensor("PS0", [P, 2048], F32))
        self.PS1 = self.st.enter_context(nc.psum_tensor("PS1", [P, 2048], F32))
        self.PS0b = self.PS0.bitcast(BF16)
        self.PS1b = self.PS1.bitcast(BF16)

        for s in ("tM", "vM", "sM", "gM", "bar"):
            setattr(self, s, Sem(self.st.enter_context(nc.semaphore(s))))
        self.dpool = {
            GPS: [Sem(self.st.enter_context(nc.semaphore(f"dW{i}"))) for i in range(12)],
            SYN: [Sem(self.st.enter_context(nc.semaphore(f"dX{i}"))) for i in range(6)],
        }
        self.dcnt = {GPS: 0, SYN: 0}
        self.dW = None
        self.dX = None

    def em(self, eng, fn):
        self.streams[eng].append(fn)

    def wait(self, eng, sem, val):
        if val and val > 0:
            self.em(eng, lambda e, s=sem, v=val: e.wait_ge(s.h, v))

    def inc(self, eng, sem, fn):
        sem.n += 1
        tgt = sem.n
        self.em(eng, lambda e, f=fn, s=sem: f(e).then_inc(s.h, 1))
        return tgt

    def chn(self, eng, sem, fn):
        t = self.inc(eng, sem, fn)
        self.wait(eng, sem, t)
        return t

    def dma(self, eng, sem_ignored, out, in_):
        pool = self.dpool[eng]
        slot = pool[self.dcnt[eng] % len(pool)]
        self.dcnt[eng] += 1
        if slot.n > 0:
            self.wait(eng, slot, slot.n)
        slot.n += 16
        tgt = slot.n
        self.em(eng, lambda e, o=out, i=in_, s=slot: e.dma_start(out=o, in_=i).then_inc(s.h, 16))
        return (slot, tgt)

    def waitt(self, eng, token):
        if token:
            self.wait(eng, token[0], token[1])

    def barrier(self):
        tgt = self.bar.n + 5
        for eng in (TEN, VEC, SCA, SYN, GPS):
            self.em(eng, lambda e: e.drain())
            self.em(eng, lambda e: e.nop().then_inc(self.bar.h, 1))
            self.em(eng, lambda e, v=tgt: e.wait_ge(self.bar.h, v))
        self.bar.n = tgt

    def tap(self, name, ap):
        if not _TAPS:
            return
        t = self.nc.dram_tensor("tap_" + name, list(ap.shape), ap.dtype, kind="ExternalOutput")
        self.tap_tensors[name] = t
        tgt = self.dma(SYN, self.dX, t[tuple(slice(None) for _ in ap.shape)], ap)
        self.waitt(SYN, tgt)

    def sb(self, name, shape, dt, birth, death):
        per_part = int(np.prod(shape[1:])) * mybir.dt.size(dt)
        self.al.reserve(name, per_part, birth, death)
        self._pending.append((name, list(shape), dt))
        return name

    def _materialize(self):
        self.al.solve()
        made = {}
        for name, shape, dt in self._pending:
            made[name] = self.nc.alloc_sbuf_tensor_at(name, shape, dt, offset=self.al.offsets[name])
        def fix(v):
            if isinstance(v, str) and v in made:
                return made[v]
            if isinstance(v, list):
                return [fix(x) for x in v]
            return v
        for k in list(self.__dict__):
            self.__dict__[k] = fix(self.__dict__[k])

    def alloc_all(self):
        sb = self.sb
        self.ident_bf = sb("ident_bf", [P, P], BF16, P0, PD)
        self.ident_f32 = sb("ident_f32", [P, P], F32, P0, PD)
        self.ones_bf = sb("ones_bf", [1, P], BF16, P0, PD)
        self.ones_f32 = sb("ones_f32", [1, P], F32, P0, PD)
        self.onec_bf = sb("onec_bf", [P, 1], BF16, P0, PD)
        self.bv_row = sb("bv_row", [1, H], BF16, P0, P1C)
        self.bo_row = sb("bo_row", [1, H], BF16, PAB, PB)
        self.bqd = sb("bqd", [P, 4], F32, P0, PD)
        self.bkvd = sb("bkvd", [P, 4], F32, P0, PD)
        self.bqup = sb("bqup", [P, 8], F32, P0, PD)
        self.bkup = sb("bkup", [P, 8], F32, P0, PD)
        self.router_sb = sb("router_sb", [P, P], BF16, P0, PD)
        self.rmsw = sb("rmsw", [P, 16], F32, P0, PD)
        self.pos0_sb = sb("pos0_sb", [P, 1], F32, P0, PD)
        self.ebias = sb("ebias", [P, 1], F32, P0, PD)
        self.stats6 = sb("stats6", [P, 24], F32, P0, PD)
        self.mv2 = sb("mv2", [P, 2], F32, P0, PD)
        for nm in ("sc_a", "sc_b", "sc_c", "sc_d", "sc_e", "sc_f"):
            setattr(self, nm, sb(nm, [P, 1], F32, P0, PD))
        self.c_pi2 = sb("c_pi2", [P, 1], F32, P0, PD)
        self.c_eps5 = sb("c_eps5", [P, 1], F32, P0, PD)
        self.c_eps6 = sb("c_eps6", [P, 1], F32, P0, PD)
        self.m8 = sb("m8", [P, 8], F32, P0, PD)
        self.probs_u = sb("probs_u", [P, 8], F32, P0, PD)
        self.maskb = sb("maskb", [P, 8], F32, P0, PD)
        self.cw = sb("cw", [P, 8], F32, P0, PD)
        self.s4 = [sb(f"s4_{i}", [P, 4], F32, P0, PD) for i in range(2)]
        self.r4 = [sb(f"r4_{i}", [P, 4], F32, P0, PD) for i in range(2)]
        self.nrow = sb("nrow", [1, 8], F32, P0, PD)
        self.brow = sb("brow", [1, 2], F32, P0, PD)

        self.W1 = sb("W1", [P, 16 * LAT], BF16, P0, P1A)
        self.W2 = sb("W2", [P, 16 * LAT], BF16, P0, P1A)
        self.W3 = sb("W3", [P, 4 * QHW], BF16, P0, P1BQ)
        self.W4 = sb("W4", [P, 4 * QHW], BF16, P0, P1BK)
        self.W5 = sb("W5", [P, 4 * H], BF16, P0, P1C)
        self.ck = [sb(f"ck{c}", [P, S], F32, P0, P1BK) for c in range(2)]
        self.sk = [sb(f"sk{c}", [P, S], F32, P0, P1BK) for c in range(2)]
        self.cq = [sb(f"cq{c}", [P, NTOK], F32, P0, P1BQ) for c in range(2)]
        self.sq = [sb(f"sq{c}", [P, NTOK], F32, P0, P1BQ) for c in range(2)]
        self.rtmp = sb("rtmp", [P, S], F32, P0, P0)
        self.rsc_scr = sb("rsc_scr", [P, S], F32, P0, P0)

        self.x_in = [sb(f"x_in{i}", [P, H], F32, P1A, P1A) for i in range(3)]
        self.x_ln = [sb(f"x_ln{i}", [P, H], BF16, P1A, P1A) for i in range(4)]
        self.x_lnT = [sb(f"x_lnT{k}", [P, 512], BF16, P1A, P1A) for k in range(16)]
        self.q_latT = [sb(f"q_latT{m}", [P, NTOK], BF16, P1A, P1BQ) for m in range(4)]
        self.kv_latT = [sb(f"kv_latT{m}", [P, S], BF16, P1A, P1C) for m in range(4)]
        self.khA = sb("khA", [P, S], F32, P1BK, P1BQ)
        self.khB = sb("khB", [P, S], F32, P1BK, P1BQ)
        self.rsc_a = sb("rsc_a", [P, 1024], F32, P1BK, P1BQ)
        self.rsc_b = sb("rsc_b", [P, 1024], F32, P1BK, P1BQ)

        self.K_h = [sb(f"K{h}", [P, S], BF16, P1BK, PAA if h < 8 else PAB) for h in range(16)]
        self.q_h = [sb(f"q{h}", [P, NTOK], BF16, P1BQ, PAA if h < 8 else PAB) for h in range(16)]
        self.V_h = [sb(f"V{h}", [P, 16 * 129], BF16, P1C, PAA if h < 8 else PAB) for h in range(16)]

        self.A_T = sb("A_T", [P, 2048], BF16, PAA, PAB)
        self.sqs = sb("sqs", [P, 1024], BF16, P1C, P1C)
        self.ao = [sb(f"ao{i}", [P, H], BF16, PAA, PAB) for i in range(8)]

        self.WoT = sb("WoT", [P, 16 * H], BF16, PAB, PB)
        self.acc = sb("acc", [P, 8 * H], F32, PB, PD)
        self.hT = [sb(f"hT{k}", [P, NTOK], BF16, PB, PC) for k in range(16)]
        self.aoT = [sb(f"aoT{i}", [P, H], BF16, PB, PB) for i in range(2)]
        self.ao_r = [sb(f"ao_r{i}", [P, H], BF16, PB, PB) for i in range(2)]
        self.x_ot = [sb(f"x_ot{i}", [P, H], F32, PB, PB) for i in range(1)]
        self.h_tok2b = [sb(f"h_tok{i}", [P, H], BF16, PB, PB) for i in range(2)]
        self.cT_sb = sb("cT_sb", [8, NTOK], F32, PB, PC)
        self.sel_sb = sb("sel_sb", [8, NE * P], F32, PB, PC)
        self.sc_a2 = sb("sc_a2", [P, 1], F32, P0, PD)
        self.sc_b2 = sb("sc_b2", [P, 1], F32, P0, PD)

        self.gbuf = [sb(f"gbuf{i}", [P, 2048], BF16, PC, PC) for i in range(2)]
        self.ubuf = [sb(f"ubuf{i}", [P, 2048], BF16, PC, PC) for i in range(2)]
        self.dbuf = sb("dbuf", [P, 8 * H], BF16, PC, PC)
        self.c_bc = [sb(f"c_bc{i}", [P, NTOK], F32, PC, PC) for i in range(2)]
        self.t1b = [sb(f"t1b{i}", [P, FF], F32, PC, PC) for i in range(2)]
        self.t2b = sb("t2b", [P, FF], F32, PC, PC)
        self.a_fm = [sb(f"a_fm{m}", [P, NTOK], BF16, PC, PC) for m in range(8)]

        off = 0
        for h in range(8):
            self.al.pin(f"K{h}", off); off += 4096
        for h in range(8):
            self.al.pin(f"q{h}", off); off += 2048
        for h in range(8):
            self.al.pin(f"V{h}", off); off += 4128
        self.al.pin("WoT", 0)
        g2 = off
        for h in range(8, 16):
            self.al.pin(f"K{h}", off); off += 4096
        for h in range(8, 16):
            self.al.pin(f"q{h}", off); off += 2048
        for h in range(8, 16):
            self.al.pin(f"V{h}", off); off += 4128
        self.al.pin("acc", g2)
        for k in range(16):
            self.al.pin(f"hT{k}", g2 + 65536 + k * 2048)

    def p0(self):
        em, wait, inc, dma = self.em, self.wait, self.inc, self.dma
        d = self.d
        g0 = inc(GPS, self.gM, lambda e: e.memset(self.ident_bf[:, :], 0.0))
        wait(GPS, self.gM, g0)
        em(GPS, lambda e: e.affine_select(out=self.ident_bf[:, :], in_=self.ident_bf[:, :],
                                          pattern=[[-1, P]], compare_op=ALU.not_equal,
                                          fill=1.0, base=0, channel_multiplier=1))
        g1 = inc(GPS, self.gM, lambda e: e.memset(self.ident_f32[:, :], 0.0))
        wait(GPS, self.gM, g1)
        em(GPS, lambda e: e.affine_select(out=self.ident_f32[:, :], in_=self.ident_f32[:, :],
                                          pattern=[[-1, P]], compare_op=ALU.not_equal,
                                          fill=1.0, base=0, channel_multiplier=1))
        em(GPS, lambda e: e.memset(self.ones_bf[:, :], 1.0))
        em(GPS, lambda e: e.memset(self.ones_f32[:, :], 1.0))
        em(GPS, lambda e: e.memset(self.onec_bf[:, :], 1.0))
        em(GPS, lambda e: e.memset(self.c_pi2[:, :], math.pi / 2))
        em(GPS, lambda e: e.memset(self.c_eps5[:, :], 1e-5))
        em(GPS, lambda e: e.memset(self.c_eps6[:, :], 1e-6))
        em(GPS, lambda e: e.iota(self.rtmp[:, :], [[1, S]], base=0, channel_multiplier=0,
                                 allow_small_or_imprecise_dtypes=True))
        em(GPS, lambda e: e.iota(self.sc_c[:, :], [[1, 1]], base=0, channel_multiplier=1,
                                 allow_small_or_imprecise_dtypes=True))
        self.g_const = inc(GPS, self.gM, lambda e: e.iota(
            self.sc_d[:, :], [[1, 1]], base=P, channel_multiplier=1,
            allow_small_or_imprecise_dtypes=True))

        self.tok_bqd = dma(GPS, self.dW, self.bqd[:, :], d["bqd"][:, :])
        self.tok_bkvd = dma(GPS, self.dW, self.bkvd[:, :], d["bkvd"][:, :])
        self.tok_bqup = dma(GPS, self.dW, self.bqup[:, :], d["bqup"][:, :])
        self.tok_bkup = dma(GPS, self.dW, self.bkup[:, :], d["bkup"][:, :])
        self.tok_bv = dma(GPS, self.dW, self.bv_row[0:1, :], d["bv"][:, :])
        self.tok_rmsw = dma(GPS, self.dW, self.rmsw[:, :], d["rmsw"][:, :])
        self.tok_pos0 = dma(GPS, self.dW, self.pos0_sb[:, :], d["pos0"][:, :])
        self.tok_router = dma(GPS, self.dW,
                              self.router_sb[:, :].rearrange("p (k e) -> p k e", e=NE),
                              d["router"].rearrange("(k p) e -> p k e", p=P))
        self.tok_w1 = dma(GPS, self.dW, self.W1[:, :].rearrange("p (k m) -> p k m", m=LAT),
                          d["wqdT"].rearrange("(k p) m -> p k m", p=P))
        self.tok_w2 = dma(GPS, self.dW, self.W2[:, :].rearrange("p (k m) -> p k m", m=LAT),
                          d["wkvdT"].rearrange("(k p) m -> p k m", p=P))
        self.tok_w3 = dma(GPS, self.dW, self.W3[:, :].rearrange("p (k m) -> p k m", m=QHW),
                          d["wqupT"].rearrange("(k p) m -> p k m", p=P))
        self.tok_w4 = dma(GPS, self.dW, self.W4[:, :].rearrange("p (k m) -> p k m", m=QHW),
                          d["wkupT"].rearrange("(k p) m -> p k m", p=P))
        self.tok_w5 = dma(GPS, self.dW, self.W5[:, :].rearrange("p (k m) -> p k m", m=H),
                          d["wvupT"].rearrange("(k p) m -> p k m", p=P))

        wait(SCA, self.gM, self.g_const)
        LNB = math.log(10000.0) / 256.0
        self.chn(SCA, self.sM, lambda e: e.activation(self.sc_c[:, :], self.sc_c[:, :], AF.Exp, scale=-LNB))
        s_ifr = self.chn(SCA, self.sM, lambda e: e.activation(self.sc_d[:, :], self.sc_d[:, :], AF.Exp, scale=-LNB))

        wait(VEC, self.gM, self.g_const)
        self.waitt(VEC, self.tok_pos0)
        wait(VEC, self.sM, s_ifr)

        def tables(cos_t, sin_t, n, use_pos0):
            chn = self.chn
            for c in range(2):
                ifr = self.sc_c if c == 0 else self.sc_d
                th = self.rsc_scr
                if use_pos0:
                    chn(VEC, self.vM, lambda e, n=n: e.tensor_scalar(th[:, :n], self.rtmp[:, :n],
                                                                     self.pos0_sb[:, 0:1], None, ALU.add))
                    chn(VEC, self.vM, lambda e, n=n, ifr=ifr: e.tensor_scalar(th[:, :n], th[:, :n],
                                                                              ifr[:, 0:1], None, ALU.mult))
                else:
                    chn(VEC, self.vM, lambda e, n=n, ifr=ifr: e.tensor_scalar(th[:, :n], self.rtmp[:, :n],
                                                                              ifr[:, 0:1], None, ALU.mult))
                chn(VEC, self.vM, lambda e, n=n, c=c: e.tensor_scalar(sin_t[c][:, :n], th[:, :n], INV2PI, None, ALU.mult))
                chn(VEC, self.vM, lambda e, n=n, c=c: e.tensor_scalar(sin_t[c][:, :n], sin_t[c][:, :n],
                                                                      MAGIC, -MAGIC, ALU.add, ALU.add))
                chn(VEC, self.vM, lambda e, n=n, c=c:
                    e.scalar_tensor_tensor(sin_t[c][:, :n], sin_t[c][:, :n], -TWOPI,
                                           th[:, :n], ALU.mult, ALU.add))
                v_s = chn(VEC, self.vM, lambda e, n=n, c=c: e.tensor_scalar(
                    sin_t[c][:, :n], sin_t[c][:, :n], 3.141591, -3.141591, ALU.min, ALU.max))
                chn(VEC, self.vM, lambda e, n=n, c=c: e.tensor_scalar(cos_t[c][:, :n], th[:, :n],
                                                                      INV2PI, 0.25, ALU.mult, ALU.add))
                chn(VEC, self.vM, lambda e, n=n, c=c: e.tensor_scalar(cos_t[c][:, :n], cos_t[c][:, :n],
                                                                      MAGIC, -MAGIC, ALU.add, ALU.add))
                chn(VEC, self.vM, lambda e, n=n, c=c:
                    e.scalar_tensor_tensor(cos_t[c][:, :n], cos_t[c][:, :n], -TWOPI,
                                           th[:, :n], ALU.mult, ALU.add))
                v_c = chn(VEC, self.vM, lambda e, n=n, c=c: e.tensor_scalar(
                    cos_t[c][:, :n], cos_t[c][:, :n], 1.570795, -4.712387, ALU.min, ALU.max))
                wait(SCA, self.vM, v_s)
                chn(SCA, self.sM, lambda e, n=n, c=c: e.activation(sin_t[c][:, :n], sin_t[c][:, :n], AF.Sin))
                wait(SCA, self.vM, v_c)
                chn(SCA, self.sM, lambda e, n=n, c=c: e.activation(cos_t[c][:, :n], cos_t[c][:, :n],
                                                                   AF.Sin, bias=self.c_pi2[:, 0:1]))

        tables(self.ck, self.sk, S, False)
        tables(self.cq, self.sq, NTOK, True)
        self.s_tables = inc(SCA, self.sM, lambda e: e.nop())

    def p1a(self):
        em, wait, inc, dma = self.em, self.wait, self.inc, self.dma
        parts = globals().get("_P1A_PARTS", 3)
        d = self.d
        ln_s = {}
        xin_free = []
        self._p1a_prev_ev = {}
        trev = {}
        latev = {}
        ttr_last = {}
        tlat_last = {}

        def emit_ln_chunk(g, tc):
            src_d = d["x_full"] if g < 4 else d["x_own"]
            row0 = (g % 4) * 512 if g < 4 else (g - 4) * 512
            if True:
                buf = self.x_in[(g * 4 + tc) % 3]
                if len(xin_free) >= 3:
                    wait(SYN, self.sM, xin_free[-3])
                if g >= 1:
                    wait(SCA, self.tM, ttr_last[(g - 1, tc)])
                dtgt = dma(SYN, self.dX, buf[:, :], src_d[row0 + tc * P: row0 + (tc + 1) * P, :])
                self.waitt(VEC, dtgt)
                for gi in range(4):
                    self.chn(VEC, self.vM, lambda e, buf=buf, gi=gi: e.bn_stats(
                        self.stats6[:, gi * 6:(gi + 1) * 6],
                        buf[:, gi * 512:(gi + 1) * 512]))
                v_mv = self.chn(VEC, self.vM, lambda e: e.bn_aggr(
                    self.mv2[:, :], self.stats6[:, :].rearrange("p (g x) -> p g x", x=3)))
                wait(SCA, self.vM, v_mv)
                self.chn(SCA, self.sM, lambda e: e.activation(self.sc_a[:, :], self.mv2[:, 1:2], AF.Ln, bias=self.c_eps5[:, 0:1]))
                s_r = self.chn(SCA, self.sM, lambda e: e.activation(self.sc_a[:, :], self.sc_a[:, :], AF.Exp, scale=-0.5))
                wait(VEC, self.sM, s_r)
                v_b = self.chn(VEC, self.vM, lambda e: e.scalar_tensor_tensor(
                    self.sc_b[:, :], self.mv2[:, 0:1], -1.0, self.sc_a[:, :], ALU.mult, ALU.mult))
                wait(SCA, self.vM, v_b)
                s_x = self.chn(SCA, self.sM, lambda e, buf=buf, o=self.x_ln[tc]:
                               e.activation(o[:, :], buf[:, :], AF.Identity,
                                            bias=self.sc_b[:, 0:1], scale=self.sc_a[:, 0:1]))
                ln_s[(g, tc)] = s_x
                xin_free.append(s_x)

        def emit_trans_chunk(g, tc):
            if g > 0 and tc == 0:
                wait(TEN, self.sM, latev[(g - 1, 3)])
                wait(VEC, self.tM, tlat_last[g - 1])
            prev_ev = self._p1a_prev_ev
            if True:
                wait(TEN, self.sM, ln_s[(g, tc)])
                for wave in range(4):
                    pv = self.PS0b if wave % 2 == 0 else self.PS1b
                    if (tc * 4 + wave) % 2 in prev_ev:
                        wait(TEN, self.vM, prev_ev[(tc * 4 + wave) % 2])
                    for kk in range(3):
                        k = wave * 4 + kk
                        em(TEN, lambda e, k=k, tc=tc, pv=pv, bank=kk:
                           e.transpose(pv[:, bank * 1024: bank * 1024 + P],
                                       self.x_ln[tc][:, k * P:(k + 1) * P], self.ident_bf[:, :]))
                    k = wave * 4 + 3
                    t_tr = inc(TEN, self.tM, lambda e, k=k, tc=tc, pv=pv:
                               e.transpose(pv[:, 3 * 1024: 3 * 1024 + P],
                                           self.x_ln[tc][:, k * P:(k + 1) * P], self.ident_bf[:, :]))
                    wait(VEC, self.tM, t_tr)
                    last = None
                    for kk in range(4):
                        k = wave * 4 + kk
                        last = inc(VEC, self.vM, lambda e, k=k, tc=tc, pv=pv, kk=kk: e.tensor_copy(
                            self.x_lnT[k][:, tc * P:(tc + 1) * P], pv[:, kk * 1024: kk * 1024 + P]))
                    prev_ev[(tc * 4 + wave) % 2] = last
            trev[(g, 15)] = last
            ttr_last[(g, tc)] = t_tr

        def emit_latents(g):
            if g < 4:
                Wt, bias_t, outs, col0 = self.W2, self.bkvd, self.kv_latT, (g % 4) * 512
                self.waitt(TEN, self.tok_w2)
                self.waitt(SCA, self.tok_bkvd)
            else:
                Wt, bias_t, outs, col0 = self.W1, self.bqd, self.q_latT, (g - 4) * 512
                self.waitt(TEN, self.tok_w1)
                self.waitt(SCA, self.tok_bqd)
            wait(TEN, self.vM, trev[(g, 15)])
            for m in range(4):
                po = (m % 2) * 512
                if m >= 2:
                    wait(TEN, self.sM, latev[(g, m - 2)])
                for k in range(15):
                    em(TEN, lambda e, k=k, m=m, po=po, Wt=Wt:
                       e.matmul(self.PS0[:, po:po + 512],
                                Wt[:, k * LAT + m * P: k * LAT + (m + 1) * P],
                                self.x_lnT[k][:, :], start=(k == 0), stop=False))
                t_l = inc(TEN, self.tM, lambda e, m=m, po=po, Wt=Wt:
                          e.matmul(self.PS0[:, po:po + 512],
                                   Wt[:, 15 * LAT + m * P: 15 * LAT + (m + 1) * P],
                                   self.x_lnT[15][:, :], start=False, stop=True))
                wait(SCA, self.tM, t_l)
                latev[(g, m)] = self.chn(SCA, self.sM, lambda e, m=m, po=po, bias_t=bias_t, outs=outs, col0=col0:
                                         e.activation(outs[m][:, col0:col0 + 512], self.PS0[:, po:po + 512],
                                                      AF.Identity, bias=bias_t[:, m:m + 1]))
            tlat_last[g] = t_l

        for tc in range(4):
            emit_ln_chunk(0, tc)
            emit_trans_chunk(0, tc)
        for g in range(6):
            if g + 1 < 6:
                for tc in range(4):
                    emit_ln_chunk(g + 1, tc)
            emit_latents(g)
            if g + 1 < 6:
                for tc in range(4):
                    emit_trans_chunk(g + 1, tc)
        self.latev_last = latev[(5, 3)]

    def build_side(self, n, latT, Wup, bias_t, cos_t, sin_t, outK, w_req):
        em, wait, inc = self.em, self.wait, self.inc
        self.waitt(SCA, self.tok_bkup if bias_t is self.bkup else self.tok_bqup)
        nsl_cnt = n // 512
        prev_rope = None
        prev_casts = None
        for j in range(4):
            evA = evB = None
            for which in range(2):
                jj = j + 4 * which
                ps = self.PS0 if which == 0 else self.PS1
                dst = self.khA if which == 0 else self.khB
                self.waitt(TEN, w_req)
                if prev_rope is not None and which == 0:
                    wait(SCA, self.vM, prev_rope)
                for k in range(4):
                    for nsl in range(nsl_cnt):
                        if k == 3 and nsl == nsl_cnt - 1:
                            continue
                        em(TEN, lambda e, k=k, nsl=nsl, jj=jj, ps=ps, latT=latT, Wup=Wup:
                           e.matmul(ps[:, nsl * 512:(nsl + 1) * 512],
                                    Wup[:, k * QHW + jj * P: k * QHW + (jj + 1) * P],
                                    latT[k][:, nsl * 512:(nsl + 1) * 512],
                                    start=(k == 0), stop=(k == 3)))
                t_kh = inc(TEN, self.tM, lambda e, jj=jj, ps=ps, latT=latT, Wup=Wup, nsl=nsl_cnt - 1:
                           e.matmul(ps[:, nsl * 512:(nsl + 1) * 512],
                                    Wup[:, 3 * QHW + jj * P: 3 * QHW + (jj + 1) * P],
                                    latT[3][:, nsl * 512:(nsl + 1) * 512],
                                    start=False, stop=True))
                wait(SCA, self.tM, t_kh)
                ev = self.chn(SCA, self.sM, lambda e, dst=dst, jj=jj, ps=ps, n=n, bias_t=bias_t:
                              e.activation(dst[:, :n], ps[:, :n], AF.Identity, bias=bias_t[:, jj:jj + 1]))
                if which == 0:
                    evA = ev
                else:
                    evB = ev
            s_c1 = inc(SCA, self.sM, lambda e, j=j, n=n, outK=outK:
                       e.activation(outK[j][:, :n], self.khA[:, :n], AF.Identity))
            s_c2 = inc(SCA, self.sM, lambda e, j=j, n=n, outK=outK:
                       e.activation(outK[j + 4][:, :n], self.khB[:, :n], AF.Identity))
            c_t, s_t = cos_t[j % 2], sin_t[j % 2]
            wait(VEC, self.sM, evB)
            for h0 in range(0, n, 1024):
                h1 = h0 + 1024
                self.chn(VEC, self.vM, lambda e, h0=h0, h1=h1, c_t=c_t: e.tensor_mul(
                    self.rsc_a[:, :], self.khA[:, h0:h1], c_t[:, h0:h1]))
                self.chn(VEC, self.vM, lambda e, h0=h0, h1=h1, s_t=s_t: e.tensor_mul(
                    self.rsc_b[:, :], self.khB[:, h0:h1], s_t[:, h0:h1]))
                self.chn(VEC, self.vM, lambda e, j=j, h0=h0, h1=h1, outK=outK: e.tensor_sub(
                    outK[8 + j][:, h0:h1], self.rsc_a[:, :], self.rsc_b[:, :]))
                self.chn(VEC, self.vM, lambda e, h0=h0, h1=h1, s_t=s_t: e.tensor_mul(
                    self.rsc_a[:, :], self.khA[:, h0:h1], s_t[:, h0:h1]))
                self.chn(VEC, self.vM, lambda e, h0=h0, h1=h1, c_t=c_t: e.tensor_mul(
                    self.rsc_b[:, :], self.khB[:, h0:h1], c_t[:, h0:h1]))
                prev_rope = self.chn(VEC, self.vM, lambda e, j=j, h0=h0, h1=h1, outK=outK:
                                e.tensor_add(outK[12 + j][:, h0:h1], self.rsc_a[:, :], self.rsc_b[:, :]))
            prev_casts = s_c2
            wait(TEN, self.sM, prev_casts)

    def p1c(self):
        em, wait, inc, dma = self.em, self.wait, self.inc, self.dma
        for h in range(15):
            em(GPS, lambda e, h=h: e.memset(
                self.V_h[h][:, :].rearrange("p (kc x) -> p kc x", x=129)[:, :, 128:129], 1.0))
        g_v1 = inc(GPS, self.gM, lambda e: e.memset(
            self.V_h[15][:, :].rearrange("p (kc x) -> p kc x", x=129)[:, :, 128:129], 1.0))
        wait(TEN, self.gM, g_v1)
        self.waitt(TEN, self.tok_w5)
        self.waitt(TEN, self.tok_bv)
        vev = {}
        for tc in range(16):
            ps = self.PS0 if tc % 2 == 0 else self.PS1
            if tc >= 2:
                wait(TEN, self.sM, vev[tc - 2])
            for nsl in range(4):
                for k in range(4):
                    em(TEN, lambda e, k=k, nsl=nsl, tc=tc, ps=ps:
                       e.matmul(ps[:, nsl * 512:(nsl + 1) * 512],
                                self.kv_latT[k][:, tc * P:(tc + 1) * P],
                                self.W5[:, k * H + nsl * 512: k * H + (nsl + 1) * 512],
                                start=(k == 0), stop=False))
                if nsl < 3:
                    em(TEN, lambda e, nsl=nsl, ps=ps:
                       e.matmul(ps[:, nsl * 512:(nsl + 1) * 512], self.ones_bf[0:1, :],
                                self.bv_row[0:1, nsl * 512:(nsl + 1) * 512], start=False, stop=True))
            t_v = inc(TEN, self.tM, lambda e, ps=ps:
                      e.matmul(ps[:, 3 * 512:4 * 512], self.ones_bf[0:1, :],
                               self.bv_row[0:1, 3 * 512:4 * 512], start=False, stop=True))
            wait(SCA, self.tM, t_v)
            last = None
            for h in range(16):
                last = inc(SCA, self.sM, lambda e, h=h, tc=tc, ps=ps:
                           e.activation(self.V_h[h][:, tc * 129: tc * 129 + 128],
                                        ps[:, h * P:(h + 1) * P], AF.Identity))
            vev[tc] = last
        self.v_done = vev[15]

    def bound_prep(self):
        em, wait, inc = self.em, self.wait, self.inc
        wait(TEN, self.sM, self.v_done)
        wait(VEC, self.sM, self.v_done)
        t_sq = None
        for h in range(16):
            for half in range(2):
                if t_sq is not None:
                    wait(VEC, self.tM, t_sq)
                v_s = self.chn(VEC, self.vM, lambda e, h=h, half=half: e.tensor_mul(
                    self.sqs[:, :], self.K_h[h][:, half * 1024:(half + 1) * 1024],
                    self.K_h[h][:, half * 1024:(half + 1) * 1024]))
                wait(TEN, self.vM, v_s)
                em(TEN, lambda e, h=h, half=half:
                   e.matmul(self.PS1[0:1, (half * 2) * 512:(half * 2 + 1) * 512],
                            self.onec_bf[:, 0:1],
                            self.sqs[:, 0:512], start=(h == 0), stop=(h == 15)))
                t_sq = inc(TEN, self.tM, lambda e, h=h, half=half:
                           e.matmul(self.PS1[0:1, (half * 2 + 1) * 512:(half * 2 + 2) * 512],
                                    self.onec_bf[:, 0:1],
                                    self.sqs[:, 512:1024], start=(h == 0), stop=(h == 15)))
        t_k = t_sq
        wait(VEC, self.tM, t_k)
        for nsl in range(4):
            self.chn(VEC, self.vM, lambda e, nsl=nsl: e.tensor_reduce(
                self.nrow[0:1, nsl:nsl + 1], self.PS1[0:1, nsl * 512:(nsl + 1) * 512], AX.X, ALU.max))
        v_k = self.chn(VEC, self.vM, lambda e: e.tensor_reduce(self.brow[0:1, 1:2],
                                                               self.nrow[0:1, 0:4], AX.X, ALU.max))
        for h in range(16):
            wait(VEC, self.tM, t_sq)
            v_s = self.chn(VEC, self.vM, lambda e, h=h: e.tensor_mul(self.sqs[:, :NTOK],
                                                                      self.q_h[h][:, :], self.q_h[h][:, :]))
            wait(TEN, self.vM, v_s)
            em(TEN, lambda e, h=h:
               e.matmul(self.PS0[0:1, 0:512], self.onec_bf[:, 0:1],
                        self.sqs[:, 0:512], start=(h == 0), stop=(h == 15)))
            t_sq = inc(TEN, self.tM, lambda e, h=h:
                       e.matmul(self.PS0[0:1, 512:1024], self.onec_bf[:, 0:1],
                                self.sqs[:, 512:1024], start=(h == 0), stop=(h == 15)))
        t_q = t_sq
        wait(VEC, self.tM, t_q)
        for nsl in range(2):
            self.chn(VEC, self.vM, lambda e, nsl=nsl: e.tensor_reduce(
                self.nrow[0:1, 4 + nsl:5 + nsl], self.PS0[0:1, nsl * 512:(nsl + 1) * 512], AX.X, ALU.max))
        v_q = self.chn(VEC, self.vM, lambda e: e.tensor_reduce(self.brow[0:1, 0:1],
                                                               self.nrow[0:1, 4:6], AX.X, ALU.max))
        wait(SCA, self.vM, v_q)
        wait(SCA, self.vM, v_k)
        self.chn(SCA, self.sM, lambda e: e.activation(self.brow[0:1, 0:1], self.brow[0:1, 0:1], AF.Ln))
        s_l = self.chn(SCA, self.sM, lambda e: e.activation(self.brow[0:1, 1:2], self.brow[0:1, 1:2], AF.Ln))
        wait(VEC, self.sM, s_l)
        v_a = self.chn(VEC, self.vM, lambda e: e.tensor_add(self.brow[0:1, 0:1],
                                                            self.brow[0:1, 0:1], self.brow[0:1, 1:2]))
        wait(SCA, self.vM, v_a)
        s_e = self.chn(SCA, self.sM, lambda e: e.activation(self.brow[0:1, 0:1], self.brow[0:1, 0:1],
                                                            AF.Exp, scale=0.5))
        wait(VEC, self.sM, s_e)
        v_e = self.chn(VEC, self.vM, lambda e: e.tensor_scalar(self.brow[0:1, 0:1], self.brow[0:1, 0:1],
                                                               -1.01 * SCL, None, ALU.mult))
        wait(TEN, self.vM, v_e)
        t_e = inc(TEN, self.tM, lambda e: e.matmul(self.PS0[:, 0:1], self.ones_f32[0:1, :],
                                                   self.brow[0:1, 0:1], start=True, stop=True))
        wait(VEC, self.tM, t_e)
        self.v_ebias = inc(VEC, self.vM, lambda e: e.tensor_copy(self.ebias[:, :], self.PS0[:, 0:1]))

    def pa(self):
        em, wait, inc, dma = self.em, self.wait, self.inc, self.dma
        wait(SCA, self.vM, self.v_ebias)
        exp_abs, spair, pvpair, r4_abs, evac_abs = {}, {}, {}, {}, {}

        def pv_pair(R, jp, h_, qt):
            wait(TEN, self.sM, exp_abs[(R, jp)])
            if jp == 0 and R >= 1:
                wait(TEN, self.vM, r4_abs[R - 1])
                wait(TEN, self.vM, evac_abs[R - 1])
            for kc in (2 * jp, 2 * jp + 1):
                for qc in range(4):
                    if kc == 2 * jp + 1 and qc == 3:
                        continue
                    em(TEN, lambda e, kc=kc, qc=qc, jp=jp, h_=h_:
                       e.matmul(self.PS1[:, qc * 512: qc * 512 + 129],
                                self.A_T[:, (jp % 2) * 1024 + (kc % 2) * 512 + qc * P:
                                         (jp % 2) * 1024 + (kc % 2) * 512 + (qc + 1) * P],
                                self.V_h[h_][:, kc * 129:(kc + 1) * 129],
                                start=(kc == 0), stop=(kc == 15)))
            kc = 2 * jp + 1
            pvpair[(R, jp)] = inc(TEN, self.tM, lambda e, kc=kc, jp=jp, h_=h_:
                e.matmul(self.PS1[:, 3 * 512: 3 * 512 + 129],
                         self.A_T[:, (jp % 2) * 1024 + (kc % 2) * 512 + 3 * P:
                                  (jp % 2) * 1024 + (kc % 2) * 512 + 4 * P],
                         self.V_h[h_][:, kc * 129:(kc + 1) * 129],
                         start=False, stop=(kc == 15)))

        for h_ in range(16):
            for qt in range(2):
                R = h_ * 2 + qt
                for jp in range(8):
                    g = R * 8 + jp
                    if g >= 2:
                        pR, pjp = divmod(g - 2, 8)
                        wait(TEN, self.sM, exp_abs[(pR, pjp)])
                    kc0 = 2 * jp
                    em(TEN, lambda e, h_=h_, qt=qt, kc=kc0:
                       e.matmul(self.PS0[:, (kc % 4) * 512:((kc % 4) + 1) * 512],
                                self.K_h[h_][:, kc * P:(kc + 1) * P],
                                self.q_h[h_][:, qt * 512:(qt + 1) * 512],
                                start=True, stop=True))
                    spair[(R, jp)] = inc(TEN, self.tM, lambda e, h_=h_, qt=qt, kc=kc0 + 1:
                        e.matmul(self.PS0[:, (kc % 4) * 512:((kc % 4) + 1) * 512],
                                 self.K_h[h_][:, kc * P:(kc + 1) * P],
                                 self.q_h[h_][:, qt * 512:(qt + 1) * 512],
                                 start=True, stop=True))
                    wait(SCA, self.tM, spair[(R, jp)])
                    if g >= 2:
                        pR, pjp = divmod(g - 2, 8)
                        wait(SCA, self.tM, pvpair[(pR, pjp)])
                    exp_abs[(R, jp)] = inc(SCA, self.sM, lambda e, jp=jp:
                        e.activation(self.A_T[:, (jp % 2) * 1024:(jp % 2) * 1024 + 1024],
                                     self.PS0[:, (jp % 2) * 1024:(jp % 2) * 1024 + 1024],
                                     AF.Exp, bias=self.ebias[:, 0:1], scale=SCL))
                    if jp >= 1:
                        pv_pair(R, jp - 1, h_, qt)
                pv_pair(R, 7, h_, qt)
                wait(VEC, self.tM, pvpair[(R, 7)])
                sb_ = self.s4[R % 2]
                rb_ = self.r4[R % 2]
                for qc in range(3):
                    em(VEC, lambda e, sb_=sb_, qc=qc: e.tensor_copy(
                        sb_[:, qc:qc + 1], self.PS1[:, qc * 512 + 128: qc * 512 + 129]))
                self.chn(VEC, self.vM, lambda e, sb_=sb_: e.tensor_copy(
                    sb_[:, 3:4], self.PS1[:, 3 * 512 + 128: 3 * 512 + 129]))
                r4_abs[R] = self.chn(VEC, self.vM, lambda e, sb_=sb_, rb_=rb_: e.reciprocal(rb_[:, :], sb_[:, :]))
                last = None
                for qc in range(4):
                    qcg = qt * 4 + qc
                    last = inc(VEC, self.vM, lambda e, qc=qc, qcg=qcg, h_=h_, rb_=rb_:
                               e.tensor_scalar(self.ao[qcg][:, h_ * P:(h_ + 1) * P],
                                               self.PS1[:, qc * 512: qc * 512 + 128],
                                               rb_[:, qc:qc + 1], None, ALU.mult))
                evac_abs[R] = last
            if h_ == 7:
                self.wait(GPS, self.tM, pvpair[(15, 7)])
                self.d_wo = dma(GPS, self.dW,
                                self.WoT[:, :].rearrange("p (k m) -> p k m", m=H),
                                self.d["woT"].rearrange("(k p) m -> p k m", p=P))
                self.d_bo = dma(GPS, self.dW, self.bo_row[0:1, :], self.d["bo"][:, :])
        self.ao_spill = {}
        for qcg in range(8):
            wait(SYN, self.vM, evac_abs[30 if qcg < 4 else 31])
            self.ao_spill[qcg] = dma(SYN, self.dX, self.d["aosp"][qcg], self.ao[qcg][:, :])
        self.attn_done = evac_abs[31]

    def pb(self):
        em, wait, inc, dma = self.em, self.wait, self.inc, self.dma
        d = self.d
        self.waitt(TEN, self.d_wo)
        self.waitt(TEN, self.d_bo)
        self.waitt(TEN, self.tok_router)
        self.waitt(SCA, self.tok_rmsw)
        for qcg in range(8):
            self.waitt(SYN, self.ao_spill[qcg])
        self.d_sel = dma(GPS, self.dW, self.sel_sb[:, :], d["esel"][:, :])
        hT_done = {}
        xo_dma = {}
        ao_dma = {}
        aotr = {}
        opev = {}
        aoev = {}
        opmm = {}
        bstage1 = {}
        bstage2 = {}
        s_h_d = {}
        bstage = {}

        def stage_a1(tc):
            if tc >= 2:
                wait(SYN, self.tM, aotr[tc - 2])
            ao_dma[tc] = dma(SYN, self.dX, self.ao_r[tc % 2][:, :], d["aosp"][tc])
            self.waitt(TEN, ao_dma[tc])
            if tc >= 1:
                wait(TEN, self.vM, aoev[tc - 1])
            if tc >= 2:
                wait(TEN, self.vM, opev[tc - 2])
                wait(TEN, self.tM, opmm[tc - 2])
            if tc >= 3:
                wait(TEN, self.sM, bstage1[tc - 3][0])
                wait(TEN, self.sM, bstage2[tc - 3][0])
                wait(TEN, self.vM, bstage2[tc - 3][1])
            for wave in range(2):
                for kk in range(7):
                    k = wave * 8 + kk
                    pv = self.PS0b if kk < 4 else self.PS1b
                    bank = kk % 4
                    em(TEN, lambda e, k=k, tc=tc, pv=pv, bank=bank:
                       e.transpose(pv[:, bank * 1024: bank * 1024 + P],
                                   self.ao_r[tc % 2][:, k * P:(k + 1) * P], self.ident_bf[:, :]))
                k = wave * 8 + 7
                t_w = inc(TEN, self.tM, lambda e, k=k, tc=tc:
                          e.transpose(self.PS1b[:, 3 * 1024: 3 * 1024 + P],
                                      self.ao_r[tc % 2][:, k * P:(k + 1) * P], self.ident_bf[:, :]))
                wait(VEC, self.tM, t_w)
                for kk in range(7):
                    k = wave * 8 + kk
                    pv = self.PS0b if kk < 4 else self.PS1b
                    bank = kk % 4
                    em(VEC, lambda e, k=k, tc=tc, pv=pv, bank=bank: e.tensor_copy(
                        self.aoT[tc % 2][:, k * P:(k + 1) * P], pv[:, bank * 1024: bank * 1024 + P]))
                k = wave * 8 + 7
                v_w = inc(VEC, self.vM, lambda e, k=k, tc=tc: e.tensor_copy(
                    self.aoT[tc % 2][:, k * P:(k + 1) * P], self.PS1b[:, 3 * 1024: 3 * 1024 + P]))
                wait(TEN, self.vM, v_w)
                if wave == 1:
                    aotr[tc] = t_w
                    aoev[tc] = v_w

        def stage_a2(tc):
            wait(TEN, self.vM, aoev[tc])
            if tc + 1 in aoev:
                wait(TEN, self.vM, aoev[tc + 1])
            if tc >= 1:
                wait(TEN, self.vM, opev[tc - 1])
            for nsl in range(4):
                for k in range(16):
                    em(TEN, lambda e, k=k, nsl=nsl, tc=tc:
                       e.matmul(self.PS0[:, nsl * 512:(nsl + 1) * 512],
                                self.aoT[tc % 2][:, k * P:(k + 1) * P],
                                self.WoT[:, k * H + nsl * 512: k * H + (nsl + 1) * 512],
                                start=(k == 0), stop=False))
                if nsl < 3:
                    em(TEN, lambda e, nsl=nsl:
                       e.matmul(self.PS0[:, nsl * 512:(nsl + 1) * 512], self.ones_bf[0:1, :],
                                self.bo_row[0:1, nsl * 512:(nsl + 1) * 512], start=False, stop=True))
            t_op = inc(TEN, self.tM, lambda e:
                       e.matmul(self.PS0[:, 3 * 512:4 * 512], self.ones_bf[0:1, :],
                                self.bo_row[0:1, 3 * 512:4 * 512], start=False, stop=True))
            opmm[tc] = t_op
            wait(VEC, self.tM, t_op)
            self.waitt(VEC, xo_dma[tc])
            a0 = tc * H
            opev[tc] = self.chn(VEC, self.vM, lambda e, tc=tc, a0=a0: e.tensor_add(
                self.acc[:, a0:a0 + H], self.PS0[:, :], self.x_ot[0][:, :]))
            if tc + 1 < 8:
                wait(SYN, self.vM, opev[tc])
                xo_dma[tc + 1] = dma(SYN, self.dX, self.x_ot[0][:, :],
                                     d["x_own"][(tc + 1) * P:(tc + 2) * P, :])
            for gi in range(4):
                self.chn(VEC, self.vM, lambda e, a0=a0, gi=gi: e.bn_stats(
                    self.stats6[:, gi * 6:(gi + 1) * 6],
                    self.acc[:, a0 + gi * 512: a0 + (gi + 1) * 512]))
            v_mv = self.chn(VEC, self.vM, lambda e: e.bn_aggr(
                self.mv2[:, :], self.stats6[:, :].rearrange("p (g x) -> p g x", x=3)))
            wait(SCA, self.vM, v_mv)
            self.chn(SCA, self.sM, lambda e: e.activation(self.sc_a[:, :], self.mv2[:, 1:2], AF.Ln, bias=self.c_eps5[:, 0:1]))
            s_r = self.chn(SCA, self.sM, lambda e: e.activation(self.sc_a[:, :], self.sc_a[:, :],
                                                                AF.Exp, scale=-0.5))
            wait(VEC, self.sM, s_r)
            self.chn(VEC, self.vM, lambda e: e.tensor_mul(self.sc_c[:, :], self.sc_a[:, :], self.sc_a[:, :]))
            v_m = self.chn(VEC, self.vM, lambda e: e.tensor_mul(self.sc_c[:, :], self.sc_c[:, :],
                                                                self.mv2[:, 1:2]))
            wait(SCA, self.vM, v_m)
            self.chn(SCA, self.sM, lambda e: e.activation(self.sc_d[:, :], self.sc_c[:, :], AF.Ln, bias=self.c_eps6[:, 0:1]))
            s_rr = self.chn(SCA, self.sM, lambda e: e.activation(self.sc_d[:, :], self.sc_d[:, :],
                                                                 AF.Exp, scale=-0.5))
            wait(VEC, self.sM, s_rr)
            self.chn(VEC, self.vM, lambda e: e.tensor_mul(self.sc_e[:, :], self.sc_a[:, :], self.sc_d[:, :]))
            v_sb = self.chn(VEC, self.vM, lambda e: e.scalar_tensor_tensor(
                self.sc_f[:, :], self.mv2[:, 0:1], -1.0, self.sc_e[:, :], ALU.mult, ALU.mult))
            wait(SCA, self.vM, v_sb)
            if tc >= 2:
                wait(SCA, self.tM, bstage1[tc - 2][1])
            s_h_d[tc] = self.chn(SCA, self.sM, lambda e, a0=a0, tc=tc: e.activation(
                self.h_tok2[tc % 2][:, :], self.acc[:, a0:a0 + H], AF.Identity,
                bias=self.sc_f[:, 0:1], scale=self.sc_e[:, 0:1]))
            if tc == 0 and _TAPS:
                self.wait(SYN, self.sM, s_h_d[tc])
                self.tap("htok0", self.h_tok2[0][:, :])

        def stage_b1(tc):
            wait(TEN, self.sM, s_h_d[tc])
            wait(TEN, self.vM, opev[tc + 1] if tc + 1 in opev else 0)
            if tc + 1 in aoev:
                wait(TEN, self.vM, aoev[tc + 1])
            if tc >= 1:
                wait(TEN, self.vM, bstage2[tc - 1][1])
                wait(TEN, self.sM, bstage2[tc - 1][0])
            s_ev = None
            for wave in range(2):
                if wave > 0:
                    wait(TEN, self.sM, s_ev)
                for kk in range(7):
                    k = wave * 8 + kk
                    pv = self.PS0b if kk < 4 else self.PS1b
                    bank = kk % 4
                    em(TEN, lambda e, k=k, tc=tc, pv=pv, bank=bank:
                       e.transpose(pv[:, bank * 1024: bank * 1024 + P],
                                   self.h_tok2[tc % 2][:, k * P:(k + 1) * P], self.ident_bf[:, :]))
                k = wave * 8 + 7
                t_h = inc(TEN, self.tM, lambda e, k=k, tc=tc:
                          e.transpose(self.PS1b[:, 3 * 1024: 3 * 1024 + P],
                                      self.h_tok2[tc % 2][:, k * P:(k + 1) * P], self.ident_bf[:, :]))
                wait(SCA, self.tM, t_h)
                for kk in range(8):
                    k = wave * 8 + kk
                    pv = self.PS0b if kk < 4 else self.PS1b
                    bank = kk % 4
                    s_ev = inc(SCA, self.sM, lambda e, k=k, kk=kk, tc=tc, pv=pv, bank=bank:
                               e.activation(self.hT[k][:, tc * P:(tc + 1) * P],
                                            pv[:, bank * 1024: bank * 1024 + P],
                                            AF.Identity, scale=self.rmsw[:, k:k + 1]))
            hT_done[tc] = s_ev
            bstage1[tc] = (s_ev, t_h)

        def stage_b2(tc):
            s_ev = bstage1[tc][0]
            wait(TEN, self.sM, s_ev)
            if tc + 2 in aoev:
                wait(TEN, self.vM, aoev[tc + 2])
            for k in range(15):
                em(TEN, lambda e, k=k, tc=tc:
                   e.matmul(self.PS1[:, 0:8], self.hT[k][:, tc * P:(tc + 1) * P],
                            self.router_sb[:, k * 8:(k + 1) * 8], start=(k == 0), stop=False))
            t_rt = inc(TEN, self.tM, lambda e, tc=tc:
                       e.matmul(self.PS1[:, 0:8], self.hT[15][:, tc * P:(tc + 1) * P],
                                self.router_sb[:, 15 * 8:16 * 8], start=False, stop=True))
            wait(VEC, self.tM, t_rt)
            v_nm = self.chn(VEC, self.vM, lambda e: e.tensor_reduce(
                self.sc_a2[:, :], self.PS1[:, 0:8], AX.X, ALU.max, negate=True))
            wait(SCA, self.vM, v_nm)
            s_pu = self.chn(SCA, self.sM, lambda e: e.activation(self.probs_u[:, :], self.PS1[:, 0:8],
                                                                 AF.Exp, bias=self.sc_a2[:, 0:1]))
            wait(VEC, self.sM, s_pu)
            if tc == 0 and _TAPS:
                self.wait(SYN, self.sM, s_pu)
                self.tap("probs0", self.probs_u[:, :])
            self.chn(VEC, self.vM, lambda e: e.max(out=self.m8[:, :], in_=self.probs_u[:, :]))
            self.chn(VEC, self.vM, lambda e: e.tensor_scalar(self.maskb[:, :], self.probs_u[:, :],
                                                             self.m8[:, 1:2], None, ALU.is_ge))
            self.chn(VEC, self.vM, lambda e: e.tensor_mul(self.cw[:, :], self.probs_u[:, :], self.maskb[:, :]))
            self.chn(VEC, self.vM, lambda e: e.tensor_add(self.sc_b2[:, :], self.m8[:, 0:1], self.m8[:, 1:2]))
            self.chn(VEC, self.vM, lambda e: e.reciprocal(self.sc_b2[:, :], self.sc_b2[:, :]))
            v_cw = self.chn(VEC, self.vM, lambda e: e.tensor_scalar(self.cw[:, :], self.cw[:, :],
                                                                    self.sc_b2[:, 0:1], None, ALU.mult))
            wait(TEN, self.vM, v_cw)
            t_c = inc(TEN, self.tM, lambda e: e.transpose(self.PS1[0:8, 1024:1024 + P],
                                                          self.cw[:, :], self.ident_f32[:, :]))
            wait(VEC, self.tM, t_c)
            v_ct = inc(VEC, self.vM, lambda e, tc=tc: e.tensor_copy(
                self.cT_sb[:, tc * P:(tc + 1) * P], self.PS1[0:8, 1024:1024 + P]))
            bstage2[tc] = (s_pu, v_ct)

        xo_dma[0] = dma(SYN, self.dX, self.x_ot[0][:, :], d["x_own"][0:P, :])
        stage_a1(0)
        stage_a1(1)
        stage_a2(0)
        for tc in range(8):
            if tc + 2 < 8:
                stage_a1(tc + 2)
            if tc + 1 < 8:
                stage_a2(tc + 1)
            stage_b1(tc)
            stage_b2(tc)
        self.hT_all = hT_done[7]
        self.cT_all = self.vM.n

    def pc(self):
        em, wait, inc, dma = self.em, self.wait, self.inc, self.dma
        d = self.d
        guev = {}
        dnev = {}
        gu_mm = {}
        gu_dma = {}
        dn_dma = {}
        cb_ev = {}

        def gu_src(e_, m):
            if e_ == 8:
                return (d["swg_r"][m].rearrange("p k c -> p (k c)"),
                        d["swu_r"][m].rearrange("p k c -> p (k c)"))
            return (d["gate_r"][e_, m].rearrange("p k c -> p (k c)"),
                    d["up_r"][e_, m].rearrange("p k c -> p (k c)"))

        def emit_gu_dma(gm):
            if gm >= 9 * 8:
                return
            e_, m = divmod(gm, 8)
            if gm >= 2:
                pe, pm = divmod(gm - 2, 8)
                self.wait(GPS, self.tM, gu_mm[(pe, pm)])
            sg, su = gu_src(e_, m)
            tg = dma(GPS, self.dW, self.gbuf[m % 2][:, :], sg)
            tu = dma(GPS, self.dW, self.ubuf[m % 2][:, :], su)
            gu_dma[(e_, m)] = (tg, tu)

        emit_gu_dma(0)
        emit_gu_dma(1)
        dn_dma[0] = dma(GPS, self.dW,
                        self.dbuf[:, :].rearrange("p (m h) -> p m h", h=H),
                        d["down_w"][0].rearrange("(m p) h -> p m h", p=P))

        dn_last = None
        for e_ in range(9):
            shared = (e_ == 8)
            if not shared:
                if e_ >= 1:
                    wait(TEN, self.vM, dnev[(e_ - 1, 7)])
                if e_ == 0:
                    self.waitt(TEN, self.d_sel)
                em(TEN, lambda e, e_=e_:
                   e.matmul(self.PS1[:, 1024:1536],
                            self.sel_sb[0:8, e_ * P:(e_ + 1) * P],
                            self.cT_sb[0:8, 0:512], start=True, stop=True))
                t_cb = inc(TEN, self.tM, lambda e, e_=e_:
                           e.matmul(self.PS1[:, 1536:2048],
                                    self.sel_sb[0:8, e_ * P:(e_ + 1) * P],
                                    self.cT_sb[0:8, 512:1024], start=True, stop=True))
                wait(VEC, self.tM, t_cb)
                cb = self.c_bc[e_ % 2]
                cb_ev[e_] = inc(VEC, self.vM, lambda e, cb=cb:
                                e.tensor_copy(cb[:, :], self.PS1[:, 1024:2048]))
            for m in range(8):
                ps_g = self.PS0 if m % 2 == 0 else self.PS1
                self.waitt(TEN, gu_dma[(e_, m)][0])
                self.waitt(TEN, gu_dma[(e_, m)][1])
                if m >= 2:
                    wait(TEN, self.vM, guev[(e_, m - 2)])
                elif e_ > 0:
                    wait(TEN, self.vM, guev[(e_ - 1, 6 + m)])
                    wait(TEN, self.vM, dnev[(e_ - 1, 6 + m)])
                if m == 1 and not shared:
                    wait(TEN, self.vM, cb_ev[e_])
                gb, ub = self.gbuf[m % 2], self.ubuf[m % 2]
                for k in range(16):
                    for half in range(2):
                        em(TEN, lambda e, k=k, half=half, ps_g=ps_g, gb=gb:
                           e.matmul(ps_g[:, half * 512:(half + 1) * 512],
                                    gb[:, k * P:(k + 1) * P],
                                    self.hT[k][:, half * 512:(half + 1) * 512],
                                    start=(k == 0), stop=(k == 15)))
                for k in range(16):
                    for half in range(2):
                        if k == 15 and half == 1:
                            continue
                        em(TEN, lambda e, k=k, half=half, ps_g=ps_g, ub=ub:
                           e.matmul(ps_g[:, 1024 + half * 512: 1024 + (half + 1) * 512],
                                    ub[:, k * P:(k + 1) * P],
                                    self.hT[k][:, half * 512:(half + 1) * 512],
                                    start=(k == 0), stop=(k == 15)))
                gu_mm[(e_, m)] = inc(TEN, self.tM, lambda e, ps_g=ps_g, ub=ub:
                    e.matmul(ps_g[:, 1536:2048], ub[:, 15 * P:16 * P],
                             self.hT[15][:, 512:1024], start=False, stop=True))
                emit_gu_dma(e_ * 8 + m + 2)
                wait(SCA, self.tM, gu_mm[(e_, m)])
                t1 = self.t1b[m % 2]
                s_si = inc(SCA, self.sM, lambda e, ps_g=ps_g, t1=t1:
                           e.activation(t1[:, :], ps_g[:, 0:1024], AF.Silu))
                wait(VEC, self.sM, s_si)
                if shared:
                    guev[(e_, m)] = self.chn(VEC, self.vM, lambda e, ps_g=ps_g, t1=t1, m=m:
                                             e.tensor_mul(self.a_fm[m][:, :], t1[:, :],
                                                          ps_g[:, 1024:2048]))
                else:
                    if m == 0:
                        wait(VEC, self.vM, cb_ev[e_])
                    self.chn(VEC, self.vM, lambda e, ps_g=ps_g, t1=t1:
                             e.tensor_mul(self.t2b[:, :], t1[:, :], ps_g[:, 1024:2048]))
                    guev[(e_, m)] = self.chn(VEC, self.vM, lambda e, m=m, cb=cb:
                                             e.tensor_mul(self.a_fm[m][:, :], self.t2b[:, :], cb[:, :]))
            self.waitt(TEN, dn_dma[e_])
            wait(TEN, self.vM, guev[(e_, 7)])
            for tc in range(8):
                ps_d = self.PS0 if tc % 2 == 0 else self.PS1
                if tc >= 2:
                    wait(TEN, self.vM, dnev[(e_, tc - 2)])
                else:
                    wait(TEN, self.vM, guev[(e_, 6 + tc)])
                for nsl in range(4):
                    for m in range(8):
                        if nsl == 3 and m == 7:
                            continue
                        em(TEN, lambda e, m=m, nsl=nsl, tc=tc, ps_d=ps_d:
                           e.matmul(ps_d[:, nsl * 512:(nsl + 1) * 512],
                                    self.a_fm[m][:, tc * P:(tc + 1) * P],
                                    self.dbuf[:, m * H + nsl * 512: m * H + (nsl + 1) * 512],
                                    start=(m == 0), stop=(m == 7)))
                t_dn = inc(TEN, self.tM, lambda e, tc=tc, ps_d=ps_d:
                           e.matmul(ps_d[:, 3 * 512:4 * 512],
                                    self.a_fm[7][:, tc * P:(tc + 1) * P],
                                    self.dbuf[:, 7 * H + 3 * 512: 7 * H + 4 * 512],
                                    start=False, stop=True))
                dn_last = t_dn
                wait(VEC, self.tM, t_dn)
                a0 = tc * H
                dnev[(e_, tc)] = self.chn(VEC, self.vM, lambda e, a0=a0, ps_d=ps_d: e.tensor_add(
                    self.acc[:, a0:a0 + H], ps_d[:, :], self.acc[:, a0:a0 + H]))
            if e_ < 8:
                self.wait(GPS, self.tM, dn_last)
                nxt = d["swd_w"] if e_ == 7 else d["down_w"][e_ + 1]
                dn_dma[e_ + 1] = dma(GPS, self.dW,
                                     self.dbuf[:, :].rearrange("p (m h) -> p m h", h=H),
                                     nxt.rearrange("(m p) h -> p m h", p=P))
        self.acc_done = dnev[(8, 7)]

    def pd(self):
        wait, dma = self.wait, self.dma
        wait(SYN, self.vM, self.acc_done)
        tgt = dma(SYN, self.dX, self.d["out"].rearrange("(tc p) h -> p tc h", p=P),
                  self.acc[:, :].rearrange("p (tc h) -> p tc h", h=H))
        self.waitt(SYN, tgt)

    def build(self, stop_after=None):
        self.alloc_all()
        self._materialize()
        self.h_tok2 = self.h_tok2b
        phases = [
            ("p0", self.p0),
            ("p1a", self.p1a),
            ("p1bk", lambda: self.build_side(S, self.kv_latT, self.W4, self.bkup,
                                             self.ck, self.sk, self.K_h, self.tok_w4)),
            ("p1bq", lambda: self.build_side(NTOK, self.q_latT, self.W3, self.bqup,
                                             self.cq, self.sq, self.q_h, self.tok_w3)),
            ("p1c", self.p1c),
            ("bound", self.bound_prep),
            ("pa", self.pa),
            ("pb", self.pb),
            ("pc", self.pc),
        ]
        stopped = False
        for name, fn in phases:
            fn()
            if name in ("p0", "p1a", "p1bk", "p1bq", "bound", "pa", "pb"):
                self.barrier()
            if _TAPS:
                if name == "p1a":
                    self.tap("kvlat0", self.kv_latT[0][:, :])
                    self.tap("qlat0", self.q_latT[0][:, :])
                elif name == "p1bk":
                    self.tap("K0", self.K_h[0][:, :])
                    self.tap("K8", self.K_h[8][:, :])
                elif name == "p1bq":
                    self.tap("q0", self.q_h[0][:, :])
                    self.tap("q12", self.q_h[12][:, :])
                elif name == "p1c":
                    pass
                elif name == "bound":
                    self.tap("V0", self.V_h[0][:, :])
                    self.tap("ebias", self.ebias[:, :])
                elif name == "pa":
                    self.tap("ao0", self.ao[0][:, :])
                elif name == "pb":
                    self.tap("acc_pb", self.acc[:, :])
                    self.tap("hT0", self.hT[0][:, :])
                    self.tap("cT", self.cT_sb[:, :])
            if name == stop_after:
                stopped = True
                break
        if stopped:
            self.barrier()
            tgt = self.dma(SYN, self.dX,
                           self.d["out"].rearrange("(tc p) h -> p tc h", p=P),
                           self.acc[:, :].rearrange("p (tc h) -> p tc h", h=H))
            self.wait(SYN, self.dX, tgt)
        else:
            self.pd()

        nc = self.nc
        with nc.Block() as block:
            @block.tensor
            def _(e):
                for f in self.streams[TEN]:
                    f(e)

            @block.vector
            def _(e):
                for f in self.streams[VEC]:
                    f(e)

            @block.scalar
            def _(e):
                for f in self.streams[SCA]:
                    f(e)

            @block.sync
            def _(e):
                for f in self.streams[SYN]:
                    f(e)

            @block.gpsimd
            def _(e):
                for f in self.streams[GPS]:
                    f(e)
        return nc


def _host_inputs(args):
    x = np.ascontiguousarray(args["L0_attn_hidden_states"], dtype=np.float32)
    com = {
        "wqdT": np.ascontiguousarray(args["L0_attn_Wq_down"].T),
        "wkvdT": np.ascontiguousarray(args["L0_attn_Wkv_down"].T),
        "wqupT": np.ascontiguousarray(args["L0_attn_Wq_up"].T),
        "wkupT": np.ascontiguousarray(args["L0_attn_Wk_up"].T),
        "wvupT": np.ascontiguousarray(args["L0_attn_Wv_up"].T),
        "woT": np.ascontiguousarray(args["L0_attn_Wo"].T),
        "bqd": np.ascontiguousarray(args["L0_attn_bq_down"].reshape(4, P).T),
        "bkvd": np.ascontiguousarray(args["L0_attn_bkv_down"].reshape(4, P).T),
        "bqup": np.ascontiguousarray(args["L0_attn_bq_up"].reshape(8, P).T),
        "bkup": np.ascontiguousarray(args["L0_attn_bk_up"].reshape(8, P).T),
        "bv": np.ascontiguousarray(args["L0_attn_bv_up"].reshape(1, H)),
        "bo": np.ascontiguousarray(args["L0_attn_bo"].reshape(1, H)),
        "rmsw": np.ascontiguousarray(args["L0_ffn_rms_weight"].reshape(16, P).T),
        "router": np.ascontiguousarray(args["L0_ffn_router_w"]),
        "gate_r": np.ascontiguousarray(
            args["L0_ffn_w_gate"].reshape(NE, 16, P, 8, P).transpose(0, 3, 2, 1, 4)),
        "up_r": np.ascontiguousarray(
            args["L0_ffn_w_up"].reshape(NE, 16, P, 8, P).transpose(0, 3, 2, 1, 4)),
        "down_w": np.ascontiguousarray(args["L0_ffn_w_down"]),
        "swg_r": np.ascontiguousarray(
            args["L0_ffn_sw_gate"].reshape(16, P, 8, P).transpose(2, 1, 0, 3)),
        "swu_r": np.ascontiguousarray(
            args["L0_ffn_sw_up"].reshape(16, P, 8, P).transpose(2, 1, 0, 3)),
        "swd_w": np.ascontiguousarray(args["L0_ffn_sw_down"]),
    }
    esel = np.zeros((8, NE * P), dtype=np.float32)
    for e in range(NE):
        esel[e, e * P:(e + 1) * P] = 1.0
    com["esel"] = esel
    for k, v in com.items():
        com[k] = np.ascontiguousarray(v, dtype=np.float32)
    in_maps = []
    for c in range(8):
        b, qh = c // 2, c % 2
        m = dict(com)
        m["x_full"] = np.ascontiguousarray(x[b])
        m["x_own"] = np.ascontiguousarray(x[b, qh * NTOK:(qh + 1) * NTOK])
        m["pos0"] = np.full((P, 1), float(qh * NTOK), dtype=np.float32)
        in_maps.append(m)
    return in_maps


_NC_CACHE = [None]


def kernel(**inputs):
    in_maps = _host_inputs(inputs)
    if _NC_CACHE[0] is None:
        _NC_CACHE[0] = KB().build()
    nc = _NC_CACHE[0]
    res = run_bass_kernel_spmd(nc, in_maps, core_ids=list(range(8)))
    if _TAPS:
        global _LAST_TAPS
        _LAST_TAPS = [{k[4:]: np.asarray(v) for k, v in r.items() if k.startswith("tap_")}
                      for r in res.results]
    out = np.empty((4, S, H), dtype=np.float32)
    for c in range(8):
        b, qh = c // 2, c % 2
        out[b, qh * NTOK:(qh + 1) * NTOK] = np.asarray(res.results[c]["out"])
    return out


# version 10
# speedup vs baseline: 1.1841x; 1.1841x over previous
import math
from contextlib import ExitStack

import numpy as np

import concourse.bass as bass
import concourse.mybir as mybir
from concourse.bass_utils import run_bass_kernel_spmd

F32 = mybir.dt.float32
BF16 = mybir.dt.bfloat16
AF = mybir.ActivationFunctionType
ALU = mybir.AluOpType
AX = mybir.AxisListType

P = 128
H = 2048
NH = 16
LAT = 512
QHW = 1024
S = 2048
NTOK = 1024
NE = 8
FF = 1024
MAGIC = 1.5 * 2 ** 23
INV2PI = float(np.float32(1.0 / (2 * math.pi)))
TWOPI = float(np.float32(2 * math.pi))
SCL = 1.0 / math.sqrt(128.0)

SBUF_BASE = 17024
SBUF_TOP = 229344

P0, P1A, P1BK, P1BQ, P1C, PAA, PAB, PB, PC, PD = range(10)

TEN, VEC, SCA, SYN, GPS = "t", "v", "s", "y", "g"
_TAPS = False
_LAST_TAPS = None


class Alloc:

    def __init__(self, base, top):
        self.base, self.top = base, top
        self.reqs = []
        self.items = []
        self.offsets = {}

    def reserve(self, name, nbytes, birth, death):
        nbytes = (nbytes + 31) // 32 * 32
        self.reqs.append((nbytes, birth, death, name))

    def pin(self, name, rel_off):
        self.pins = getattr(self, "pins", {})
        self.pins[name] = self.base + rel_off

    def _try(self, order):
        pins = getattr(self, "pins", {})
        items, offsets = [], {}
        for nbytes, birth, death, name in order:
            if name in pins:
                start, end = pins[name], pins[name] + nbytes
                assert end <= self.top, f"pin {name} overflows"
                for (s, e, b, d, n) in items:
                    assert not (start < e and s < end and birth <= d and b <= death), \
                        f"pin {name} collides with {n}"
                items.append((start, end, birth, death, name))
                offsets[name] = start
                continue
            cands = sorted({self.base} | {e for (s, e, b, d, n) in items})
            placed = False
            for start in cands:
                end = start + nbytes
                if end > self.top:
                    continue
                if all(not (start < e and s < end and birth <= d and b <= death)
                       for (s, e, b, d, n) in items):
                    items.append((start, end, birth, death, name))
                    offsets[name] = start
                    placed = True
                    break
            if not placed:
                return None
        self.items, self.offsets = items, offsets
        return offsets

    def solve(self):
        pins = getattr(self, "pins", {})
        keys = [
            lambda r: (r[3] not in pins, -(r[2] - r[1]), -r[0]),
            lambda r: (r[3] not in pins, -r[0], r[1]),
            lambda r: (r[3] not in pins, r[1], -(r[2] - r[1]), -r[0]),
            lambda r: (r[3] not in pins, -r[0] * (r[2] - r[1] + 1)),
        ]
        for k in keys:
            if self._try(sorted(self.reqs, key=k)) is not None:
                return
        raise RuntimeError("SBUF overflow: no ordering found a packing")


class Sem:
    def __init__(self, h):
        self.h = h
        self.n = 0


class KB:

    def __init__(self):
        nc = bass.Bass("TRN2", target_bir_lowering=False)
        self.nc = nc
        self.st = ExitStack()
        self.streams = {k: [] for k in (TEN, VEC, SCA, SYN, GPS)}
        self.al = Alloc(SBUF_BASE, SBUF_TOP)
        self._pending = []

        d = {}
        def dram(name, shape, dt=F32, kind="ExternalInput"):
            d[name] = nc.dram_tensor(name, list(shape), dt, kind=kind)
        dram("x_full", [S, H]); dram("x_own", [NTOK, H]); dram("pos0", [P, 1])
        dram("wqdT", [H, LAT]); dram("wkvdT", [H, LAT])
        dram("wqupT", [LAT, QHW]); dram("wkupT", [LAT, QHW]); dram("wvupT", [LAT, H])
        dram("woT", [H, H])
        dram("bqd", [P, 4]); dram("bkvd", [P, 4]); dram("bqup", [P, 8]); dram("bkup", [P, 8])
        dram("bv", [1, H]); dram("bo", [1, H])
        dram("rmsw", [P, 16]); dram("router", [H, NE])
        dram("gate_r", [NE, 8, P, 16, P]); dram("up_r", [NE, 8, P, 16, P])
        dram("down_w", [NE, FF, H])
        dram("swg_r", [8, P, 16, P]); dram("swu_r", [8, P, 16, P]); dram("swd_w", [FF, H])
        dram("esel", [8, NE * P])
        dram("out", [NTOK, H], F32, kind="ExternalOutput")
        self.tap_tensors = {}
        d["aosp"] = nc.dram_tensor("aosp", [8, P, H], BF16, kind="Internal")
        self.d = d

        self.PS0 = self.st.enter_context(nc.psum_tensor("PS0", [P, 2048], F32))
        self.PS1 = self.st.enter_context(nc.psum_tensor("PS1", [P, 2048], F32))
        self.PS0b = self.PS0.bitcast(BF16)
        self.PS1b = self.PS1.bitcast(BF16)

        for s in ("tM", "vM", "sM", "gM", "bar"):
            setattr(self, s, Sem(self.st.enter_context(nc.semaphore(s))))
        self.dpool = {
            GPS: [Sem(self.st.enter_context(nc.semaphore(f"dW{i}"))) for i in range(12)],
            SYN: [Sem(self.st.enter_context(nc.semaphore(f"dX{i}"))) for i in range(6)],
        }
        self.dcnt = {GPS: 0, SYN: 0}
        self.dW = None
        self.dX = None

    def em(self, eng, fn):
        self.streams[eng].append(fn)

    def wait(self, eng, sem, val):
        if val and val > 0:
            self.em(eng, lambda e, s=sem, v=val: e.wait_ge(s.h, v))

    def inc(self, eng, sem, fn):
        sem.n += 1
        tgt = sem.n
        self.em(eng, lambda e, f=fn, s=sem: f(e).then_inc(s.h, 1))
        return tgt

    def chn(self, eng, sem, fn):
        t = self.inc(eng, sem, fn)
        self.wait(eng, sem, t)
        return t

    def dma(self, eng, sem_ignored, out, in_):
        pool = self.dpool[eng]
        slot = pool[self.dcnt[eng] % len(pool)]
        self.dcnt[eng] += 1
        if slot.n > 0:
            self.wait(eng, slot, slot.n)
        slot.n += 16
        tgt = slot.n
        self.em(eng, lambda e, o=out, i=in_, s=slot: e.dma_start(out=o, in_=i).then_inc(s.h, 16))
        return (slot, tgt)

    def waitt(self, eng, token):
        if token:
            self.wait(eng, token[0], token[1])

    def barrier(self):
        tgt = self.bar.n + 5
        for eng in (TEN, VEC, SCA, SYN, GPS):
            self.em(eng, lambda e: e.drain())
            self.em(eng, lambda e: e.nop().then_inc(self.bar.h, 1))
            self.em(eng, lambda e, v=tgt: e.wait_ge(self.bar.h, v))
        self.bar.n = tgt

    def tap(self, name, ap):
        if not _TAPS:
            return
        t = self.nc.dram_tensor("tap_" + name, list(ap.shape), ap.dtype, kind="ExternalOutput")
        self.tap_tensors[name] = t
        tgt = self.dma(SYN, self.dX, t[tuple(slice(None) for _ in ap.shape)], ap)
        self.waitt(SYN, tgt)

    def sb(self, name, shape, dt, birth, death):
        per_part = int(np.prod(shape[1:])) * mybir.dt.size(dt)
        self.al.reserve(name, per_part, birth, death)
        self._pending.append((name, list(shape), dt))
        return name

    def _materialize(self):
        self.al.solve()
        made = {}
        for name, shape, dt in self._pending:
            made[name] = self.nc.alloc_sbuf_tensor_at(name, shape, dt, offset=self.al.offsets[name])
        def fix(v):
            if isinstance(v, str) and v in made:
                return made[v]
            if isinstance(v, list):
                return [fix(x) for x in v]
            return v
        for k in list(self.__dict__):
            self.__dict__[k] = fix(self.__dict__[k])

    def alloc_all(self):
        sb = self.sb
        self.ident_bf = sb("ident_bf", [P, P], BF16, P0, PD)
        self.ident_f32 = sb("ident_f32", [P, P], F32, P0, PD)
        self.ones_bf = sb("ones_bf", [1, P], BF16, P0, PD)
        self.ones_f32 = sb("ones_f32", [1, P], F32, P0, PD)
        self.onec_bf = sb("onec_bf", [P, 1], BF16, P0, PD)
        self.bv_row = sb("bv_row", [1, H], BF16, P0, P1C)
        self.bo_row = sb("bo_row", [1, H], BF16, PAB, PB)
        self.bqd = sb("bqd", [P, 4], F32, P0, PD)
        self.bkvd = sb("bkvd", [P, 4], F32, P0, PD)
        self.bqup = sb("bqup", [P, 8], F32, P0, PD)
        self.bkup = sb("bkup", [P, 8], F32, P0, PD)
        self.router_sb = sb("router_sb", [P, P], BF16, P0, PD)
        self.rmsw = sb("rmsw", [P, 16], F32, P0, PD)
        self.pos0_sb = sb("pos0_sb", [P, 1], F32, P0, PD)
        self.ebias = sb("ebias", [P, 1], F32, P0, PD)
        self.stats6 = sb("stats6", [P, 24], F32, P0, PD)
        self.mv2 = sb("mv2", [P, 2], F32, P0, PD)
        for nm in ("sc_a", "sc_b", "sc_c", "sc_d", "sc_e", "sc_f"):
            setattr(self, nm, sb(nm, [P, 1], F32, P0, PD))
        self.c_pi2 = sb("c_pi2", [P, 1], F32, P0, PD)
        self.c_eps5 = sb("c_eps5", [P, 1], F32, P0, PD)
        self.c_eps6 = sb("c_eps6", [P, 1], F32, P0, PD)
        self.m8 = sb("m8", [P, 8], F32, P0, PD)
        self.probs_u = sb("probs_u", [P, 8], F32, P0, PD)
        self.maskb = sb("maskb", [P, 8], F32, P0, PD)
        self.cw = sb("cw", [P, 8], F32, P0, PD)
        self.s4 = [sb(f"s4_{i}", [P, 4], F32, P0, PD) for i in range(2)]
        self.r4 = [sb(f"r4_{i}", [P, 4], F32, P0, PD) for i in range(2)]
        self.nrow = sb("nrow", [1, 8], F32, P0, PD)
        self.brow = sb("brow", [1, 2], F32, P0, PD)

        self.W1 = sb("W1", [P, 16 * LAT], BF16, P0, P1A)
        self.W2 = sb("W2", [P, 16 * LAT], BF16, P0, P1A)
        self.W3 = sb("W3", [P, 4 * QHW], BF16, P0, P1BQ)
        self.W4 = sb("W4", [P, 4 * QHW], BF16, P0, P1BK)
        self.W5 = sb("W5", [P, 4 * H], BF16, P0, P1C)
        self.ck = [sb(f"ck{c}", [P, S], F32, P0, P1BK) for c in range(2)]
        self.sk = [sb(f"sk{c}", [P, S], F32, P0, P1BK) for c in range(2)]
        self.cq = [sb(f"cq{c}", [P, NTOK], F32, P0, P1BQ) for c in range(2)]
        self.sq = [sb(f"sq{c}", [P, NTOK], F32, P0, P1BQ) for c in range(2)]
        self.rtmp = sb("rtmp", [P, S], F32, P0, P0)
        self.rsc_scr = sb("rsc_scr", [P, S], F32, P0, P0)

        self.x_in = [sb(f"x_in{i}", [P, H], F32, P1A, P1A) for i in range(3)]
        self.x_ln = [sb(f"x_ln{i}", [P, H], BF16, P1A, P1A) for i in range(4)]
        self.x_lnT = [sb(f"x_lnT{k}", [P, 512], BF16, P1A, P1A) for k in range(16)]
        self.q_latT = [sb(f"q_latT{m}", [P, NTOK], BF16, P1A, P1BQ) for m in range(4)]
        self.kv_latT = [sb(f"kv_latT{m}", [P, S], BF16, P1A, P1C) for m in range(4)]
        self.khA = sb("khA", [P, S], F32, P1BK, P1BQ)
        self.khB = sb("khB", [P, S], F32, P1BK, P1BQ)
        self.rsc_a = sb("rsc_a", [P, 1024], F32, P1BK, P1BQ)
        self.rsc_b = sb("rsc_b", [P, 1024], F32, P1BK, P1BQ)

        self.K_h = [sb(f"K{h}", [P, S], BF16, P1BK, PAA if h < 8 else PAB) for h in range(16)]
        self.q_h = [sb(f"q{h}", [P, NTOK], BF16, P1BQ, PAA if h < 8 else PAB) for h in range(16)]
        self.V_h = [sb(f"V{h}", [P, 16 * 129], BF16, P1C, PAA if h < 8 else PAB) for h in range(16)]

        self.A_T = sb("A_T", [P, 2048], BF16, PAA, PAB)
        self.sqs = sb("sqs", [P, 1024], BF16, P1C, P1C)
        self.ao = [sb(f"ao{i}", [P, H], BF16, PAA, PAB) for i in range(8)]

        self.WoT = sb("WoT", [P, 16 * H], BF16, PAB, PB)
        self.acc = sb("acc", [P, 8 * H], F32, PB, PD)
        self.hT = [sb(f"hT{k}", [P, NTOK], BF16, PB, PC) for k in range(16)]
        self.aoT = [sb(f"aoT{i}", [P, H], BF16, PB, PB) for i in range(2)]
        self.ao_r = [sb(f"ao_r{i}", [P, H], BF16, PB, PB) for i in range(2)]
        self.x_ot = [sb(f"x_ot{i}", [P, H], F32, PB, PB) for i in range(1)]
        self.h_tok2b = [sb(f"h_tok{i}", [P, H], BF16, PB, PB) for i in range(2)]
        self.cT_sb = sb("cT_sb", [8, NTOK], F32, PB, PC)
        self.sel_sb = sb("sel_sb", [8, NE * P], F32, PB, PC)
        self.sc_a2 = sb("sc_a2", [P, 1], F32, P0, PD)
        self.sc_b2 = sb("sc_b2", [P, 1], F32, P0, PD)

        self.gbuf = [sb(f"gbuf{i}", [P, 2048], BF16, PC, PC) for i in range(2)]
        self.ubuf = [sb(f"ubuf{i}", [P, 2048], BF16, PC, PC) for i in range(2)]
        self.dbuf = sb("dbuf", [P, 8 * H], BF16, PC, PC)
        self.c_bc = [sb(f"c_bc{i}", [P, NTOK], F32, PC, PC) for i in range(2)]
        self.t1b = [sb(f"t1b{i}", [P, FF], F32, PC, PC) for i in range(2)]
        self.t2b = sb("t2b", [P, FF], F32, PC, PC)
        self.a_fm = [sb(f"a_fm{m}", [P, NTOK], BF16, PC, PC) for m in range(8)]

        off = 0
        for h in range(8):
            self.al.pin(f"K{h}", off); off += 4096
        for h in range(8):
            self.al.pin(f"q{h}", off); off += 2048
        for h in range(8):
            self.al.pin(f"V{h}", off); off += 4128
        self.al.pin("WoT", 0)
        g2 = off
        for h in range(8, 16):
            self.al.pin(f"K{h}", off); off += 4096
        for h in range(8, 16):
            self.al.pin(f"q{h}", off); off += 2048
        for h in range(8, 16):
            self.al.pin(f"V{h}", off); off += 4128
        self.al.pin("acc", g2)
        for k in range(16):
            self.al.pin(f"hT{k}", g2 + 65536 + k * 2048)

    def p0(self):
        em, wait, inc, dma = self.em, self.wait, self.inc, self.dma
        d = self.d
        g0 = inc(GPS, self.gM, lambda e: e.memset(self.ident_bf[:, :], 0.0))
        wait(GPS, self.gM, g0)
        em(GPS, lambda e: e.affine_select(out=self.ident_bf[:, :], in_=self.ident_bf[:, :],
                                          pattern=[[-1, P]], compare_op=ALU.not_equal,
                                          fill=1.0, base=0, channel_multiplier=1))
        g1 = inc(GPS, self.gM, lambda e: e.memset(self.ident_f32[:, :], 0.0))
        wait(GPS, self.gM, g1)
        em(GPS, lambda e: e.affine_select(out=self.ident_f32[:, :], in_=self.ident_f32[:, :],
                                          pattern=[[-1, P]], compare_op=ALU.not_equal,
                                          fill=1.0, base=0, channel_multiplier=1))
        em(GPS, lambda e: e.memset(self.ones_bf[:, :], 1.0))
        em(GPS, lambda e: e.memset(self.ones_f32[:, :], 1.0))
        em(GPS, lambda e: e.memset(self.onec_bf[:, :], 1.0))
        em(GPS, lambda e: e.memset(self.c_pi2[:, :], math.pi / 2))
        em(GPS, lambda e: e.memset(self.c_eps5[:, :], 1e-5))
        em(GPS, lambda e: e.memset(self.c_eps6[:, :], 1e-6))
        em(GPS, lambda e: e.iota(self.rtmp[:, :], [[1, S]], base=0, channel_multiplier=0,
                                 allow_small_or_imprecise_dtypes=True))
        em(GPS, lambda e: e.iota(self.sc_c[:, :], [[1, 1]], base=0, channel_multiplier=1,
                                 allow_small_or_imprecise_dtypes=True))
        self.g_const = inc(GPS, self.gM, lambda e: e.iota(
            self.sc_d[:, :], [[1, 1]], base=P, channel_multiplier=1,
            allow_small_or_imprecise_dtypes=True))

        self.tok_bqd = dma(GPS, self.dW, self.bqd[:, :], d["bqd"][:, :])
        self.tok_bkvd = dma(GPS, self.dW, self.bkvd[:, :], d["bkvd"][:, :])
        self.tok_bqup = dma(GPS, self.dW, self.bqup[:, :], d["bqup"][:, :])
        self.tok_bkup = dma(GPS, self.dW, self.bkup[:, :], d["bkup"][:, :])
        self.tok_bv = dma(GPS, self.dW, self.bv_row[0:1, :], d["bv"][:, :])
        self.tok_rmsw = dma(GPS, self.dW, self.rmsw[:, :], d["rmsw"][:, :])
        self.tok_pos0 = dma(GPS, self.dW, self.pos0_sb[:, :], d["pos0"][:, :])
        self.tok_router = dma(GPS, self.dW,
                              self.router_sb[:, :].rearrange("p (k e) -> p k e", e=NE),
                              d["router"].rearrange("(k p) e -> p k e", p=P))
        self.tok_w1 = dma(GPS, self.dW, self.W1[:, :].rearrange("p (k m) -> p k m", m=LAT),
                          d["wqdT"].rearrange("(k p) m -> p k m", p=P))
        self.tok_w2 = dma(GPS, self.dW, self.W2[:, :].rearrange("p (k m) -> p k m", m=LAT),
                          d["wkvdT"].rearrange("(k p) m -> p k m", p=P))
        self.tok_w3 = dma(GPS, self.dW, self.W3[:, :].rearrange("p (k m) -> p k m", m=QHW),
                          d["wqupT"].rearrange("(k p) m -> p k m", p=P))
        self.tok_w4 = dma(GPS, self.dW, self.W4[:, :].rearrange("p (k m) -> p k m", m=QHW),
                          d["wkupT"].rearrange("(k p) m -> p k m", p=P))
        self.tok_w5 = dma(GPS, self.dW, self.W5[:, :].rearrange("p (k m) -> p k m", m=H),
                          d["wvupT"].rearrange("(k p) m -> p k m", p=P))

        wait(SCA, self.gM, self.g_const)
        LNB = math.log(10000.0) / 256.0
        self.chn(SCA, self.sM, lambda e: e.activation(self.sc_c[:, :], self.sc_c[:, :], AF.Exp, scale=-LNB))
        s_ifr = self.chn(SCA, self.sM, lambda e: e.activation(self.sc_d[:, :], self.sc_d[:, :], AF.Exp, scale=-LNB))

        wait(VEC, self.gM, self.g_const)
        self.waitt(VEC, self.tok_pos0)
        wait(VEC, self.sM, s_ifr)

        def tables(cos_t, sin_t, n, use_pos0):
            chn = self.chn
            for c in range(2):
                ifr = self.sc_c if c == 0 else self.sc_d
                th = self.rsc_scr
                if use_pos0:
                    chn(VEC, self.vM, lambda e, n=n: e.tensor_scalar(th[:, :n], self.rtmp[:, :n],
                                                                     self.pos0_sb[:, 0:1], None, ALU.add))
                    chn(VEC, self.vM, lambda e, n=n, ifr=ifr: e.tensor_scalar(th[:, :n], th[:, :n],
                                                                              ifr[:, 0:1], None, ALU.mult))
                else:
                    chn(VEC, self.vM, lambda e, n=n, ifr=ifr: e.tensor_scalar(th[:, :n], self.rtmp[:, :n],
                                                                              ifr[:, 0:1], None, ALU.mult))
                chn(VEC, self.vM, lambda e, n=n, c=c: e.tensor_scalar(sin_t[c][:, :n], th[:, :n], INV2PI, None, ALU.mult))
                chn(VEC, self.vM, lambda e, n=n, c=c: e.tensor_scalar(sin_t[c][:, :n], sin_t[c][:, :n],
                                                                      MAGIC, -MAGIC, ALU.add, ALU.add))
                chn(VEC, self.vM, lambda e, n=n, c=c:
                    e.scalar_tensor_tensor(sin_t[c][:, :n], sin_t[c][:, :n], -TWOPI,
                                           th[:, :n], ALU.mult, ALU.add))
                v_s = chn(VEC, self.vM, lambda e, n=n, c=c: e.tensor_scalar(
                    sin_t[c][:, :n], sin_t[c][:, :n], 3.141591, -3.141591, ALU.min, ALU.max))
                chn(VEC, self.vM, lambda e, n=n, c=c: e.tensor_scalar(cos_t[c][:, :n], th[:, :n],
                                                                      INV2PI, 0.25, ALU.mult, ALU.add))
                chn(VEC, self.vM, lambda e, n=n, c=c: e.tensor_scalar(cos_t[c][:, :n], cos_t[c][:, :n],
                                                                      MAGIC, -MAGIC, ALU.add, ALU.add))
                chn(VEC, self.vM, lambda e, n=n, c=c:
                    e.scalar_tensor_tensor(cos_t[c][:, :n], cos_t[c][:, :n], -TWOPI,
                                           th[:, :n], ALU.mult, ALU.add))
                v_c = chn(VEC, self.vM, lambda e, n=n, c=c: e.tensor_scalar(
                    cos_t[c][:, :n], cos_t[c][:, :n], 1.570795, -4.712387, ALU.min, ALU.max))
                wait(SCA, self.vM, v_s)
                chn(SCA, self.sM, lambda e, n=n, c=c: e.activation(sin_t[c][:, :n], sin_t[c][:, :n], AF.Sin))
                wait(SCA, self.vM, v_c)
                chn(SCA, self.sM, lambda e, n=n, c=c: e.activation(cos_t[c][:, :n], cos_t[c][:, :n],
                                                                   AF.Sin, bias=self.c_pi2[:, 0:1]))

        tables(self.ck, self.sk, S, False)
        tables(self.cq, self.sq, NTOK, True)
        self.s_tables = inc(SCA, self.sM, lambda e: e.nop())

    def p1a(self):
        em, wait, inc, dma = self.em, self.wait, self.inc, self.dma
        parts = globals().get("_P1A_PARTS", 3)
        d = self.d
        ln_s = {}
        xin_free = []
        self._p1a_prev_ev = {}
        trev = {}
        latev = {}
        ttr_last = {}
        tlat_last = {}

        def emit_ln_chunk(g, tc):
            src_d = d["x_full"] if g < 4 else d["x_own"]
            row0 = (g % 4) * 512 if g < 4 else (g - 4) * 512
            if True:
                buf = self.x_in[(g * 4 + tc) % 3]
                if len(xin_free) >= 3:
                    wait(SYN, self.sM, xin_free[-3])
                if g >= 1:
                    wait(SCA, self.tM, ttr_last[(g - 1, tc)])
                dtgt = dma(SYN, self.dX, buf[:, :], src_d[row0 + tc * P: row0 + (tc + 1) * P, :])
                self.waitt(VEC, dtgt)
                for gi in range(4):
                    self.chn(VEC, self.vM, lambda e, buf=buf, gi=gi: e.bn_stats(
                        self.stats6[:, gi * 6:(gi + 1) * 6],
                        buf[:, gi * 512:(gi + 1) * 512]))
                v_mv = self.chn(VEC, self.vM, lambda e: e.bn_aggr(
                    self.mv2[:, :], self.stats6[:, :].rearrange("p (g x) -> p g x", x=3)))
                wait(SCA, self.vM, v_mv)
                self.chn(SCA, self.sM, lambda e: e.activation(self.sc_a[:, :], self.mv2[:, 1:2], AF.Ln, bias=self.c_eps5[:, 0:1]))
                s_r = self.chn(SCA, self.sM, lambda e: e.activation(self.sc_a[:, :], self.sc_a[:, :], AF.Exp, scale=-0.5))
                wait(VEC, self.sM, s_r)
                v_b = self.chn(VEC, self.vM, lambda e: e.scalar_tensor_tensor(
                    self.sc_b[:, :], self.mv2[:, 0:1], -1.0, self.sc_a[:, :], ALU.mult, ALU.mult))
                wait(SCA, self.vM, v_b)
                s_x = self.chn(SCA, self.sM, lambda e, buf=buf, o=self.x_ln[tc]:
                               e.activation(o[:, :], buf[:, :], AF.Identity,
                                            bias=self.sc_b[:, 0:1], scale=self.sc_a[:, 0:1]))
                ln_s[(g, tc)] = s_x
                xin_free.append(s_x)

        def emit_trans_chunk(g, tc):
            if g > 0 and tc == 0:
                wait(TEN, self.sM, latev[(g - 1, 3)])
                wait(VEC, self.tM, tlat_last[g - 1])
            prev_ev = self._p1a_prev_ev
            if True:
                wait(TEN, self.sM, ln_s[(g, tc)])
                for wave in range(4):
                    pv = self.PS0b if wave % 2 == 0 else self.PS1b
                    if (tc * 4 + wave) % 2 in prev_ev:
                        wait(TEN, self.vM, prev_ev[(tc * 4 + wave) % 2])
                    for kk in range(3):
                        k = wave * 4 + kk
                        em(TEN, lambda e, k=k, tc=tc, pv=pv, bank=kk:
                           e.transpose(pv[:, bank * 1024: bank * 1024 + P],
                                       self.x_ln[tc][:, k * P:(k + 1) * P], self.ident_bf[:, :]))
                    k = wave * 4 + 3
                    t_tr = inc(TEN, self.tM, lambda e, k=k, tc=tc, pv=pv:
                               e.transpose(pv[:, 3 * 1024: 3 * 1024 + P],
                                           self.x_ln[tc][:, k * P:(k + 1) * P], self.ident_bf[:, :]))
                    wait(VEC, self.tM, t_tr)
                    last = None
                    for kk in range(4):
                        k = wave * 4 + kk
                        last = inc(VEC, self.vM, lambda e, k=k, tc=tc, pv=pv, kk=kk: e.tensor_copy(
                            self.x_lnT[k][:, tc * P:(tc + 1) * P], pv[:, kk * 1024: kk * 1024 + P]))
                    prev_ev[(tc * 4 + wave) % 2] = last
            trev[(g, 15)] = last
            ttr_last[(g, tc)] = t_tr

        def emit_latents(g):
            if g < 4:
                Wt, bias_t, outs, col0 = self.W2, self.bkvd, self.kv_latT, (g % 4) * 512
                self.waitt(TEN, self.tok_w2)
                self.waitt(SCA, self.tok_bkvd)
            else:
                Wt, bias_t, outs, col0 = self.W1, self.bqd, self.q_latT, (g - 4) * 512
                self.waitt(TEN, self.tok_w1)
                self.waitt(SCA, self.tok_bqd)
            wait(TEN, self.vM, trev[(g, 15)])
            for m in range(4):
                po = (m % 2) * 512
                if m >= 2:
                    wait(TEN, self.sM, latev[(g, m - 2)])
                for k in range(15):
                    em(TEN, lambda e, k=k, m=m, po=po, Wt=Wt:
                       e.matmul(self.PS0[:, po:po + 512],
                                Wt[:, k * LAT + m * P: k * LAT + (m + 1) * P],
                                self.x_lnT[k][:, :], start=(k == 0), stop=False))
                t_l = inc(TEN, self.tM, lambda e, m=m, po=po, Wt=Wt:
                          e.matmul(self.PS0[:, po:po + 512],
                                   Wt[:, 15 * LAT + m * P: 15 * LAT + (m + 1) * P],
                                   self.x_lnT[15][:, :], start=False, stop=True))
                wait(SCA, self.tM, t_l)
                latev[(g, m)] = self.chn(SCA, self.sM, lambda e, m=m, po=po, bias_t=bias_t, outs=outs, col0=col0:
                                         e.activation(outs[m][:, col0:col0 + 512], self.PS0[:, po:po + 512],
                                                      AF.Identity, bias=bias_t[:, m:m + 1]))
            tlat_last[g] = t_l

        for tc in range(4):
            emit_ln_chunk(0, tc)
            emit_trans_chunk(0, tc)
        for g in range(6):
            if g + 1 < 6:
                for tc in range(4):
                    emit_ln_chunk(g + 1, tc)
            emit_latents(g)
            if g + 1 < 6:
                for tc in range(4):
                    emit_trans_chunk(g + 1, tc)
        self.latev_last = latev[(5, 3)]

    def build_side(self, n, latT, Wup, bias_t, cos_t, sin_t, outK, w_req):
        em, wait, inc = self.em, self.wait, self.inc
        self.waitt(SCA, self.tok_bkup if bias_t is self.bkup else self.tok_bqup)
        nsl_cnt = n // 512
        prev_rope = None
        prev_casts = None
        for j in range(4):
            evA = evB = None
            for which in range(2):
                jj = j + 4 * which
                ps = self.PS0 if which == 0 else self.PS1
                dst = self.khA if which == 0 else self.khB
                self.waitt(TEN, w_req)
                if prev_rope is not None and which == 0:
                    wait(SCA, self.vM, prev_rope)
                for k in range(4):
                    for nsl in range(nsl_cnt):
                        if k == 3 and nsl == nsl_cnt - 1:
                            continue
                        em(TEN, lambda e, k=k, nsl=nsl, jj=jj, ps=ps, latT=latT, Wup=Wup:
                           e.matmul(ps[:, nsl * 512:(nsl + 1) * 512],
                                    Wup[:, k * QHW + jj * P: k * QHW + (jj + 1) * P],
                                    latT[k][:, nsl * 512:(nsl + 1) * 512],
                                    start=(k == 0), stop=(k == 3)))
                t_kh = inc(TEN, self.tM, lambda e, jj=jj, ps=ps, latT=latT, Wup=Wup, nsl=nsl_cnt - 1:
                           e.matmul(ps[:, nsl * 512:(nsl + 1) * 512],
                                    Wup[:, 3 * QHW + jj * P: 3 * QHW + (jj + 1) * P],
                                    latT[3][:, nsl * 512:(nsl + 1) * 512],
                                    start=False, stop=True))
                wait(SCA, self.tM, t_kh)
                ev = self.chn(SCA, self.sM, lambda e, dst=dst, jj=jj, ps=ps, n=n, bias_t=bias_t:
                              e.activation(dst[:, :n], ps[:, :n], AF.Identity, bias=bias_t[:, jj:jj + 1]))
                if which == 0:
                    evA = ev
                else:
                    evB = ev
            s_c1 = inc(SCA, self.sM, lambda e, j=j, n=n, outK=outK:
                       e.activation(outK[j][:, :n], self.khA[:, :n], AF.Identity))
            s_c2 = inc(SCA, self.sM, lambda e, j=j, n=n, outK=outK:
                       e.activation(outK[j + 4][:, :n], self.khB[:, :n], AF.Identity))
            c_t, s_t = cos_t[j % 2], sin_t[j % 2]
            wait(VEC, self.sM, evB)
            for h0 in range(0, n, 1024):
                h1 = h0 + 1024
                self.chn(VEC, self.vM, lambda e, h0=h0, h1=h1, c_t=c_t: e.tensor_mul(
                    self.rsc_a[:, :], self.khA[:, h0:h1], c_t[:, h0:h1]))
                self.chn(VEC, self.vM, lambda e, h0=h0, h1=h1, s_t=s_t: e.tensor_mul(
                    self.rsc_b[:, :], self.khB[:, h0:h1], s_t[:, h0:h1]))
                self.chn(VEC, self.vM, lambda e, j=j, h0=h0, h1=h1, outK=outK: e.tensor_sub(
                    outK[8 + j][:, h0:h1], self.rsc_a[:, :], self.rsc_b[:, :]))
                self.chn(VEC, self.vM, lambda e, h0=h0, h1=h1, s_t=s_t: e.tensor_mul(
                    self.rsc_a[:, :], self.khA[:, h0:h1], s_t[:, h0:h1]))
                self.chn(VEC, self.vM, lambda e, h0=h0, h1=h1, c_t=c_t: e.tensor_mul(
                    self.rsc_b[:, :], self.khB[:, h0:h1], c_t[:, h0:h1]))
                prev_rope = self.chn(VEC, self.vM, lambda e, j=j, h0=h0, h1=h1, outK=outK:
                                e.tensor_add(outK[12 + j][:, h0:h1], self.rsc_a[:, :], self.rsc_b[:, :]))
            prev_casts = s_c2
            wait(TEN, self.sM, prev_casts)

    def p1c(self):
        em, wait, inc, dma = self.em, self.wait, self.inc, self.dma
        for h in range(15):
            em(GPS, lambda e, h=h: e.memset(
                self.V_h[h][:, :].rearrange("p (kc x) -> p kc x", x=129)[:, :, 128:129], 1.0))
        g_v1 = inc(GPS, self.gM, lambda e: e.memset(
            self.V_h[15][:, :].rearrange("p (kc x) -> p kc x", x=129)[:, :, 128:129], 1.0))
        wait(TEN, self.gM, g_v1)
        self.waitt(TEN, self.tok_w5)
        self.waitt(TEN, self.tok_bv)
        vev = {}
        for tc in range(16):
            ps = self.PS0 if tc % 2 == 0 else self.PS1
            if tc >= 2:
                wait(TEN, self.sM, vev[tc - 2])
            for nsl in range(4):
                for k in range(4):
                    em(TEN, lambda e, k=k, nsl=nsl, tc=tc, ps=ps:
                       e.matmul(ps[:, nsl * 512:(nsl + 1) * 512],
                                self.kv_latT[k][:, tc * P:(tc + 1) * P],
                                self.W5[:, k * H + nsl * 512: k * H + (nsl + 1) * 512],
                                start=(k == 0), stop=False))
                if nsl < 3:
                    em(TEN, lambda e, nsl=nsl, ps=ps:
                       e.matmul(ps[:, nsl * 512:(nsl + 1) * 512], self.ones_bf[0:1, :],
                                self.bv_row[0:1, nsl * 512:(nsl + 1) * 512], start=False, stop=True))
            t_v = inc(TEN, self.tM, lambda e, ps=ps:
                      e.matmul(ps[:, 3 * 512:4 * 512], self.ones_bf[0:1, :],
                               self.bv_row[0:1, 3 * 512:4 * 512], start=False, stop=True))
            wait(SCA, self.tM, t_v)
            last = None
            for h in range(16):
                last = inc(SCA, self.sM, lambda e, h=h, tc=tc, ps=ps:
                           e.activation(self.V_h[h][:, tc * 129: tc * 129 + 128],
                                        ps[:, h * P:(h + 1) * P], AF.Identity))
            vev[tc] = last
        self.v_done = vev[15]

    def bound_prep(self):
        em, wait, inc = self.em, self.wait, self.inc
        wait(TEN, self.sM, self.v_done)
        wait(VEC, self.sM, self.v_done)
        t_sq = None
        for h in range(16):
            for half in range(2):
                if t_sq is not None:
                    wait(VEC, self.tM, t_sq)
                v_s = self.chn(VEC, self.vM, lambda e, h=h, half=half: e.tensor_mul(
                    self.sqs[:, :], self.K_h[h][:, half * 1024:(half + 1) * 1024],
                    self.K_h[h][:, half * 1024:(half + 1) * 1024]))
                wait(TEN, self.vM, v_s)
                em(TEN, lambda e, h=h, half=half:
                   e.matmul(self.PS1[0:1, (half * 2) * 512:(half * 2 + 1) * 512],
                            self.onec_bf[:, 0:1],
                            self.sqs[:, 0:512], start=(h == 0), stop=(h == 15)))
                t_sq = inc(TEN, self.tM, lambda e, h=h, half=half:
                           e.matmul(self.PS1[0:1, (half * 2 + 1) * 512:(half * 2 + 2) * 512],
                                    self.onec_bf[:, 0:1],
                                    self.sqs[:, 512:1024], start=(h == 0), stop=(h == 15)))
        t_k = t_sq
        wait(VEC, self.tM, t_k)
        for nsl in range(4):
            self.chn(VEC, self.vM, lambda e, nsl=nsl: e.tensor_reduce(
                self.nrow[0:1, nsl:nsl + 1], self.PS1[0:1, nsl * 512:(nsl + 1) * 512], AX.X, ALU.max))
        v_k = self.chn(VEC, self.vM, lambda e: e.tensor_reduce(self.brow[0:1, 1:2],
                                                               self.nrow[0:1, 0:4], AX.X, ALU.max))
        for h in range(16):
            wait(VEC, self.tM, t_sq)
            v_s = self.chn(VEC, self.vM, lambda e, h=h: e.tensor_mul(self.sqs[:, :NTOK],
                                                                      self.q_h[h][:, :], self.q_h[h][:, :]))
            wait(TEN, self.vM, v_s)
            em(TEN, lambda e, h=h:
               e.matmul(self.PS0[0:1, 0:512], self.onec_bf[:, 0:1],
                        self.sqs[:, 0:512], start=(h == 0), stop=(h == 15)))
            t_sq = inc(TEN, self.tM, lambda e, h=h:
                       e.matmul(self.PS0[0:1, 512:1024], self.onec_bf[:, 0:1],
                                self.sqs[:, 512:1024], start=(h == 0), stop=(h == 15)))
        t_q = t_sq
        wait(VEC, self.tM, t_q)
        for nsl in range(2):
            self.chn(VEC, self.vM, lambda e, nsl=nsl: e.tensor_reduce(
                self.nrow[0:1, 4 + nsl:5 + nsl], self.PS0[0:1, nsl * 512:(nsl + 1) * 512], AX.X, ALU.max))
        v_q = self.chn(VEC, self.vM, lambda e: e.tensor_reduce(self.brow[0:1, 0:1],
                                                               self.nrow[0:1, 4:6], AX.X, ALU.max))
        wait(SCA, self.vM, v_q)
        wait(SCA, self.vM, v_k)
        self.chn(SCA, self.sM, lambda e: e.activation(self.brow[0:1, 0:1], self.brow[0:1, 0:1], AF.Ln))
        s_l = self.chn(SCA, self.sM, lambda e: e.activation(self.brow[0:1, 1:2], self.brow[0:1, 1:2], AF.Ln))
        wait(VEC, self.sM, s_l)
        v_a = self.chn(VEC, self.vM, lambda e: e.tensor_add(self.brow[0:1, 0:1],
                                                            self.brow[0:1, 0:1], self.brow[0:1, 1:2]))
        wait(SCA, self.vM, v_a)
        s_e = self.chn(SCA, self.sM, lambda e: e.activation(self.brow[0:1, 0:1], self.brow[0:1, 0:1],
                                                            AF.Exp, scale=0.5))
        wait(VEC, self.sM, s_e)
        v_e = self.chn(VEC, self.vM, lambda e: e.tensor_scalar(self.brow[0:1, 0:1], self.brow[0:1, 0:1],
                                                               -1.01 * SCL, None, ALU.mult))
        wait(TEN, self.vM, v_e)
        t_e = inc(TEN, self.tM, lambda e: e.matmul(self.PS0[:, 0:1], self.ones_f32[0:1, :],
                                                   self.brow[0:1, 0:1], start=True, stop=True))
        wait(VEC, self.tM, t_e)
        self.v_ebias = inc(VEC, self.vM, lambda e: e.tensor_copy(self.ebias[:, :], self.PS0[:, 0:1]))

    def pa(self):
        em, wait, inc, dma = self.em, self.wait, self.inc, self.dma
        wait(SCA, self.vM, self.v_ebias)
        exp_abs, spair, pvpair, r4_abs, evac_abs = {}, {}, {}, {}, {}

        def pv_pair(R, jp, h_, qt):
            wait(TEN, self.sM, exp_abs[(R, jp)])
            if jp == 0 and R >= 1:
                wait(TEN, self.vM, r4_abs[R - 1])
                wait(TEN, self.vM, evac_abs[R - 1])
            for kc in (2 * jp, 2 * jp + 1):
                for qc in range(4):
                    if kc == 2 * jp + 1 and qc == 3:
                        continue
                    em(TEN, lambda e, kc=kc, qc=qc, jp=jp, h_=h_:
                       e.matmul(self.PS1[:, qc * 512: qc * 512 + 129],
                                self.A_T[:, (jp % 2) * 1024 + (kc % 2) * 512 + qc * P:
                                         (jp % 2) * 1024 + (kc % 2) * 512 + (qc + 1) * P],
                                self.V_h[h_][:, kc * 129:(kc + 1) * 129],
                                start=(kc == 0), stop=(kc == 15)))
            kc = 2 * jp + 1
            pvpair[(R, jp)] = inc(TEN, self.tM, lambda e, kc=kc, jp=jp, h_=h_:
                e.matmul(self.PS1[:, 3 * 512: 3 * 512 + 129],
                         self.A_T[:, (jp % 2) * 1024 + (kc % 2) * 512 + 3 * P:
                                  (jp % 2) * 1024 + (kc % 2) * 512 + 4 * P],
                         self.V_h[h_][:, kc * 129:(kc + 1) * 129],
                         start=False, stop=(kc == 15)))

        for h_ in range(16):
            for qt in range(2):
                R = h_ * 2 + qt
                for jp in range(8):
                    g = R * 8 + jp
                    if g >= 2:
                        pR, pjp = divmod(g - 2, 8)
                        wait(TEN, self.sM, exp_abs[(pR, pjp)])
                    kc0 = 2 * jp
                    em(TEN, lambda e, h_=h_, qt=qt, kc=kc0:
                       e.matmul(self.PS0[:, (kc % 4) * 512:((kc % 4) + 1) * 512],
                                self.K_h[h_][:, kc * P:(kc + 1) * P],
                                self.q_h[h_][:, qt * 512:(qt + 1) * 512],
                                start=True, stop=True))
                    spair[(R, jp)] = inc(TEN, self.tM, lambda e, h_=h_, qt=qt, kc=kc0 + 1:
                        e.matmul(self.PS0[:, (kc % 4) * 512:((kc % 4) + 1) * 512],
                                 self.K_h[h_][:, kc * P:(kc + 1) * P],
                                 self.q_h[h_][:, qt * 512:(qt + 1) * 512],
                                 start=True, stop=True))
                    wait(SCA, self.tM, spair[(R, jp)])
                    if g >= 2:
                        pR, pjp = divmod(g - 2, 8)
                        wait(SCA, self.tM, pvpair[(pR, pjp)])
                    exp_abs[(R, jp)] = inc(SCA, self.sM, lambda e, jp=jp:
                        e.activation(self.A_T[:, (jp % 2) * 1024:(jp % 2) * 1024 + 1024],
                                     self.PS0[:, (jp % 2) * 1024:(jp % 2) * 1024 + 1024],
                                     AF.Exp, bias=self.ebias[:, 0:1], scale=SCL))
                    if jp >= 1:
                        pv_pair(R, jp - 1, h_, qt)
                pv_pair(R, 7, h_, qt)
                wait(VEC, self.tM, pvpair[(R, 7)])
                sb_ = self.s4[R % 2]
                rb_ = self.r4[R % 2]
                for qc in range(3):
                    em(VEC, lambda e, sb_=sb_, qc=qc: e.tensor_copy(
                        sb_[:, qc:qc + 1], self.PS1[:, qc * 512 + 128: qc * 512 + 129]))
                self.chn(VEC, self.vM, lambda e, sb_=sb_: e.tensor_copy(
                    sb_[:, 3:4], self.PS1[:, 3 * 512 + 128: 3 * 512 + 129]))
                r4_abs[R] = self.chn(VEC, self.vM, lambda e, sb_=sb_, rb_=rb_: e.reciprocal(rb_[:, :], sb_[:, :]))
                last = None
                for qc in range(4):
                    qcg = qt * 4 + qc
                    last = inc(VEC, self.vM, lambda e, qc=qc, qcg=qcg, h_=h_, rb_=rb_:
                               e.tensor_scalar(self.ao[qcg][:, h_ * P:(h_ + 1) * P],
                                               self.PS1[:, qc * 512: qc * 512 + 128],
                                               rb_[:, qc:qc + 1], None, ALU.mult))
                evac_abs[R] = last
            if h_ == 7:
                self.wait(GPS, self.tM, pvpair[(15, 7)])
                self.d_wo = dma(GPS, self.dW,
                                self.WoT[:, :].rearrange("p (k m) -> p k m", m=H),
                                self.d["woT"].rearrange("(k p) m -> p k m", p=P))
                self.d_bo = dma(GPS, self.dW, self.bo_row[0:1, :], self.d["bo"][:, :])
        self.ao_spill = {}
        for qcg in range(8):
            wait(SYN, self.vM, evac_abs[30 if qcg < 4 else 31])
            self.ao_spill[qcg] = dma(SYN, self.dX, self.d["aosp"][qcg], self.ao[qcg][:, :])
        self.attn_done = evac_abs[31]

    def pb(self):
        em, wait, inc, dma = self.em, self.wait, self.inc, self.dma
        d = self.d
        self.waitt(TEN, self.d_wo)
        self.waitt(TEN, self.d_bo)
        self.waitt(TEN, self.tok_router)
        self.waitt(SCA, self.tok_rmsw)
        for qcg in range(8):
            self.waitt(SYN, self.ao_spill[qcg])
        self.d_sel = dma(GPS, self.dW, self.sel_sb[:, :], d["esel"][:, :])
        hT_done = {}
        xo_dma = {}
        ao_dma = {}
        aotr = {}
        opev = {}
        aoev = {}
        opmm = {}
        bstage1 = {}
        bstage2 = {}
        s_h_d = {}
        bstage = {}

        def stage_a1(tc):
            if tc >= 2:
                wait(SYN, self.tM, aotr[tc - 2])
            ao_dma[tc] = dma(SYN, self.dX, self.ao_r[tc % 2][:, :], d["aosp"][tc])
            self.waitt(TEN, ao_dma[tc])
            if tc >= 1:
                wait(TEN, self.vM, aoev[tc - 1])
            if tc >= 2:
                wait(TEN, self.vM, opev[tc - 2])
                wait(TEN, self.tM, opmm[tc - 2])
            if tc >= 3:
                wait(TEN, self.sM, bstage1[tc - 3][0])
                wait(TEN, self.sM, bstage2[tc - 3][0])
                wait(TEN, self.vM, bstage2[tc - 3][1])
            for wave in range(2):
                for kk in range(7):
                    k = wave * 8 + kk
                    pv = self.PS0b if kk < 4 else self.PS1b
                    bank = kk % 4
                    em(TEN, lambda e, k=k, tc=tc, pv=pv, bank=bank:
                       e.transpose(pv[:, bank * 1024: bank * 1024 + P],
                                   self.ao_r[tc % 2][:, k * P:(k + 1) * P], self.ident_bf[:, :]))
                k = wave * 8 + 7
                t_w = inc(TEN, self.tM, lambda e, k=k, tc=tc:
                          e.transpose(self.PS1b[:, 3 * 1024: 3 * 1024 + P],
                                      self.ao_r[tc % 2][:, k * P:(k + 1) * P], self.ident_bf[:, :]))
                wait(VEC, self.tM, t_w)
                for kk in range(7):
                    k = wave * 8 + kk
                    pv = self.PS0b if kk < 4 else self.PS1b
                    bank = kk % 4
                    em(VEC, lambda e, k=k, tc=tc, pv=pv, bank=bank: e.tensor_copy(
                        self.aoT[tc % 2][:, k * P:(k + 1) * P], pv[:, bank * 1024: bank * 1024 + P]))
                k = wave * 8 + 7
                v_w = inc(VEC, self.vM, lambda e, k=k, tc=tc: e.tensor_copy(
                    self.aoT[tc % 2][:, k * P:(k + 1) * P], self.PS1b[:, 3 * 1024: 3 * 1024 + P]))
                wait(TEN, self.vM, v_w)
                if wave == 1:
                    aotr[tc] = t_w
                    aoev[tc] = v_w

        def stage_a2(tc):
            wait(TEN, self.vM, aoev[tc])
            if tc + 1 in aoev:
                wait(TEN, self.vM, aoev[tc + 1])
            if tc >= 1:
                wait(TEN, self.vM, opev[tc - 1])
                wait(TEN, self.sM, bstage1[tc - 1][0])
            for nsl in range(4):
                for k in range(16):
                    em(TEN, lambda e, k=k, nsl=nsl, tc=tc:
                       e.matmul(self.PS0[:, nsl * 512:(nsl + 1) * 512],
                                self.aoT[tc % 2][:, k * P:(k + 1) * P],
                                self.WoT[:, k * H + nsl * 512: k * H + (nsl + 1) * 512],
                                start=(k == 0), stop=False))
                if nsl < 3:
                    em(TEN, lambda e, nsl=nsl:
                       e.matmul(self.PS0[:, nsl * 512:(nsl + 1) * 512], self.ones_bf[0:1, :],
                                self.bo_row[0:1, nsl * 512:(nsl + 1) * 512], start=False, stop=True))
            t_op = inc(TEN, self.tM, lambda e:
                       e.matmul(self.PS0[:, 3 * 512:4 * 512], self.ones_bf[0:1, :],
                                self.bo_row[0:1, 3 * 512:4 * 512], start=False, stop=True))
            opmm[tc] = t_op
            wait(VEC, self.tM, t_op)
            self.waitt(VEC, xo_dma[tc])
            a0 = tc * H
            opev[tc] = self.chn(VEC, self.vM, lambda e, tc=tc, a0=a0: e.tensor_add(
                self.acc[:, a0:a0 + H], self.PS0[:, :], self.x_ot[0][:, :]))
            if tc + 1 < 8:
                wait(SYN, self.vM, opev[tc])
                xo_dma[tc + 1] = dma(SYN, self.dX, self.x_ot[0][:, :],
                                     d["x_own"][(tc + 1) * P:(tc + 2) * P, :])
            for gi in range(4):
                self.chn(VEC, self.vM, lambda e, a0=a0, gi=gi: e.bn_stats(
                    self.stats6[:, gi * 6:(gi + 1) * 6],
                    self.acc[:, a0 + gi * 512: a0 + (gi + 1) * 512]))
            v_mv = self.chn(VEC, self.vM, lambda e: e.bn_aggr(
                self.mv2[:, :], self.stats6[:, :].rearrange("p (g x) -> p g x", x=3)))
            wait(SCA, self.vM, v_mv)
            self.chn(SCA, self.sM, lambda e: e.activation(self.sc_a[:, :], self.mv2[:, 1:2], AF.Ln, bias=self.c_eps5[:, 0:1]))
            s_r = self.chn(SCA, self.sM, lambda e: e.activation(self.sc_a[:, :], self.sc_a[:, :],
                                                                AF.Exp, scale=-0.5))
            wait(VEC, self.sM, s_r)
            self.chn(VEC, self.vM, lambda e: e.tensor_mul(self.sc_c[:, :], self.sc_a[:, :], self.sc_a[:, :]))
            v_m = self.chn(VEC, self.vM, lambda e: e.tensor_mul(self.sc_c[:, :], self.sc_c[:, :],
                                                                self.mv2[:, 1:2]))
            wait(SCA, self.vM, v_m)
            self.chn(SCA, self.sM, lambda e: e.activation(self.sc_d[:, :], self.sc_c[:, :], AF.Ln, bias=self.c_eps6[:, 0:1]))
            s_rr = self.chn(SCA, self.sM, lambda e: e.activation(self.sc_d[:, :], self.sc_d[:, :],
                                                                 AF.Exp, scale=-0.5))
            wait(VEC, self.sM, s_rr)
            self.chn(VEC, self.vM, lambda e: e.tensor_mul(self.sc_e[:, :], self.sc_a[:, :], self.sc_d[:, :]))
            v_sb = self.chn(VEC, self.vM, lambda e: e.scalar_tensor_tensor(
                self.sc_f[:, :], self.mv2[:, 0:1], -1.0, self.sc_e[:, :], ALU.mult, ALU.mult))
            wait(SCA, self.vM, v_sb)
            if tc >= 2:
                wait(SCA, self.tM, bstage1[tc - 2][1])
            s_h_d[tc] = self.chn(SCA, self.sM, lambda e, a0=a0, tc=tc: e.activation(
                self.h_tok2[tc % 2][:, :], self.acc[:, a0:a0 + H], AF.Identity,
                bias=self.sc_f[:, 0:1], scale=self.sc_e[:, 0:1]))
            if tc == 0 and _TAPS:
                self.wait(SYN, self.sM, s_h_d[tc])
                self.tap("htok0", self.h_tok2[0][:, :])

        def stage_b1(tc):
            wait(TEN, self.sM, s_h_d[tc])
            wait(TEN, self.vM, opev[tc])
            if tc + 2 in aoev:
                wait(TEN, self.vM, aoev[tc + 2])
            if tc >= 1:
                wait(TEN, self.vM, bstage2[tc - 1][1])
                wait(TEN, self.sM, bstage2[tc - 1][0])
            s_ev = None
            for wave in range(2):
                if wave > 0:
                    wait(TEN, self.sM, s_ev)
                for kk in range(7):
                    k = wave * 8 + kk
                    pv = self.PS0b if kk < 4 else self.PS1b
                    bank = kk % 4
                    em(TEN, lambda e, k=k, tc=tc, pv=pv, bank=bank:
                       e.transpose(pv[:, bank * 1024: bank * 1024 + P],
                                   self.h_tok2[tc % 2][:, k * P:(k + 1) * P], self.ident_bf[:, :]))
                k = wave * 8 + 7
                t_h = inc(TEN, self.tM, lambda e, k=k, tc=tc:
                          e.transpose(self.PS1b[:, 3 * 1024: 3 * 1024 + P],
                                      self.h_tok2[tc % 2][:, k * P:(k + 1) * P], self.ident_bf[:, :]))
                wait(SCA, self.tM, t_h)
                for kk in range(8):
                    k = wave * 8 + kk
                    pv = self.PS0b if kk < 4 else self.PS1b
                    bank = kk % 4
                    s_ev = inc(SCA, self.sM, lambda e, k=k, kk=kk, tc=tc, pv=pv, bank=bank:
                               e.activation(self.hT[k][:, tc * P:(tc + 1) * P],
                                            pv[:, bank * 1024: bank * 1024 + P],
                                            AF.Identity, scale=self.rmsw[:, k:k + 1]))
            hT_done[tc] = s_ev
            bstage1[tc] = (s_ev, t_h)

        def stage_b2(tc):
            s_ev = bstage1[tc][0]
            wait(TEN, self.sM, s_ev)
            if tc + 2 in aoev:
                wait(TEN, self.vM, aoev[tc + 2])
            for k in range(15):
                em(TEN, lambda e, k=k, tc=tc:
                   e.matmul(self.PS1[:, 0:8], self.hT[k][:, tc * P:(tc + 1) * P],
                            self.router_sb[:, k * 8:(k + 1) * 8], start=(k == 0), stop=False))
            t_rt = inc(TEN, self.tM, lambda e, tc=tc:
                       e.matmul(self.PS1[:, 0:8], self.hT[15][:, tc * P:(tc + 1) * P],
                                self.router_sb[:, 15 * 8:16 * 8], start=False, stop=True))
            wait(VEC, self.tM, t_rt)
            v_nm = self.chn(VEC, self.vM, lambda e: e.tensor_reduce(
                self.sc_a2[:, :], self.PS1[:, 0:8], AX.X, ALU.max, negate=True))
            wait(SCA, self.vM, v_nm)
            s_pu = self.chn(SCA, self.sM, lambda e: e.activation(self.probs_u[:, :], self.PS1[:, 0:8],
                                                                 AF.Exp, bias=self.sc_a2[:, 0:1]))
            wait(VEC, self.sM, s_pu)
            if tc == 0 and _TAPS:
                self.wait(SYN, self.sM, s_pu)
                self.tap("probs0", self.probs_u[:, :])
            self.chn(VEC, self.vM, lambda e: e.max(out=self.m8[:, :], in_=self.probs_u[:, :]))
            self.chn(VEC, self.vM, lambda e: e.tensor_scalar(self.maskb[:, :], self.probs_u[:, :],
                                                             self.m8[:, 1:2], None, ALU.is_ge))
            self.chn(VEC, self.vM, lambda e: e.tensor_mul(self.cw[:, :], self.probs_u[:, :], self.maskb[:, :]))
            self.chn(VEC, self.vM, lambda e: e.tensor_add(self.sc_b2[:, :], self.m8[:, 0:1], self.m8[:, 1:2]))
            self.chn(VEC, self.vM, lambda e: e.reciprocal(self.sc_b2[:, :], self.sc_b2[:, :]))
            v_cw = self.chn(VEC, self.vM, lambda e: e.tensor_scalar(self.cw[:, :], self.cw[:, :],
                                                                    self.sc_b2[:, 0:1], None, ALU.mult))
            wait(TEN, self.vM, v_cw)
            t_c = inc(TEN, self.tM, lambda e: e.transpose(self.PS1[0:8, 1024:1024 + P],
                                                          self.cw[:, :], self.ident_f32[:, :]))
            wait(VEC, self.tM, t_c)
            v_ct = inc(VEC, self.vM, lambda e, tc=tc: e.tensor_copy(
                self.cT_sb[:, tc * P:(tc + 1) * P], self.PS1[0:8, 1024:1024 + P]))
            bstage2[tc] = (s_pu, v_ct)

        xo_dma[0] = dma(SYN, self.dX, self.x_ot[0][:, :], d["x_own"][0:P, :])
        stage_a1(0)
        stage_a1(1)
        stage_a2(0)
        for tc in range(8):
            if tc + 2 < 8:
                stage_a1(tc + 2)
            stage_b1(tc)
            if tc + 1 < 8:
                stage_a2(tc + 1)
            stage_b2(tc)
        self.hT_all = hT_done[7]
        self.cT_all = self.vM.n

    def pc(self):
        em, wait, inc, dma = self.em, self.wait, self.inc, self.dma
        d = self.d
        guev = {}
        dnev = {}
        gu_mm = {}
        gu_dma = {}
        dn_dma = {}
        cb_ev = {}

        def gu_src(e_, m):
            if e_ == 8:
                return (d["swg_r"][m].rearrange("p k c -> p (k c)"),
                        d["swu_r"][m].rearrange("p k c -> p (k c)"))
            return (d["gate_r"][e_, m].rearrange("p k c -> p (k c)"),
                    d["up_r"][e_, m].rearrange("p k c -> p (k c)"))

        def emit_gu_dma(gm):
            if gm >= 9 * 8:
                return
            e_, m = divmod(gm, 8)
            if gm >= 2:
                pe, pm = divmod(gm - 2, 8)
                self.wait(GPS, self.tM, gu_mm[(pe, pm)])
            sg, su = gu_src(e_, m)
            tg = dma(GPS, self.dW, self.gbuf[m % 2][:, :], sg)
            tu = dma(GPS, self.dW, self.ubuf[m % 2][:, :], su)
            gu_dma[(e_, m)] = (tg, tu)

        emit_gu_dma(0)
        emit_gu_dma(1)
        dn_dma[0] = dma(GPS, self.dW,
                        self.dbuf[:, :].rearrange("p (m h) -> p m h", h=H),
                        d["down_w"][0].rearrange("(m p) h -> p m h", p=P))

        dn_last = None
        for e_ in range(9):
            shared = (e_ == 8)
            if not shared:
                if e_ >= 1:
                    wait(TEN, self.vM, dnev[(e_ - 1, 7)])
                if e_ == 0:
                    self.waitt(TEN, self.d_sel)
                em(TEN, lambda e, e_=e_:
                   e.matmul(self.PS1[:, 1024:1536],
                            self.sel_sb[0:8, e_ * P:(e_ + 1) * P],
                            self.cT_sb[0:8, 0:512], start=True, stop=True))
                t_cb = inc(TEN, self.tM, lambda e, e_=e_:
                           e.matmul(self.PS1[:, 1536:2048],
                                    self.sel_sb[0:8, e_ * P:(e_ + 1) * P],
                                    self.cT_sb[0:8, 512:1024], start=True, stop=True))
                wait(VEC, self.tM, t_cb)
                cb = self.c_bc[e_ % 2]
                cb_ev[e_] = inc(VEC, self.vM, lambda e, cb=cb:
                                e.tensor_copy(cb[:, :], self.PS1[:, 1024:2048]))
            for m in range(8):
                ps_g = self.PS0 if m % 2 == 0 else self.PS1
                self.waitt(TEN, gu_dma[(e_, m)][0])
                self.waitt(TEN, gu_dma[(e_, m)][1])
                if m >= 2:
                    wait(TEN, self.vM, guev[(e_, m - 2)])
                elif e_ > 0:
                    wait(TEN, self.vM, guev[(e_ - 1, 6 + m)])
                    wait(TEN, self.vM, dnev[(e_ - 1, 6 + m)])
                if m == 1 and not shared:
                    wait(TEN, self.vM, cb_ev[e_])
                gb, ub = self.gbuf[m % 2], self.ubuf[m % 2]
                for k in range(16):
                    for half in range(2):
                        em(TEN, lambda e, k=k, half=half, ps_g=ps_g, gb=gb:
                           e.matmul(ps_g[:, half * 512:(half + 1) * 512],
                                    gb[:, k * P:(k + 1) * P],
                                    self.hT[k][:, half * 512:(half + 1) * 512],
                                    start=(k == 0), stop=(k == 15)))
                for k in range(16):
                    for half in range(2):
                        if k == 15 and half == 1:
                            continue
                        em(TEN, lambda e, k=k, half=half, ps_g=ps_g, ub=ub:
                           e.matmul(ps_g[:, 1024 + half * 512: 1024 + (half + 1) * 512],
                                    ub[:, k * P:(k + 1) * P],
                                    self.hT[k][:, half * 512:(half + 1) * 512],
                                    start=(k == 0), stop=(k == 15)))
                gu_mm[(e_, m)] = inc(TEN, self.tM, lambda e, ps_g=ps_g, ub=ub:
                    e.matmul(ps_g[:, 1536:2048], ub[:, 15 * P:16 * P],
                             self.hT[15][:, 512:1024], start=False, stop=True))
                emit_gu_dma(e_ * 8 + m + 2)
                wait(SCA, self.tM, gu_mm[(e_, m)])
                t1 = self.t1b[m % 2]
                s_si = inc(SCA, self.sM, lambda e, ps_g=ps_g, t1=t1:
                           e.activation(t1[:, :], ps_g[:, 0:1024], AF.Silu))
                wait(VEC, self.sM, s_si)
                if shared:
                    guev[(e_, m)] = self.chn(VEC, self.vM, lambda e, ps_g=ps_g, t1=t1, m=m:
                                             e.tensor_mul(self.a_fm[m][:, :], t1[:, :],
                                                          ps_g[:, 1024:2048]))
                else:
                    if m == 0:
                        wait(VEC, self.vM, cb_ev[e_])
                    self.chn(VEC, self.vM, lambda e, ps_g=ps_g, t1=t1:
                             e.tensor_mul(self.t2b[:, :], t1[:, :], ps_g[:, 1024:2048]))
                    guev[(e_, m)] = self.chn(VEC, self.vM, lambda e, m=m, cb=cb:
                                             e.tensor_mul(self.a_fm[m][:, :], self.t2b[:, :], cb[:, :]))
            self.waitt(TEN, dn_dma[e_])
            wait(TEN, self.vM, guev[(e_, 7)])
            for tc in range(8):
                ps_d = self.PS0 if tc % 2 == 0 else self.PS1
                if tc >= 2:
                    wait(TEN, self.vM, dnev[(e_, tc - 2)])
                else:
                    wait(TEN, self.vM, guev[(e_, 6 + tc)])
                for nsl in range(4):
                    for m in range(8):
                        if nsl == 3 and m == 7:
                            continue
                        em(TEN, lambda e, m=m, nsl=nsl, tc=tc, ps_d=ps_d:
                           e.matmul(ps_d[:, nsl * 512:(nsl + 1) * 512],
                                    self.a_fm[m][:, tc * P:(tc + 1) * P],
                                    self.dbuf[:, m * H + nsl * 512: m * H + (nsl + 1) * 512],
                                    start=(m == 0), stop=(m == 7)))
                t_dn = inc(TEN, self.tM, lambda e, tc=tc, ps_d=ps_d:
                           e.matmul(ps_d[:, 3 * 512:4 * 512],
                                    self.a_fm[7][:, tc * P:(tc + 1) * P],
                                    self.dbuf[:, 7 * H + 3 * 512: 7 * H + 4 * 512],
                                    start=False, stop=True))
                dn_last = t_dn
                wait(VEC, self.tM, t_dn)
                a0 = tc * H
                dnev[(e_, tc)] = self.chn(VEC, self.vM, lambda e, a0=a0, ps_d=ps_d: e.tensor_add(
                    self.acc[:, a0:a0 + H], ps_d[:, :], self.acc[:, a0:a0 + H]))
            if e_ < 8:
                self.wait(GPS, self.tM, dn_last)
                nxt = d["swd_w"] if e_ == 7 else d["down_w"][e_ + 1]
                dn_dma[e_ + 1] = dma(GPS, self.dW,
                                     self.dbuf[:, :].rearrange("p (m h) -> p m h", h=H),
                                     nxt.rearrange("(m p) h -> p m h", p=P))
        self.acc_done = dnev[(8, 7)]

    def pd(self):
        wait, dma = self.wait, self.dma
        wait(SYN, self.vM, self.acc_done)
        tgt = dma(SYN, self.dX, self.d["out"].rearrange("(tc p) h -> p tc h", p=P),
                  self.acc[:, :].rearrange("p (tc h) -> p tc h", h=H))
        self.waitt(SYN, tgt)

    def build(self, stop_after=None):
        self.alloc_all()
        self._materialize()
        self.h_tok2 = self.h_tok2b
        phases = [
            ("p0", self.p0),
            ("p1a", self.p1a),
            ("p1bk", lambda: self.build_side(S, self.kv_latT, self.W4, self.bkup,
                                             self.ck, self.sk, self.K_h, self.tok_w4)),
            ("p1bq", lambda: self.build_side(NTOK, self.q_latT, self.W3, self.bqup,
                                             self.cq, self.sq, self.q_h, self.tok_w3)),
            ("p1c", self.p1c),
            ("bound", self.bound_prep),
            ("pa", self.pa),
            ("pb", self.pb),
            ("pc", self.pc),
        ]
        stopped = False
        for name, fn in phases:
            fn()
            if name in ("p0", "p1a", "p1bk", "p1bq", "bound", "pa", "pb"):
                self.barrier()
            if _TAPS:
                if name == "p1a":
                    self.tap("kvlat0", self.kv_latT[0][:, :])
                    self.tap("qlat0", self.q_latT[0][:, :])
                elif name == "p1bk":
                    self.tap("K0", self.K_h[0][:, :])
                    self.tap("K8", self.K_h[8][:, :])
                elif name == "p1bq":
                    self.tap("q0", self.q_h[0][:, :])
                    self.tap("q12", self.q_h[12][:, :])
                elif name == "p1c":
                    pass
                elif name == "bound":
                    self.tap("V0", self.V_h[0][:, :])
                    self.tap("ebias", self.ebias[:, :])
                elif name == "pa":
                    self.tap("ao0", self.ao[0][:, :])
                elif name == "pb":
                    self.tap("acc_pb", self.acc[:, :])
                    self.tap("hT0", self.hT[0][:, :])
                    self.tap("cT", self.cT_sb[:, :])
            if name == stop_after:
                stopped = True
                break
        if stopped:
            self.barrier()
            tgt = self.dma(SYN, self.dX,
                           self.d["out"].rearrange("(tc p) h -> p tc h", p=P),
                           self.acc[:, :].rearrange("p (tc h) -> p tc h", h=H))
            self.wait(SYN, self.dX, tgt)
        else:
            self.pd()

        nc = self.nc
        with nc.Block() as block:
            @block.tensor
            def _(e):
                for f in self.streams[TEN]:
                    f(e)

            @block.vector
            def _(e):
                for f in self.streams[VEC]:
                    f(e)

            @block.scalar
            def _(e):
                for f in self.streams[SCA]:
                    f(e)

            @block.sync
            def _(e):
                for f in self.streams[SYN]:
                    f(e)

            @block.gpsimd
            def _(e):
                for f in self.streams[GPS]:
                    f(e)
        return nc


def _host_inputs(args):
    x = np.ascontiguousarray(args["L0_attn_hidden_states"], dtype=np.float32)
    com = {
        "wqdT": np.ascontiguousarray(args["L0_attn_Wq_down"].T),
        "wkvdT": np.ascontiguousarray(args["L0_attn_Wkv_down"].T),
        "wqupT": np.ascontiguousarray(args["L0_attn_Wq_up"].T),
        "wkupT": np.ascontiguousarray(args["L0_attn_Wk_up"].T),
        "wvupT": np.ascontiguousarray(args["L0_attn_Wv_up"].T),
        "woT": np.ascontiguousarray(args["L0_attn_Wo"].T),
        "bqd": np.ascontiguousarray(args["L0_attn_bq_down"].reshape(4, P).T),
        "bkvd": np.ascontiguousarray(args["L0_attn_bkv_down"].reshape(4, P).T),
        "bqup": np.ascontiguousarray(args["L0_attn_bq_up"].reshape(8, P).T),
        "bkup": np.ascontiguousarray(args["L0_attn_bk_up"].reshape(8, P).T),
        "bv": np.ascontiguousarray(args["L0_attn_bv_up"].reshape(1, H)),
        "bo": np.ascontiguousarray(args["L0_attn_bo"].reshape(1, H)),
        "rmsw": np.ascontiguousarray(args["L0_ffn_rms_weight"].reshape(16, P).T),
        "router": np.ascontiguousarray(args["L0_ffn_router_w"]),
        "gate_r": np.ascontiguousarray(
            args["L0_ffn_w_gate"].reshape(NE, 16, P, 8, P).transpose(0, 3, 2, 1, 4)),
        "up_r": np.ascontiguousarray(
            args["L0_ffn_w_up"].reshape(NE, 16, P, 8, P).transpose(0, 3, 2, 1, 4)),
        "down_w": np.ascontiguousarray(args["L0_ffn_w_down"]),
        "swg_r": np.ascontiguousarray(
            args["L0_ffn_sw_gate"].reshape(16, P, 8, P).transpose(2, 1, 0, 3)),
        "swu_r": np.ascontiguousarray(
            args["L0_ffn_sw_up"].reshape(16, P, 8, P).transpose(2, 1, 0, 3)),
        "swd_w": np.ascontiguousarray(args["L0_ffn_sw_down"]),
    }
    esel = np.zeros((8, NE * P), dtype=np.float32)
    for e in range(NE):
        esel[e, e * P:(e + 1) * P] = 1.0
    com["esel"] = esel
    for k, v in com.items():
        com[k] = np.ascontiguousarray(v, dtype=np.float32)
    in_maps = []
    for c in range(8):
        b, qh = c // 2, c % 2
        m = dict(com)
        m["x_full"] = np.ascontiguousarray(x[b])
        m["x_own"] = np.ascontiguousarray(x[b, qh * NTOK:(qh + 1) * NTOK])
        m["pos0"] = np.full((P, 1), float(qh * NTOK), dtype=np.float32)
        in_maps.append(m)
    return in_maps


_NC_CACHE = [None]


def kernel(**inputs):
    in_maps = _host_inputs(inputs)
    if _NC_CACHE[0] is None:
        _NC_CACHE[0] = KB().build()
    nc = _NC_CACHE[0]
    res = run_bass_kernel_spmd(nc, in_maps, core_ids=list(range(8)))
    if _TAPS:
        global _LAST_TAPS
        _LAST_TAPS = [{k[4:]: np.asarray(v) for k, v in r.items() if k.startswith("tap_")}
                      for r in res.results]
    out = np.empty((4, S, H), dtype=np.float32)
    for c in range(8):
        b, qh = c // 2, c % 2
        out[b, qh * NTOK:(qh + 1) * NTOK] = np.asarray(res.results[c]["out"])
    return out


# version 11
# speedup vs baseline: 1.2093x; 1.0213x over previous
import math
from contextlib import ExitStack

import numpy as np

import concourse.bass as bass
import concourse.mybir as mybir
from concourse.bass_utils import run_bass_kernel_spmd

F32 = mybir.dt.float32
BF16 = mybir.dt.bfloat16
AF = mybir.ActivationFunctionType
ALU = mybir.AluOpType
AX = mybir.AxisListType

P = 128
H = 2048
NH = 16
LAT = 512
QHW = 1024
S = 2048
NTOK = 1024
NE = 8
FF = 1024
MAGIC = 1.5 * 2 ** 23
INV2PI = float(np.float32(1.0 / (2 * math.pi)))
TWOPI = float(np.float32(2 * math.pi))
SCL = 1.0 / math.sqrt(128.0)

SBUF_BASE = 17024
SBUF_TOP = 229344

P0, P1A, P1BK, P1BQ, P1C, PAA, PAB, PB, PC, PD = range(10)

TEN, VEC, SCA, SYN, GPS = "t", "v", "s", "y", "g"
_TAPS = False
_LAST_TAPS = None


class Alloc:

    def __init__(self, base, top):
        self.base, self.top = base, top
        self.reqs = []
        self.items = []
        self.offsets = {}

    def reserve(self, name, nbytes, birth, death):
        nbytes = (nbytes + 31) // 32 * 32
        self.reqs.append((nbytes, birth, death, name))

    def pin(self, name, rel_off):
        self.pins = getattr(self, "pins", {})
        self.pins[name] = self.base + rel_off

    def _try(self, order):
        pins = getattr(self, "pins", {})
        items, offsets = [], {}
        for nbytes, birth, death, name in order:
            if name in pins:
                start, end = pins[name], pins[name] + nbytes
                assert end <= self.top, f"pin {name} overflows"
                for (s, e, b, d, n) in items:
                    assert not (start < e and s < end and birth <= d and b <= death), \
                        f"pin {name} collides with {n}"
                items.append((start, end, birth, death, name))
                offsets[name] = start
                continue
            cands = sorted({self.base} | {e for (s, e, b, d, n) in items})
            placed = False
            for start in cands:
                end = start + nbytes
                if end > self.top:
                    continue
                if all(not (start < e and s < end and birth <= d and b <= death)
                       for (s, e, b, d, n) in items):
                    items.append((start, end, birth, death, name))
                    offsets[name] = start
                    placed = True
                    break
            if not placed:
                return None
        self.items, self.offsets = items, offsets
        return offsets

    def solve(self):
        pins = getattr(self, "pins", {})
        keys = [
            lambda r: (r[3] not in pins, -(r[2] - r[1]), -r[0]),
            lambda r: (r[3] not in pins, -r[0], r[1]),
            lambda r: (r[3] not in pins, r[1], -(r[2] - r[1]), -r[0]),
            lambda r: (r[3] not in pins, -r[0] * (r[2] - r[1] + 1)),
        ]
        for k in keys:
            if self._try(sorted(self.reqs, key=k)) is not None:
                return
        raise RuntimeError("SBUF overflow: no ordering found a packing")


class Sem:
    def __init__(self, h):
        self.h = h
        self.n = 0


class KB:

    def __init__(self):
        nc = bass.Bass("TRN2", target_bir_lowering=False)
        self.nc = nc
        self.st = ExitStack()
        self.streams = {k: [] for k in (TEN, VEC, SCA, SYN, GPS)}
        self.al = Alloc(SBUF_BASE, SBUF_TOP)
        self._pending = []

        d = {}
        def dram(name, shape, dt=F32, kind="ExternalInput"):
            d[name] = nc.dram_tensor(name, list(shape), dt, kind=kind)
        dram("x_full", [S, H]); dram("x_own", [NTOK, H]); dram("pos0", [P, 1])
        dram("wqdT", [H, LAT]); dram("wkvdT", [H, LAT])
        dram("wqupT", [LAT, QHW]); dram("wkupT", [LAT, QHW]); dram("wvupT", [LAT, H])
        dram("woT", [H, H])
        dram("bqd", [P, 4]); dram("bkvd", [P, 4]); dram("bqup", [P, 8]); dram("bkup", [P, 8])
        dram("bv", [1, H]); dram("bo", [1, H])
        dram("rmsw", [P, 16]); dram("router", [H, NE])
        dram("gate_r", [NE, 8, P, 16, P]); dram("up_r", [NE, 8, P, 16, P])
        dram("down_w", [NE, FF, H])
        dram("swg_r", [8, P, 16, P]); dram("swu_r", [8, P, 16, P]); dram("swd_w", [FF, H])
        dram("esel", [8, NE * P])
        dram("out", [NTOK, H], F32, kind="ExternalOutput")
        self.tap_tensors = {}
        d["aosp"] = nc.dram_tensor("aosp", [8, P, H], BF16, kind="Internal")
        self.d = d

        self.PS0 = self.st.enter_context(nc.psum_tensor("PS0", [P, 2048], F32))
        self.PS1 = self.st.enter_context(nc.psum_tensor("PS1", [P, 2048], F32))
        self.PS0b = self.PS0.bitcast(BF16)
        self.PS1b = self.PS1.bitcast(BF16)

        for s in ("tM", "vM", "sM", "gM", "bar"):
            setattr(self, s, Sem(self.st.enter_context(nc.semaphore(s))))
        self.dpool = {
            GPS: [Sem(self.st.enter_context(nc.semaphore(f"dW{i}"))) for i in range(12)],
            SYN: [Sem(self.st.enter_context(nc.semaphore(f"dX{i}"))) for i in range(6)],
        }
        self.dcnt = {GPS: 0, SYN: 0}
        self.dW = None
        self.dX = None

    def em(self, eng, fn):
        self.streams[eng].append(fn)

    def wait(self, eng, sem, val):
        if val and val > 0:
            self.em(eng, lambda e, s=sem, v=val: e.wait_ge(s.h, v))

    def inc(self, eng, sem, fn):
        sem.n += 1
        tgt = sem.n
        self.em(eng, lambda e, f=fn, s=sem: f(e).then_inc(s.h, 1))
        return tgt

    def chn(self, eng, sem, fn):
        t = self.inc(eng, sem, fn)
        self.wait(eng, sem, t)
        return t

    def dma(self, eng, sem_ignored, out, in_):
        pool = self.dpool[eng]
        slot = pool[self.dcnt[eng] % len(pool)]
        self.dcnt[eng] += 1
        if slot.n > 0:
            self.wait(eng, slot, slot.n)
        slot.n += 16
        tgt = slot.n
        self.em(eng, lambda e, o=out, i=in_, s=slot: e.dma_start(out=o, in_=i).then_inc(s.h, 16))
        return (slot, tgt)

    def waitt(self, eng, token):
        if token:
            self.wait(eng, token[0], token[1])

    def barrier(self):
        tgt = self.bar.n + 5
        for eng in (TEN, VEC, SCA, SYN, GPS):
            self.em(eng, lambda e: e.drain())
            self.em(eng, lambda e: e.nop().then_inc(self.bar.h, 1))
            self.em(eng, lambda e, v=tgt: e.wait_ge(self.bar.h, v))
        self.bar.n = tgt

    def tap(self, name, ap):
        if not _TAPS:
            return
        t = self.nc.dram_tensor("tap_" + name, list(ap.shape), ap.dtype, kind="ExternalOutput")
        self.tap_tensors[name] = t
        tgt = self.dma(SYN, self.dX, t[tuple(slice(None) for _ in ap.shape)], ap)
        self.waitt(SYN, tgt)

    def sb(self, name, shape, dt, birth, death):
        per_part = int(np.prod(shape[1:])) * mybir.dt.size(dt)
        self.al.reserve(name, per_part, birth, death)
        self._pending.append((name, list(shape), dt))
        return name

    def _materialize(self):
        self.al.solve()
        made = {}
        for name, shape, dt in self._pending:
            made[name] = self.nc.alloc_sbuf_tensor_at(name, shape, dt, offset=self.al.offsets[name])
        def fix(v):
            if isinstance(v, str) and v in made:
                return made[v]
            if isinstance(v, list):
                return [fix(x) for x in v]
            return v
        for k in list(self.__dict__):
            self.__dict__[k] = fix(self.__dict__[k])

    def alloc_all(self):
        sb = self.sb
        self.ident_bf = sb("ident_bf", [P, P], BF16, P0, PD)
        self.ident_f32 = sb("ident_f32", [P, P], F32, P0, PD)
        self.ones_bf = sb("ones_bf", [1, P], BF16, P0, PD)
        self.ones_f32 = sb("ones_f32", [1, P], F32, P0, PD)
        self.onec_bf = sb("onec_bf", [P, 1], BF16, P0, PD)
        self.bv_row = sb("bv_row", [1, H], BF16, P0, P1C)
        self.bo_row = sb("bo_row", [1, H], BF16, PAB, PB)
        self.bqd = sb("bqd", [P, 4], F32, P0, PD)
        self.bkvd = sb("bkvd", [P, 4], F32, P0, PD)
        self.bqup = sb("bqup", [P, 8], F32, P0, PD)
        self.bkup = sb("bkup", [P, 8], F32, P0, PD)
        self.router_sb = sb("router_sb", [P, P], BF16, P0, PD)
        self.rmsw = sb("rmsw", [P, 16], F32, P0, PD)
        self.pos0_sb = sb("pos0_sb", [P, 1], F32, P0, PD)
        self.ebias = sb("ebias", [P, 1], F32, P0, PD)
        self.stats6 = sb("stats6", [P, 24], F32, P0, PD)
        self.mv2 = sb("mv2", [P, 2], F32, P0, PD)
        for nm in ("sc_a", "sc_b", "sc_c", "sc_d", "sc_e", "sc_f"):
            setattr(self, nm, sb(nm, [P, 1], F32, P0, PD))
        self.c_pi2 = sb("c_pi2", [P, 1], F32, P0, PD)
        self.c_eps5 = sb("c_eps5", [P, 1], F32, P0, PD)
        self.c_eps6 = sb("c_eps6", [P, 1], F32, P0, PD)
        self.c_lne = sb("c_lne", [P, 1], F32, P0, PD)
        self.c_b0 = sb("c_b0", [P, 1], F32, P0, PD)
        self.m8 = sb("m8", [P, 8], F32, P0, PD)
        self.probs_u = sb("probs_u", [P, 8], F32, P0, PD)
        self.maskb = sb("maskb", [P, 8], F32, P0, PD)
        self.cw = sb("cw", [P, 8], F32, P0, PD)
        self.s4 = [sb(f"s4_{i}", [P, 4], F32, P0, PD) for i in range(2)]
        self.r4 = [sb(f"r4_{i}", [P, 4], F32, P0, PD) for i in range(2)]
        self.nrow = sb("nrow", [1, 8], F32, P0, PD)
        self.brow = sb("brow", [1, 2], F32, P0, PD)

        self.W1 = sb("W1", [P, 16 * LAT], BF16, P0, P1A)
        self.W2 = sb("W2", [P, 16 * LAT], BF16, P0, P1A)
        self.W3 = sb("W3", [P, 4 * QHW], BF16, P0, P1BQ)
        self.W4 = sb("W4", [P, 4 * QHW], BF16, P0, P1BK)
        self.W5 = sb("W5", [P, 4 * H], BF16, P0, P1C)
        self.ck = [sb(f"ck{c}", [P, S], F32, P0, P1BK) for c in range(2)]
        self.sk = [sb(f"sk{c}", [P, S], F32, P0, P1BK) for c in range(2)]
        self.cq = [sb(f"cq{c}", [P, NTOK], F32, P0, P1BQ) for c in range(2)]
        self.sq = [sb(f"sq{c}", [P, NTOK], F32, P0, P1BQ) for c in range(2)]
        self.rtmp = sb("rtmp", [P, S], F32, P0, P0)
        self.rsc_scr = sb("rsc_scr", [P, S], F32, P0, P0)

        self.x_in = [sb(f"x_in{i}", [P, H], F32, P1A, P1A) for i in range(3)]
        self.x_ln = [sb(f"x_ln{i}", [P, H], BF16, P1A, P1A) for i in range(4)]
        self.x_lnT = [sb(f"x_lnT{k}", [P, 512], BF16, P1A, P1A) for k in range(16)]
        self.q_latT = [sb(f"q_latT{m}", [P, NTOK], BF16, P1A, P1BQ) for m in range(4)]
        self.kv_latT = [sb(f"kv_latT{m}", [P, S], BF16, P1A, P1C) for m in range(4)]
        self.khA = sb("khA", [P, S], F32, P1BK, P1BQ)
        self.khB = sb("khB", [P, S], F32, P1BK, P1BQ)
        self.rsc_a = sb("rsc_a", [P, 1024], F32, P1BK, P1BQ)
        self.rsc_b = sb("rsc_b", [P, 1024], F32, P1BK, P1BQ)

        self.K_h = [sb(f"K{h}", [P, S], BF16, P1BK, PAA if h < 8 else PAB) for h in range(16)]
        self.q_h = [sb(f"q{h}", [P, NTOK], BF16, P1BQ, PAA if h < 8 else PAB) for h in range(16)]
        self.V_h = [sb(f"V{h}", [P, 16 * 129], BF16, P1C, PAA if h < 8 else PAB) for h in range(16)]

        self.A_T = sb("A_T", [P, 2048], BF16, PAA, PAB)
        self.sqs = sb("sqs", [P, 1024], BF16, P1C, P1C)
        self.ao = [sb(f"ao{i}", [P, H], BF16, PAA, PAB) for i in range(8)]

        self.WoT = sb("WoT", [P, 16 * H], BF16, PAB, PB)
        self.acc = sb("acc", [P, 8 * H], F32, PB, PD)
        self.hT = [sb(f"hT{k}", [P, NTOK], BF16, PB, PC) for k in range(16)]
        self.aoT = [sb(f"aoT{i}", [P, H], BF16, PB, PB) for i in range(2)]
        self.ao_r = [sb(f"ao_r{i}", [P, H], BF16, PB, PB) for i in range(2)]
        self.x_ot = [sb(f"x_ot{i}", [P, H], F32, PB, PB) for i in range(1)]
        self.h_tok2b = [sb(f"h_tok{i}", [P, H], BF16, PB, PB) for i in range(2)]
        self.cT_sb = sb("cT_sb", [8, NTOK], F32, PB, PC)
        self.sel_sb = sb("sel_sb", [8, NE * P], F32, PB, PC)
        self.sc_a2 = sb("sc_a2", [P, 1], F32, P0, PD)
        self.sc_b2 = sb("sc_b2", [P, 1], F32, P0, PD)

        self.gbuf = [sb(f"gbuf{i}", [P, 2048], BF16, PC, PC) for i in range(2)]
        self.ubuf = [sb(f"ubuf{i}", [P, 2048], BF16, PC, PC) for i in range(2)]
        self.dbuf = sb("dbuf", [P, 8 * H], BF16, PC, PC)
        self.c_bc = [sb(f"c_bc{i}", [P, NTOK], F32, PC, PC) for i in range(2)]
        self.t1b = [sb(f"t1b{i}", [P, FF], F32, PC, PC) for i in range(2)]
        self.t2b = sb("t2b", [P, FF], F32, PC, PC)
        self.a_fm = [sb(f"a_fm{m}", [P, NTOK], BF16, PC, PC) for m in range(8)]

        off = 0
        for h in range(8):
            self.al.pin(f"K{h}", off); off += 4096
        for h in range(8):
            self.al.pin(f"q{h}", off); off += 2048
        for h in range(8):
            self.al.pin(f"V{h}", off); off += 4128
        self.al.pin("WoT", 0)
        g2 = off
        for h in range(8, 16):
            self.al.pin(f"K{h}", off); off += 4096
        for h in range(8, 16):
            self.al.pin(f"q{h}", off); off += 2048
        for h in range(8, 16):
            self.al.pin(f"V{h}", off); off += 4128
        self.al.pin("acc", g2)
        for k in range(16):
            self.al.pin(f"hT{k}", g2 + 65536 + k * 2048)

    def p0(self):
        em, wait, inc, dma = self.em, self.wait, self.inc, self.dma
        d = self.d
        g0 = inc(GPS, self.gM, lambda e: e.memset(self.ident_bf[:, :], 0.0))
        wait(GPS, self.gM, g0)
        em(GPS, lambda e: e.affine_select(out=self.ident_bf[:, :], in_=self.ident_bf[:, :],
                                          pattern=[[-1, P]], compare_op=ALU.not_equal,
                                          fill=1.0, base=0, channel_multiplier=1))
        g1 = inc(GPS, self.gM, lambda e: e.memset(self.ident_f32[:, :], 0.0))
        wait(GPS, self.gM, g1)
        em(GPS, lambda e: e.affine_select(out=self.ident_f32[:, :], in_=self.ident_f32[:, :],
                                          pattern=[[-1, P]], compare_op=ALU.not_equal,
                                          fill=1.0, base=0, channel_multiplier=1))
        em(GPS, lambda e: e.memset(self.ones_bf[:, :], 1.0))
        em(GPS, lambda e: e.memset(self.ones_f32[:, :], 1.0))
        em(GPS, lambda e: e.memset(self.onec_bf[:, :], 1.0))
        em(GPS, lambda e: e.memset(self.c_pi2[:, :], math.pi / 2))
        em(GPS, lambda e: e.memset(self.c_eps5[:, :], 1e-5))
        em(GPS, lambda e: e.memset(self.c_eps6[:, :], 1e-6))
        em(GPS, lambda e: e.memset(self.c_lne[:, :], 1e-5 * 1e-6 / (1.0 + 1e-6)))
        em(GPS, lambda e: e.memset(self.c_b0[:, :], -0.5 * math.log(1.0 + 1e-6)))
        em(GPS, lambda e: e.iota(self.rtmp[:, :], [[1, S]], base=0, channel_multiplier=0,
                                 allow_small_or_imprecise_dtypes=True))
        em(GPS, lambda e: e.iota(self.sc_c[:, :], [[1, 1]], base=0, channel_multiplier=1,
                                 allow_small_or_imprecise_dtypes=True))
        self.g_const = inc(GPS, self.gM, lambda e: e.iota(
            self.sc_d[:, :], [[1, 1]], base=P, channel_multiplier=1,
            allow_small_or_imprecise_dtypes=True))

        self.tok_bqd = dma(GPS, self.dW, self.bqd[:, :], d["bqd"][:, :])
        self.tok_bkvd = dma(GPS, self.dW, self.bkvd[:, :], d["bkvd"][:, :])
        self.tok_bqup = dma(GPS, self.dW, self.bqup[:, :], d["bqup"][:, :])
        self.tok_bkup = dma(GPS, self.dW, self.bkup[:, :], d["bkup"][:, :])
        self.tok_bv = dma(GPS, self.dW, self.bv_row[0:1, :], d["bv"][:, :])
        self.tok_rmsw = dma(GPS, self.dW, self.rmsw[:, :], d["rmsw"][:, :])
        self.tok_pos0 = dma(GPS, self.dW, self.pos0_sb[:, :], d["pos0"][:, :])
        self.tok_router = dma(GPS, self.dW,
                              self.router_sb[:, :].rearrange("p (k e) -> p k e", e=NE),
                              d["router"].rearrange("(k p) e -> p k e", p=P))
        self.tok_w1 = dma(GPS, self.dW, self.W1[:, :].rearrange("p (k m) -> p k m", m=LAT),
                          d["wqdT"].rearrange("(k p) m -> p k m", p=P))
        self.tok_w2 = dma(GPS, self.dW, self.W2[:, :].rearrange("p (k m) -> p k m", m=LAT),
                          d["wkvdT"].rearrange("(k p) m -> p k m", p=P))
        self.tok_w3 = dma(GPS, self.dW, self.W3[:, :].rearrange("p (k m) -> p k m", m=QHW),
                          d["wqupT"].rearrange("(k p) m -> p k m", p=P))
        self.tok_w4 = dma(GPS, self.dW, self.W4[:, :].rearrange("p (k m) -> p k m", m=QHW),
                          d["wkupT"].rearrange("(k p) m -> p k m", p=P))
        self.tok_w5 = dma(GPS, self.dW, self.W5[:, :].rearrange("p (k m) -> p k m", m=H),
                          d["wvupT"].rearrange("(k p) m -> p k m", p=P))

        wait(SCA, self.gM, self.g_const)
        LNB = math.log(10000.0) / 256.0
        self.chn(SCA, self.sM, lambda e: e.activation(self.sc_c[:, :], self.sc_c[:, :], AF.Exp, scale=-LNB))
        s_ifr = self.chn(SCA, self.sM, lambda e: e.activation(self.sc_d[:, :], self.sc_d[:, :], AF.Exp, scale=-LNB))

        wait(VEC, self.gM, self.g_const)
        self.waitt(VEC, self.tok_pos0)
        wait(VEC, self.sM, s_ifr)

        def tables(cos_t, sin_t, n, use_pos0):
            chn = self.chn
            for c in range(2):
                ifr = self.sc_c if c == 0 else self.sc_d
                th = self.rsc_scr
                if use_pos0:
                    chn(VEC, self.vM, lambda e, n=n: e.tensor_scalar(th[:, :n], self.rtmp[:, :n],
                                                                     self.pos0_sb[:, 0:1], None, ALU.add))
                    chn(VEC, self.vM, lambda e, n=n, ifr=ifr: e.tensor_scalar(th[:, :n], th[:, :n],
                                                                              ifr[:, 0:1], None, ALU.mult))
                else:
                    chn(VEC, self.vM, lambda e, n=n, ifr=ifr: e.tensor_scalar(th[:, :n], self.rtmp[:, :n],
                                                                              ifr[:, 0:1], None, ALU.mult))
                chn(VEC, self.vM, lambda e, n=n, c=c: e.tensor_scalar(sin_t[c][:, :n], th[:, :n], INV2PI, None, ALU.mult))
                chn(VEC, self.vM, lambda e, n=n, c=c: e.tensor_scalar(sin_t[c][:, :n], sin_t[c][:, :n],
                                                                      MAGIC, -MAGIC, ALU.add, ALU.add))
                chn(VEC, self.vM, lambda e, n=n, c=c:
                    e.scalar_tensor_tensor(sin_t[c][:, :n], sin_t[c][:, :n], -TWOPI,
                                           th[:, :n], ALU.mult, ALU.add))
                v_s = chn(VEC, self.vM, lambda e, n=n, c=c: e.tensor_scalar(
                    sin_t[c][:, :n], sin_t[c][:, :n], 3.141591, -3.141591, ALU.min, ALU.max))
                chn(VEC, self.vM, lambda e, n=n, c=c: e.tensor_scalar(cos_t[c][:, :n], th[:, :n],
                                                                      INV2PI, 0.25, ALU.mult, ALU.add))
                chn(VEC, self.vM, lambda e, n=n, c=c: e.tensor_scalar(cos_t[c][:, :n], cos_t[c][:, :n],
                                                                      MAGIC, -MAGIC, ALU.add, ALU.add))
                chn(VEC, self.vM, lambda e, n=n, c=c:
                    e.scalar_tensor_tensor(cos_t[c][:, :n], cos_t[c][:, :n], -TWOPI,
                                           th[:, :n], ALU.mult, ALU.add))
                v_c = chn(VEC, self.vM, lambda e, n=n, c=c: e.tensor_scalar(
                    cos_t[c][:, :n], cos_t[c][:, :n], 1.570795, -4.712387, ALU.min, ALU.max))
                wait(SCA, self.vM, v_s)
                chn(SCA, self.sM, lambda e, n=n, c=c: e.activation(sin_t[c][:, :n], sin_t[c][:, :n], AF.Sin))
                wait(SCA, self.vM, v_c)
                chn(SCA, self.sM, lambda e, n=n, c=c: e.activation(cos_t[c][:, :n], cos_t[c][:, :n],
                                                                   AF.Sin, bias=self.c_pi2[:, 0:1]))

        tables(self.ck, self.sk, S, False)
        tables(self.cq, self.sq, NTOK, True)
        self.s_tables = inc(SCA, self.sM, lambda e: e.nop())

    def p1a(self):
        em, wait, inc, dma = self.em, self.wait, self.inc, self.dma
        parts = globals().get("_P1A_PARTS", 3)
        d = self.d
        ln_s = {}
        xin_free = []
        self._p1a_prev_ev = {}
        trev = {}
        latev = {}
        ttr_last = {}
        tlat_last = {}

        def emit_ln_chunk(g, tc):
            src_d = d["x_full"] if g < 4 else d["x_own"]
            row0 = (g % 4) * 512 if g < 4 else (g - 4) * 512
            if True:
                buf = self.x_in[(g * 4 + tc) % 3]
                if len(xin_free) >= 3:
                    wait(SYN, self.sM, xin_free[-3])
                if g >= 1:
                    wait(SCA, self.tM, ttr_last[(g - 1, tc)])
                dtgt = dma(SYN, self.dX, buf[:, :], src_d[row0 + tc * P: row0 + (tc + 1) * P, :])
                self.waitt(VEC, dtgt)
                for gi in range(4):
                    self.chn(VEC, self.vM, lambda e, buf=buf, gi=gi: e.bn_stats(
                        self.stats6[:, gi * 6:(gi + 1) * 6],
                        buf[:, gi * 512:(gi + 1) * 512]))
                v_mv = self.chn(VEC, self.vM, lambda e: e.bn_aggr(
                    self.mv2[:, :], self.stats6[:, :].rearrange("p (g x) -> p g x", x=3)))
                wait(SCA, self.vM, v_mv)
                self.chn(SCA, self.sM, lambda e: e.activation(self.sc_a[:, :], self.mv2[:, 1:2], AF.Ln, bias=self.c_eps5[:, 0:1]))
                s_r = self.chn(SCA, self.sM, lambda e: e.activation(self.sc_a[:, :], self.sc_a[:, :], AF.Exp, scale=-0.5))
                wait(VEC, self.sM, s_r)
                v_b = self.chn(VEC, self.vM, lambda e: e.scalar_tensor_tensor(
                    self.sc_b[:, :], self.mv2[:, 0:1], -1.0, self.sc_a[:, :], ALU.mult, ALU.mult))
                wait(SCA, self.vM, v_b)
                s_x = self.chn(SCA, self.sM, lambda e, buf=buf, o=self.x_ln[tc]:
                               e.activation(o[:, :], buf[:, :], AF.Identity,
                                            bias=self.sc_b[:, 0:1], scale=self.sc_a[:, 0:1]))
                ln_s[(g, tc)] = s_x
                xin_free.append(s_x)

        def emit_trans_chunk(g, tc):
            if g > 0 and tc == 0:
                wait(TEN, self.sM, latev[(g - 1, 3)])
                wait(VEC, self.tM, tlat_last[g - 1])
            prev_ev = self._p1a_prev_ev
            if True:
                wait(TEN, self.sM, ln_s[(g, tc)])
                for wave in range(4):
                    pv = self.PS0b if wave % 2 == 0 else self.PS1b
                    if (tc * 4 + wave) % 2 in prev_ev:
                        wait(TEN, self.vM, prev_ev[(tc * 4 + wave) % 2])
                    for kk in range(3):
                        k = wave * 4 + kk
                        em(TEN, lambda e, k=k, tc=tc, pv=pv, bank=kk:
                           e.transpose(pv[:, bank * 1024: bank * 1024 + P],
                                       self.x_ln[tc][:, k * P:(k + 1) * P], self.ident_bf[:, :]))
                    k = wave * 4 + 3
                    t_tr = inc(TEN, self.tM, lambda e, k=k, tc=tc, pv=pv:
                               e.transpose(pv[:, 3 * 1024: 3 * 1024 + P],
                                           self.x_ln[tc][:, k * P:(k + 1) * P], self.ident_bf[:, :]))
                    wait(VEC, self.tM, t_tr)
                    last = None
                    for kk in range(4):
                        k = wave * 4 + kk
                        last = inc(VEC, self.vM, lambda e, k=k, tc=tc, pv=pv, kk=kk: e.tensor_copy(
                            self.x_lnT[k][:, tc * P:(tc + 1) * P], pv[:, kk * 1024: kk * 1024 + P]))
                    prev_ev[(tc * 4 + wave) % 2] = last
            trev[(g, 15)] = last
            ttr_last[(g, tc)] = t_tr

        def emit_latents(g):
            if g < 4:
                Wt, bias_t, outs, col0 = self.W2, self.bkvd, self.kv_latT, (g % 4) * 512
                self.waitt(TEN, self.tok_w2)
                self.waitt(SCA, self.tok_bkvd)
            else:
                Wt, bias_t, outs, col0 = self.W1, self.bqd, self.q_latT, (g - 4) * 512
                self.waitt(TEN, self.tok_w1)
                self.waitt(SCA, self.tok_bqd)
            wait(TEN, self.vM, trev[(g, 15)])
            for m in range(4):
                po = (m % 2) * 512
                if m >= 2:
                    wait(TEN, self.sM, latev[(g, m - 2)])
                for k in range(15):
                    em(TEN, lambda e, k=k, m=m, po=po, Wt=Wt:
                       e.matmul(self.PS0[:, po:po + 512],
                                Wt[:, k * LAT + m * P: k * LAT + (m + 1) * P],
                                self.x_lnT[k][:, :], start=(k == 0), stop=False))
                t_l = inc(TEN, self.tM, lambda e, m=m, po=po, Wt=Wt:
                          e.matmul(self.PS0[:, po:po + 512],
                                   Wt[:, 15 * LAT + m * P: 15 * LAT + (m + 1) * P],
                                   self.x_lnT[15][:, :], start=False, stop=True))
                wait(SCA, self.tM, t_l)
                latev[(g, m)] = self.chn(SCA, self.sM, lambda e, m=m, po=po, bias_t=bias_t, outs=outs, col0=col0:
                                         e.activation(outs[m][:, col0:col0 + 512], self.PS0[:, po:po + 512],
                                                      AF.Identity, bias=bias_t[:, m:m + 1]))
            tlat_last[g] = t_l

        for tc in range(4):
            emit_ln_chunk(0, tc)
            emit_trans_chunk(0, tc)
        for g in range(6):
            if g + 1 < 6:
                for tc in range(4):
                    emit_ln_chunk(g + 1, tc)
            emit_latents(g)
            if g + 1 < 6:
                for tc in range(4):
                    emit_trans_chunk(g + 1, tc)
        self.latev_last = latev[(5, 3)]

    def build_side(self, n, latT, Wup, bias_t, cos_t, sin_t, outK, w_req):
        em, wait, inc = self.em, self.wait, self.inc
        self.waitt(SCA, self.tok_bkup if bias_t is self.bkup else self.tok_bqup)
        nsl_cnt = n // 512
        prev_rope = None
        prev_casts = None
        for j in range(4):
            evA = evB = None
            for which in range(2):
                jj = j + 4 * which
                ps = self.PS0 if which == 0 else self.PS1
                dst = self.khA if which == 0 else self.khB
                self.waitt(TEN, w_req)
                if prev_rope is not None and which == 0:
                    wait(SCA, self.vM, prev_rope)
                for k in range(4):
                    for nsl in range(nsl_cnt):
                        if k == 3 and nsl == nsl_cnt - 1:
                            continue
                        em(TEN, lambda e, k=k, nsl=nsl, jj=jj, ps=ps, latT=latT, Wup=Wup:
                           e.matmul(ps[:, nsl * 512:(nsl + 1) * 512],
                                    Wup[:, k * QHW + jj * P: k * QHW + (jj + 1) * P],
                                    latT[k][:, nsl * 512:(nsl + 1) * 512],
                                    start=(k == 0), stop=(k == 3)))
                t_kh = inc(TEN, self.tM, lambda e, jj=jj, ps=ps, latT=latT, Wup=Wup, nsl=nsl_cnt - 1:
                           e.matmul(ps[:, nsl * 512:(nsl + 1) * 512],
                                    Wup[:, 3 * QHW + jj * P: 3 * QHW + (jj + 1) * P],
                                    latT[3][:, nsl * 512:(nsl + 1) * 512],
                                    start=False, stop=True))
                wait(SCA, self.tM, t_kh)
                ev = self.chn(SCA, self.sM, lambda e, dst=dst, jj=jj, ps=ps, n=n, bias_t=bias_t:
                              e.activation(dst[:, :n], ps[:, :n], AF.Identity, bias=bias_t[:, jj:jj + 1]))
                if which == 0:
                    evA = ev
                else:
                    evB = ev
            s_c1 = inc(SCA, self.sM, lambda e, j=j, n=n, outK=outK:
                       e.activation(outK[j][:, :n], self.khA[:, :n], AF.Identity))
            s_c2 = inc(SCA, self.sM, lambda e, j=j, n=n, outK=outK:
                       e.activation(outK[j + 4][:, :n], self.khB[:, :n], AF.Identity))
            c_t, s_t = cos_t[j % 2], sin_t[j % 2]
            wait(VEC, self.sM, evB)
            for h0 in range(0, n, 1024):
                h1 = h0 + 1024
                self.chn(VEC, self.vM, lambda e, h0=h0, h1=h1, c_t=c_t: e.tensor_mul(
                    self.rsc_a[:, :], self.khA[:, h0:h1], c_t[:, h0:h1]))
                self.chn(VEC, self.vM, lambda e, h0=h0, h1=h1, s_t=s_t: e.tensor_mul(
                    self.rsc_b[:, :], self.khB[:, h0:h1], s_t[:, h0:h1]))
                self.chn(VEC, self.vM, lambda e, j=j, h0=h0, h1=h1, outK=outK: e.tensor_sub(
                    outK[8 + j][:, h0:h1], self.rsc_a[:, :], self.rsc_b[:, :]))
                self.chn(VEC, self.vM, lambda e, h0=h0, h1=h1, s_t=s_t: e.tensor_mul(
                    self.rsc_a[:, :], self.khA[:, h0:h1], s_t[:, h0:h1]))
                self.chn(VEC, self.vM, lambda e, h0=h0, h1=h1, c_t=c_t: e.tensor_mul(
                    self.rsc_b[:, :], self.khB[:, h0:h1], c_t[:, h0:h1]))
                prev_rope = self.chn(VEC, self.vM, lambda e, j=j, h0=h0, h1=h1, outK=outK:
                                e.tensor_add(outK[12 + j][:, h0:h1], self.rsc_a[:, :], self.rsc_b[:, :]))
            prev_casts = s_c2
            wait(TEN, self.sM, prev_casts)

    def p1c(self):
        em, wait, inc, dma = self.em, self.wait, self.inc, self.dma
        for h in range(15):
            em(GPS, lambda e, h=h: e.memset(
                self.V_h[h][:, :].rearrange("p (kc x) -> p kc x", x=129)[:, :, 128:129], 1.0))
        g_v1 = inc(GPS, self.gM, lambda e: e.memset(
            self.V_h[15][:, :].rearrange("p (kc x) -> p kc x", x=129)[:, :, 128:129], 1.0))
        wait(TEN, self.gM, g_v1)
        self.waitt(TEN, self.tok_w5)
        self.waitt(TEN, self.tok_bv)
        vev = {}
        for tc in range(16):
            ps = self.PS0 if tc % 2 == 0 else self.PS1
            if tc >= 2:
                wait(TEN, self.sM, vev[tc - 2])
            for nsl in range(4):
                for k in range(4):
                    em(TEN, lambda e, k=k, nsl=nsl, tc=tc, ps=ps:
                       e.matmul(ps[:, nsl * 512:(nsl + 1) * 512],
                                self.kv_latT[k][:, tc * P:(tc + 1) * P],
                                self.W5[:, k * H + nsl * 512: k * H + (nsl + 1) * 512],
                                start=(k == 0), stop=False))
                if nsl < 3:
                    em(TEN, lambda e, nsl=nsl, ps=ps:
                       e.matmul(ps[:, nsl * 512:(nsl + 1) * 512], self.ones_bf[0:1, :],
                                self.bv_row[0:1, nsl * 512:(nsl + 1) * 512], start=False, stop=True))
            t_v = inc(TEN, self.tM, lambda e, ps=ps:
                      e.matmul(ps[:, 3 * 512:4 * 512], self.ones_bf[0:1, :],
                               self.bv_row[0:1, 3 * 512:4 * 512], start=False, stop=True))
            wait(SCA, self.tM, t_v)
            last = None
            for h in range(16):
                last = inc(SCA, self.sM, lambda e, h=h, tc=tc, ps=ps:
                           e.activation(self.V_h[h][:, tc * 129: tc * 129 + 128],
                                        ps[:, h * P:(h + 1) * P], AF.Identity))
            vev[tc] = last
        self.v_done = vev[15]

    def bound_prep(self):
        em, wait, inc = self.em, self.wait, self.inc
        wait(TEN, self.sM, self.v_done)
        wait(VEC, self.sM, self.v_done)
        t_sq = None
        for h in range(16):
            for half in range(2):
                if t_sq is not None:
                    wait(VEC, self.tM, t_sq)
                v_s = self.chn(VEC, self.vM, lambda e, h=h, half=half: e.tensor_mul(
                    self.sqs[:, :], self.K_h[h][:, half * 1024:(half + 1) * 1024],
                    self.K_h[h][:, half * 1024:(half + 1) * 1024]))
                wait(TEN, self.vM, v_s)
                em(TEN, lambda e, h=h, half=half:
                   e.matmul(self.PS1[0:1, (half * 2) * 512:(half * 2 + 1) * 512],
                            self.onec_bf[:, 0:1],
                            self.sqs[:, 0:512], start=(h == 0), stop=(h == 15)))
                t_sq = inc(TEN, self.tM, lambda e, h=h, half=half:
                           e.matmul(self.PS1[0:1, (half * 2 + 1) * 512:(half * 2 + 2) * 512],
                                    self.onec_bf[:, 0:1],
                                    self.sqs[:, 512:1024], start=(h == 0), stop=(h == 15)))
        t_k = t_sq
        wait(VEC, self.tM, t_k)
        for nsl in range(4):
            self.chn(VEC, self.vM, lambda e, nsl=nsl: e.tensor_reduce(
                self.nrow[0:1, nsl:nsl + 1], self.PS1[0:1, nsl * 512:(nsl + 1) * 512], AX.X, ALU.max))
        v_k = self.chn(VEC, self.vM, lambda e: e.tensor_reduce(self.brow[0:1, 1:2],
                                                               self.nrow[0:1, 0:4], AX.X, ALU.max))
        for h in range(16):
            wait(VEC, self.tM, t_sq)
            v_s = self.chn(VEC, self.vM, lambda e, h=h: e.tensor_mul(self.sqs[:, :NTOK],
                                                                      self.q_h[h][:, :], self.q_h[h][:, :]))
            wait(TEN, self.vM, v_s)
            em(TEN, lambda e, h=h:
               e.matmul(self.PS0[0:1, 0:512], self.onec_bf[:, 0:1],
                        self.sqs[:, 0:512], start=(h == 0), stop=(h == 15)))
            t_sq = inc(TEN, self.tM, lambda e, h=h:
                       e.matmul(self.PS0[0:1, 512:1024], self.onec_bf[:, 0:1],
                                self.sqs[:, 512:1024], start=(h == 0), stop=(h == 15)))
        t_q = t_sq
        wait(VEC, self.tM, t_q)
        for nsl in range(2):
            self.chn(VEC, self.vM, lambda e, nsl=nsl: e.tensor_reduce(
                self.nrow[0:1, 4 + nsl:5 + nsl], self.PS0[0:1, nsl * 512:(nsl + 1) * 512], AX.X, ALU.max))
        v_q = self.chn(VEC, self.vM, lambda e: e.tensor_reduce(self.brow[0:1, 0:1],
                                                               self.nrow[0:1, 4:6], AX.X, ALU.max))
        wait(SCA, self.vM, v_q)
        wait(SCA, self.vM, v_k)
        self.chn(SCA, self.sM, lambda e: e.activation(self.brow[0:1, 0:1], self.brow[0:1, 0:1], AF.Ln))
        s_l = self.chn(SCA, self.sM, lambda e: e.activation(self.brow[0:1, 1:2], self.brow[0:1, 1:2], AF.Ln))
        wait(VEC, self.sM, s_l)
        v_a = self.chn(VEC, self.vM, lambda e: e.tensor_add(self.brow[0:1, 0:1],
                                                            self.brow[0:1, 0:1], self.brow[0:1, 1:2]))
        wait(SCA, self.vM, v_a)
        s_e = self.chn(SCA, self.sM, lambda e: e.activation(self.brow[0:1, 0:1], self.brow[0:1, 0:1],
                                                            AF.Exp, scale=0.5))
        wait(VEC, self.sM, s_e)
        v_e = self.chn(VEC, self.vM, lambda e: e.tensor_scalar(self.brow[0:1, 0:1], self.brow[0:1, 0:1],
                                                               -1.01 * SCL, None, ALU.mult))
        wait(TEN, self.vM, v_e)
        t_e = inc(TEN, self.tM, lambda e: e.matmul(self.PS0[:, 0:1], self.ones_f32[0:1, :],
                                                   self.brow[0:1, 0:1], start=True, stop=True))
        wait(VEC, self.tM, t_e)
        self.v_ebias = inc(VEC, self.vM, lambda e: e.tensor_copy(self.ebias[:, :], self.PS0[:, 0:1]))

    def pa(self):
        em, wait, inc, dma = self.em, self.wait, self.inc, self.dma
        wait(SCA, self.vM, self.v_ebias)
        exp_abs, spair, pvpair, r4_abs, evac_abs = {}, {}, {}, {}, {}

        def pv_pair(R, jp, h_, qt):
            wait(TEN, self.sM, exp_abs[(R, jp)])
            if jp == 0 and R >= 1:
                wait(TEN, self.vM, r4_abs[R - 1])
                wait(TEN, self.vM, evac_abs[R - 1])
            for kc in (2 * jp, 2 * jp + 1):
                for qc in range(4):
                    if kc == 2 * jp + 1 and qc == 3:
                        continue
                    em(TEN, lambda e, kc=kc, qc=qc, jp=jp, h_=h_:
                       e.matmul(self.PS1[:, qc * 512: qc * 512 + 129],
                                self.A_T[:, (jp % 2) * 1024 + (kc % 2) * 512 + qc * P:
                                         (jp % 2) * 1024 + (kc % 2) * 512 + (qc + 1) * P],
                                self.V_h[h_][:, kc * 129:(kc + 1) * 129],
                                start=(kc == 0), stop=(kc == 15)))
            kc = 2 * jp + 1
            pvpair[(R, jp)] = inc(TEN, self.tM, lambda e, kc=kc, jp=jp, h_=h_:
                e.matmul(self.PS1[:, 3 * 512: 3 * 512 + 129],
                         self.A_T[:, (jp % 2) * 1024 + (kc % 2) * 512 + 3 * P:
                                  (jp % 2) * 1024 + (kc % 2) * 512 + 4 * P],
                         self.V_h[h_][:, kc * 129:(kc + 1) * 129],
                         start=False, stop=(kc == 15)))

        for h_ in range(16):
            for qt in range(2):
                R = h_ * 2 + qt
                for jp in range(8):
                    g = R * 8 + jp
                    if g >= 2:
                        pR, pjp = divmod(g - 2, 8)
                        wait(TEN, self.sM, exp_abs[(pR, pjp)])
                    kc0 = 2 * jp
                    em(TEN, lambda e, h_=h_, qt=qt, kc=kc0:
                       e.matmul(self.PS0[:, (kc % 4) * 512:((kc % 4) + 1) * 512],
                                self.K_h[h_][:, kc * P:(kc + 1) * P],
                                self.q_h[h_][:, qt * 512:(qt + 1) * 512],
                                start=True, stop=True))
                    spair[(R, jp)] = inc(TEN, self.tM, lambda e, h_=h_, qt=qt, kc=kc0 + 1:
                        e.matmul(self.PS0[:, (kc % 4) * 512:((kc % 4) + 1) * 512],
                                 self.K_h[h_][:, kc * P:(kc + 1) * P],
                                 self.q_h[h_][:, qt * 512:(qt + 1) * 512],
                                 start=True, stop=True))
                    wait(SCA, self.tM, spair[(R, jp)])
                    if g >= 2:
                        pR, pjp = divmod(g - 2, 8)
                        wait(SCA, self.tM, pvpair[(pR, pjp)])
                    exp_abs[(R, jp)] = inc(SCA, self.sM, lambda e, jp=jp:
                        e.activation(self.A_T[:, (jp % 2) * 1024:(jp % 2) * 1024 + 1024],
                                     self.PS0[:, (jp % 2) * 1024:(jp % 2) * 1024 + 1024],
                                     AF.Exp, bias=self.ebias[:, 0:1], scale=SCL))
                    if jp >= 1:
                        pv_pair(R, jp - 1, h_, qt)
                pv_pair(R, 7, h_, qt)
                wait(VEC, self.tM, pvpair[(R, 7)])
                sb_ = self.s4[R % 2]
                rb_ = self.r4[R % 2]
                for qc in range(3):
                    em(VEC, lambda e, sb_=sb_, qc=qc: e.tensor_copy(
                        sb_[:, qc:qc + 1], self.PS1[:, qc * 512 + 128: qc * 512 + 129]))
                self.chn(VEC, self.vM, lambda e, sb_=sb_: e.tensor_copy(
                    sb_[:, 3:4], self.PS1[:, 3 * 512 + 128: 3 * 512 + 129]))
                r4_abs[R] = self.chn(VEC, self.vM, lambda e, sb_=sb_, rb_=rb_: e.reciprocal(rb_[:, :], sb_[:, :]))
                last = None
                for qc in range(4):
                    qcg = qt * 4 + qc
                    last = inc(VEC, self.vM, lambda e, qc=qc, qcg=qcg, h_=h_, rb_=rb_:
                               e.tensor_scalar(self.ao[qcg][:, h_ * P:(h_ + 1) * P],
                                               self.PS1[:, qc * 512: qc * 512 + 128],
                                               rb_[:, qc:qc + 1], None, ALU.mult))
                evac_abs[R] = last
            if h_ == 7:
                self.wait(GPS, self.tM, pvpair[(15, 7)])
                self.d_wo = dma(GPS, self.dW,
                                self.WoT[:, :].rearrange("p (k m) -> p k m", m=H),
                                self.d["woT"].rearrange("(k p) m -> p k m", p=P))
                self.d_bo = dma(GPS, self.dW, self.bo_row[0:1, :], self.d["bo"][:, :])
        self.ao_spill = {}
        for qcg in range(8):
            wait(SYN, self.vM, evac_abs[30 if qcg < 4 else 31])
            self.ao_spill[qcg] = dma(SYN, self.dX, self.d["aosp"][qcg], self.ao[qcg][:, :])
        self.attn_done = evac_abs[31]

    def pb(self):
        em, wait, inc, dma = self.em, self.wait, self.inc, self.dma
        d = self.d
        self.waitt(TEN, self.d_wo)
        self.waitt(TEN, self.d_bo)
        self.waitt(TEN, self.tok_router)
        self.waitt(SCA, self.tok_rmsw)
        for qcg in range(8):
            self.waitt(SYN, self.ao_spill[qcg])
        self.d_sel = dma(GPS, self.dW, self.sel_sb[:, :], d["esel"][:, :])
        hT_done = {}
        xo_dma = {}
        ao_dma = {}
        aotr = {}
        opev = {}
        aoev = {}
        opmm = {}
        bstage1 = {}
        bstage2 = {}
        s_h_d = {}
        mvv = {}
        bstage = {}

        def stage_a1(tc):
            if tc >= 2:
                wait(SYN, self.tM, aotr[tc - 2])
            ao_dma[tc] = dma(SYN, self.dX, self.ao_r[tc % 2][:, :], d["aosp"][tc])
            self.waitt(TEN, ao_dma[tc])
            if tc >= 1:
                wait(TEN, self.vM, aoev[tc - 1])
            if tc >= 2:
                wait(TEN, self.vM, opev[tc - 2])
                wait(TEN, self.tM, opmm[tc - 2])
            if tc >= 3:
                wait(TEN, self.sM, bstage1[tc - 3][0])
                wait(TEN, self.sM, bstage2[tc - 3][0])
                wait(TEN, self.vM, bstage2[tc - 3][1])
            for wave in range(2):
                for kk in range(7):
                    k = wave * 8 + kk
                    pv = self.PS0b if kk < 4 else self.PS1b
                    bank = kk % 4
                    em(TEN, lambda e, k=k, tc=tc, pv=pv, bank=bank:
                       e.transpose(pv[:, bank * 1024: bank * 1024 + P],
                                   self.ao_r[tc % 2][:, k * P:(k + 1) * P], self.ident_bf[:, :]))
                k = wave * 8 + 7
                t_w = inc(TEN, self.tM, lambda e, k=k, tc=tc:
                          e.transpose(self.PS1b[:, 3 * 1024: 3 * 1024 + P],
                                      self.ao_r[tc % 2][:, k * P:(k + 1) * P], self.ident_bf[:, :]))
                wait(VEC, self.tM, t_w)
                for kk in range(7):
                    k = wave * 8 + kk
                    pv = self.PS0b if kk < 4 else self.PS1b
                    bank = kk % 4
                    em(VEC, lambda e, k=k, tc=tc, pv=pv, bank=bank: e.tensor_copy(
                        self.aoT[tc % 2][:, k * P:(k + 1) * P], pv[:, bank * 1024: bank * 1024 + P]))
                k = wave * 8 + 7
                v_w = inc(VEC, self.vM, lambda e, k=k, tc=tc: e.tensor_copy(
                    self.aoT[tc % 2][:, k * P:(k + 1) * P], self.PS1b[:, 3 * 1024: 3 * 1024 + P]))
                wait(TEN, self.vM, v_w)
                if wave == 1:
                    aotr[tc] = t_w
                    aoev[tc] = v_w

        def stage_a2(tc):
            wait(TEN, self.vM, aoev[tc])
            if tc + 1 in aoev:
                wait(TEN, self.vM, aoev[tc + 1])
            if tc >= 1:
                wait(TEN, self.vM, opev[tc - 1])
            for nsl in range(4):
                for k in range(16):
                    em(TEN, lambda e, k=k, nsl=nsl, tc=tc:
                       e.matmul(self.PS0[:, nsl * 512:(nsl + 1) * 512],
                                self.aoT[tc % 2][:, k * P:(k + 1) * P],
                                self.WoT[:, k * H + nsl * 512: k * H + (nsl + 1) * 512],
                                start=(k == 0), stop=False))
                if nsl < 3:
                    em(TEN, lambda e, nsl=nsl:
                       e.matmul(self.PS0[:, nsl * 512:(nsl + 1) * 512], self.ones_bf[0:1, :],
                                self.bo_row[0:1, nsl * 512:(nsl + 1) * 512], start=False, stop=True))
            t_op = inc(TEN, self.tM, lambda e:
                       e.matmul(self.PS0[:, 3 * 512:4 * 512], self.ones_bf[0:1, :],
                                self.bo_row[0:1, 3 * 512:4 * 512], start=False, stop=True))
            opmm[tc] = t_op
            wait(VEC, self.tM, t_op)
            self.waitt(VEC, xo_dma[tc])
            a0 = tc * H
            opev[tc] = self.chn(VEC, self.vM, lambda e, tc=tc, a0=a0: e.tensor_add(
                self.acc[:, a0:a0 + H], self.PS0[:, :], self.x_ot[0][:, :]))
            if tc + 1 < 8:
                wait(SYN, self.vM, opev[tc])
                xo_dma[tc + 1] = dma(SYN, self.dX, self.x_ot[0][:, :],
                                     d["x_own"][(tc + 1) * P:(tc + 2) * P, :])
            for gi in range(4):
                self.chn(VEC, self.vM, lambda e, a0=a0, gi=gi: e.bn_stats(
                    self.stats6[:, gi * 6:(gi + 1) * 6],
                    self.acc[:, a0 + gi * 512: a0 + (gi + 1) * 512]))
            mvv[tc] = self.chn(VEC, self.vM, lambda e: e.bn_aggr(
                self.mv2[:, :], self.stats6[:, :].rearrange("p (g x) -> p g x", x=3)))

        def stage_a2t(tc):
            a0 = tc * H
            wait(SCA, self.vM, mvv[tc])
            self.chn(SCA, self.sM, lambda e: e.activation(self.sc_a[:, :], self.mv2[:, 1:2], AF.Ln, bias=self.c_lne[:, 0:1]))
            s_r = self.chn(SCA, self.sM, lambda e: e.activation(self.sc_e[:, :], self.sc_a[:, :],
                                                                AF.Exp, scale=-0.5, bias=self.c_b0[:, 0:1]))
            wait(VEC, self.sM, s_r)
            v_sb = self.chn(VEC, self.vM, lambda e: e.scalar_tensor_tensor(
                self.sc_f[:, :], self.mv2[:, 0:1], -1.0, self.sc_e[:, :], ALU.mult, ALU.mult))
            wait(SCA, self.vM, v_sb)
            if tc >= 2:
                wait(SCA, self.tM, bstage1[tc - 2][1])
            s_h_d[tc] = self.chn(SCA, self.sM, lambda e, a0=a0, tc=tc: e.activation(
                self.h_tok2[tc % 2][:, :], self.acc[:, a0:a0 + H], AF.Identity,
                bias=self.sc_f[:, 0:1], scale=self.sc_e[:, 0:1]))
            if tc == 0 and _TAPS:
                self.wait(SYN, self.sM, s_h_d[tc])
                self.tap("htok0", self.h_tok2[0][:, :])

        def stage_b1(tc):
            wait(TEN, self.sM, s_h_d[tc])
            wait(TEN, self.vM, opev[tc + 1] if tc + 1 in opev else 0)
            if tc + 1 in aoev:
                wait(TEN, self.vM, aoev[tc + 1])
            if tc >= 1:
                wait(TEN, self.vM, bstage2[tc - 1][1])
                wait(TEN, self.sM, bstage2[tc - 1][0])
            s_ev = None
            for wave in range(2):
                if wave > 0:
                    wait(TEN, self.sM, s_ev)
                for kk in range(7):
                    k = wave * 8 + kk
                    pv = self.PS0b if kk < 4 else self.PS1b
                    bank = kk % 4
                    em(TEN, lambda e, k=k, tc=tc, pv=pv, bank=bank:
                       e.transpose(pv[:, bank * 1024: bank * 1024 + P],
                                   self.h_tok2[tc % 2][:, k * P:(k + 1) * P], self.ident_bf[:, :]))
                k = wave * 8 + 7
                t_h = inc(TEN, self.tM, lambda e, k=k, tc=tc:
                          e.transpose(self.PS1b[:, 3 * 1024: 3 * 1024 + P],
                                      self.h_tok2[tc % 2][:, k * P:(k + 1) * P], self.ident_bf[:, :]))
                wait(SCA, self.tM, t_h)
                for kk in range(8):
                    k = wave * 8 + kk
                    pv = self.PS0b if kk < 4 else self.PS1b
                    bank = kk % 4
                    s_ev = inc(SCA, self.sM, lambda e, k=k, kk=kk, tc=tc, pv=pv, bank=bank:
                               e.activation(self.hT[k][:, tc * P:(tc + 1) * P],
                                            pv[:, bank * 1024: bank * 1024 + P],
                                            AF.Identity, scale=self.rmsw[:, k:k + 1]))
            hT_done[tc] = s_ev
            bstage1[tc] = (s_ev, t_h)

        def stage_b2(tc):
            s_ev = bstage1[tc][0]
            wait(TEN, self.sM, s_ev)
            if tc + 2 in aoev:
                wait(TEN, self.vM, aoev[tc + 2])
            for k in range(15):
                em(TEN, lambda e, k=k, tc=tc:
                   e.matmul(self.PS1[:, 0:8], self.hT[k][:, tc * P:(tc + 1) * P],
                            self.router_sb[:, k * 8:(k + 1) * 8], start=(k == 0), stop=False))
            t_rt = inc(TEN, self.tM, lambda e, tc=tc:
                       e.matmul(self.PS1[:, 0:8], self.hT[15][:, tc * P:(tc + 1) * P],
                                self.router_sb[:, 15 * 8:16 * 8], start=False, stop=True))
            wait(VEC, self.tM, t_rt)
            v_nm = self.chn(VEC, self.vM, lambda e: e.tensor_reduce(
                self.sc_a2[:, :], self.PS1[:, 0:8], AX.X, ALU.max, negate=True))
            wait(SCA, self.vM, v_nm)
            s_pu = self.chn(SCA, self.sM, lambda e: e.activation(self.probs_u[:, :], self.PS1[:, 0:8],
                                                                 AF.Exp, bias=self.sc_a2[:, 0:1]))
            wait(VEC, self.sM, s_pu)
            if tc == 0 and _TAPS:
                self.wait(SYN, self.sM, s_pu)
                self.tap("probs0", self.probs_u[:, :])
            self.chn(VEC, self.vM, lambda e: e.max(out=self.m8[:, :], in_=self.probs_u[:, :]))
            self.chn(VEC, self.vM, lambda e: e.tensor_scalar(self.maskb[:, :], self.probs_u[:, :],
                                                             self.m8[:, 1:2], None, ALU.is_ge))
            self.chn(VEC, self.vM, lambda e: e.tensor_mul(self.cw[:, :], self.probs_u[:, :], self.maskb[:, :]))
            self.chn(VEC, self.vM, lambda e: e.tensor_add(self.sc_b2[:, :], self.m8[:, 0:1], self.m8[:, 1:2]))
            self.chn(VEC, self.vM, lambda e: e.reciprocal(self.sc_b2[:, :], self.sc_b2[:, :]))
            v_cw = self.chn(VEC, self.vM, lambda e: e.tensor_scalar(self.cw[:, :], self.cw[:, :],
                                                                    self.sc_b2[:, 0:1], None, ALU.mult))
            wait(TEN, self.vM, v_cw)
            t_c = inc(TEN, self.tM, lambda e: e.transpose(self.PS1[0:8, 1024:1024 + P],
                                                          self.cw[:, :], self.ident_f32[:, :]))
            wait(VEC, self.tM, t_c)
            v_ct = inc(VEC, self.vM, lambda e, tc=tc: e.tensor_copy(
                self.cT_sb[:, tc * P:(tc + 1) * P], self.PS1[0:8, 1024:1024 + P]))
            bstage2[tc] = (s_pu, v_ct)

        xo_dma[0] = dma(SYN, self.dX, self.x_ot[0][:, :], d["x_own"][0:P, :])
        stage_a1(0)
        stage_a1(1)
        stage_a2(0)
        stage_a2t(0)
        for tc in range(8):
            if tc + 2 < 8:
                stage_a1(tc + 2)
            if tc + 1 < 8:
                stage_a2(tc + 1)
            stage_b1(tc)
            if tc + 1 < 8:
                stage_a2t(tc + 1)
            stage_b2(tc)
        self.hT_all = hT_done[7]
        self.cT_all = self.vM.n

    def pc(self):
        em, wait, inc, dma = self.em, self.wait, self.inc, self.dma
        d = self.d
        guev = {}
        dnev = {}
        gu_mm = {}
        gu_dma = {}
        dn_dma = {}
        cb_ev = {}

        def gu_src(e_, m):
            if e_ == 8:
                return (d["swg_r"][m].rearrange("p k c -> p (k c)"),
                        d["swu_r"][m].rearrange("p k c -> p (k c)"))
            return (d["gate_r"][e_, m].rearrange("p k c -> p (k c)"),
                    d["up_r"][e_, m].rearrange("p k c -> p (k c)"))

        def emit_gu_dma(gm):
            if gm >= 9 * 8:
                return
            e_, m = divmod(gm, 8)
            if gm >= 2:
                pe, pm = divmod(gm - 2, 8)
                self.wait(GPS, self.tM, gu_mm[(pe, pm)])
            sg, su = gu_src(e_, m)
            tg = dma(GPS, self.dW, self.gbuf[m % 2][:, :], sg)
            tu = dma(GPS, self.dW, self.ubuf[m % 2][:, :], su)
            gu_dma[(e_, m)] = (tg, tu)

        emit_gu_dma(0)
        emit_gu_dma(1)
        dn_dma[0] = dma(GPS, self.dW,
                        self.dbuf[:, :].rearrange("p (m h) -> p m h", h=H),
                        d["down_w"][0].rearrange("(m p) h -> p m h", p=P))

        dn_last = None
        for e_ in range(9):
            shared = (e_ == 8)
            if not shared:
                if e_ >= 1:
                    wait(TEN, self.vM, dnev[(e_ - 1, 7)])
                if e_ == 0:
                    self.waitt(TEN, self.d_sel)
                em(TEN, lambda e, e_=e_:
                   e.matmul(self.PS1[:, 1024:1536],
                            self.sel_sb[0:8, e_ * P:(e_ + 1) * P],
                            self.cT_sb[0:8, 0:512], start=True, stop=True))
                t_cb = inc(TEN, self.tM, lambda e, e_=e_:
                           e.matmul(self.PS1[:, 1536:2048],
                                    self.sel_sb[0:8, e_ * P:(e_ + 1) * P],
                                    self.cT_sb[0:8, 512:1024], start=True, stop=True))
                wait(VEC, self.tM, t_cb)
                cb = self.c_bc[e_ % 2]
                cb_ev[e_] = inc(VEC, self.vM, lambda e, cb=cb:
                                e.tensor_copy(cb[:, :], self.PS1[:, 1024:2048]))
            for m in range(8):
                ps_g = self.PS0 if m % 2 == 0 else self.PS1
                self.waitt(TEN, gu_dma[(e_, m)][0])
                self.waitt(TEN, gu_dma[(e_, m)][1])
                if m >= 2:
                    wait(TEN, self.vM, guev[(e_, m - 2)])
                elif e_ > 0:
                    wait(TEN, self.vM, guev[(e_ - 1, 6 + m)])
                    wait(TEN, self.vM, dnev[(e_ - 1, 6 + m)])
                if m == 1 and not shared:
                    wait(TEN, self.vM, cb_ev[e_])
                gb, ub = self.gbuf[m % 2], self.ubuf[m % 2]
                for k in range(16):
                    for half in range(2):
                        em(TEN, lambda e, k=k, half=half, ps_g=ps_g, gb=gb:
                           e.matmul(ps_g[:, half * 512:(half + 1) * 512],
                                    gb[:, k * P:(k + 1) * P],
                                    self.hT[k][:, half * 512:(half + 1) * 512],
                                    start=(k == 0), stop=(k == 15)))
                for k in range(16):
                    for half in range(2):
                        if k == 15 and half == 1:
                            continue
                        em(TEN, lambda e, k=k, half=half, ps_g=ps_g, ub=ub:
                           e.matmul(ps_g[:, 1024 + half * 512: 1024 + (half + 1) * 512],
                                    ub[:, k * P:(k + 1) * P],
                                    self.hT[k][:, half * 512:(half + 1) * 512],
                                    start=(k == 0), stop=(k == 15)))
                gu_mm[(e_, m)] = inc(TEN, self.tM, lambda e, ps_g=ps_g, ub=ub:
                    e.matmul(ps_g[:, 1536:2048], ub[:, 15 * P:16 * P],
                             self.hT[15][:, 512:1024], start=False, stop=True))
                emit_gu_dma(e_ * 8 + m + 2)
                wait(SCA, self.tM, gu_mm[(e_, m)])
                t1 = self.t1b[m % 2]
                s_si = inc(SCA, self.sM, lambda e, ps_g=ps_g, t1=t1:
                           e.activation(t1[:, :], ps_g[:, 0:1024], AF.Silu))
                wait(VEC, self.sM, s_si)
                if shared:
                    guev[(e_, m)] = self.chn(VEC, self.vM, lambda e, ps_g=ps_g, t1=t1, m=m:
                                             e.tensor_mul(self.a_fm[m][:, :], t1[:, :],
                                                          ps_g[:, 1024:2048]))
                else:
                    if m == 0:
                        wait(VEC, self.vM, cb_ev[e_])
                    self.chn(VEC, self.vM, lambda e, ps_g=ps_g, t1=t1:
                             e.tensor_mul(self.t2b[:, :], t1[:, :], ps_g[:, 1024:2048]))
                    guev[(e_, m)] = self.chn(VEC, self.vM, lambda e, m=m, cb=cb:
                                             e.tensor_mul(self.a_fm[m][:, :], self.t2b[:, :], cb[:, :]))
            self.waitt(TEN, dn_dma[e_])
            wait(TEN, self.vM, guev[(e_, 7)])
            for tc in range(8):
                ps_d = self.PS0 if tc % 2 == 0 else self.PS1
                if tc >= 2:
                    wait(TEN, self.vM, dnev[(e_, tc - 2)])
                else:
                    wait(TEN, self.vM, guev[(e_, 6 + tc)])
                for nsl in range(4):
                    for m in range(8):
                        if nsl == 3 and m == 7:
                            continue
                        em(TEN, lambda e, m=m, nsl=nsl, tc=tc, ps_d=ps_d:
                           e.matmul(ps_d[:, nsl * 512:(nsl + 1) * 512],
                                    self.a_fm[m][:, tc * P:(tc + 1) * P],
                                    self.dbuf[:, m * H + nsl * 512: m * H + (nsl + 1) * 512],
                                    start=(m == 0), stop=(m == 7)))
                t_dn = inc(TEN, self.tM, lambda e, tc=tc, ps_d=ps_d:
                           e.matmul(ps_d[:, 3 * 512:4 * 512],
                                    self.a_fm[7][:, tc * P:(tc + 1) * P],
                                    self.dbuf[:, 7 * H + 3 * 512: 7 * H + 4 * 512],
                                    start=False, stop=True))
                dn_last = t_dn
                wait(VEC, self.tM, t_dn)
                a0 = tc * H
                dnev[(e_, tc)] = self.chn(VEC, self.vM, lambda e, a0=a0, ps_d=ps_d: e.tensor_add(
                    self.acc[:, a0:a0 + H], ps_d[:, :], self.acc[:, a0:a0 + H]))
            if e_ < 8:
                self.wait(GPS, self.tM, dn_last)
                nxt = d["swd_w"] if e_ == 7 else d["down_w"][e_ + 1]
                dn_dma[e_ + 1] = dma(GPS, self.dW,
                                     self.dbuf[:, :].rearrange("p (m h) -> p m h", h=H),
                                     nxt.rearrange("(m p) h -> p m h", p=P))
        self.acc_done = dnev[(8, 7)]

    def pd(self):
        wait, dma = self.wait, self.dma
        wait(SYN, self.vM, self.acc_done)
        tgt = dma(SYN, self.dX, self.d["out"].rearrange("(tc p) h -> p tc h", p=P),
                  self.acc[:, :].rearrange("p (tc h) -> p tc h", h=H))
        self.waitt(SYN, tgt)

    def build(self, stop_after=None):
        self.alloc_all()
        self._materialize()
        self.h_tok2 = self.h_tok2b
        phases = [
            ("p0", self.p0),
            ("p1a", self.p1a),
            ("p1bk", lambda: self.build_side(S, self.kv_latT, self.W4, self.bkup,
                                             self.ck, self.sk, self.K_h, self.tok_w4)),
            ("p1bq", lambda: self.build_side(NTOK, self.q_latT, self.W3, self.bqup,
                                             self.cq, self.sq, self.q_h, self.tok_w3)),
            ("p1c", self.p1c),
            ("bound", self.bound_prep),
            ("pa", self.pa),
            ("pb", self.pb),
            ("pc", self.pc),
        ]
        stopped = False
        for name, fn in phases:
            fn()
            if name in ("p0", "p1a", "p1bk", "p1bq", "bound", "pa", "pb"):
                self.barrier()
            if _TAPS:
                if name == "p1a":
                    self.tap("kvlat0", self.kv_latT[0][:, :])
                    self.tap("qlat0", self.q_latT[0][:, :])
                elif name == "p1bk":
                    self.tap("K0", self.K_h[0][:, :])
                    self.tap("K8", self.K_h[8][:, :])
                elif name == "p1bq":
                    self.tap("q0", self.q_h[0][:, :])
                    self.tap("q12", self.q_h[12][:, :])
                elif name == "p1c":
                    pass
                elif name == "bound":
                    self.tap("V0", self.V_h[0][:, :])
                    self.tap("ebias", self.ebias[:, :])
                elif name == "pa":
                    self.tap("ao0", self.ao[0][:, :])
                elif name == "pb":
                    self.tap("acc_pb", self.acc[:, :])
                    self.tap("hT0", self.hT[0][:, :])
                    self.tap("cT", self.cT_sb[:, :])
            if name == stop_after:
                stopped = True
                break
        if stopped:
            self.barrier()
            tgt = self.dma(SYN, self.dX,
                           self.d["out"].rearrange("(tc p) h -> p tc h", p=P),
                           self.acc[:, :].rearrange("p (tc h) -> p tc h", h=H))
            self.wait(SYN, self.dX, tgt)
        else:
            self.pd()

        nc = self.nc
        with nc.Block() as block:
            @block.tensor
            def _(e):
                for f in self.streams[TEN]:
                    f(e)

            @block.vector
            def _(e):
                for f in self.streams[VEC]:
                    f(e)

            @block.scalar
            def _(e):
                for f in self.streams[SCA]:
                    f(e)

            @block.sync
            def _(e):
                for f in self.streams[SYN]:
                    f(e)

            @block.gpsimd
            def _(e):
                for f in self.streams[GPS]:
                    f(e)
        return nc


def _host_inputs(args):
    x = np.ascontiguousarray(args["L0_attn_hidden_states"], dtype=np.float32)
    com = {
        "wqdT": np.ascontiguousarray(args["L0_attn_Wq_down"].T),
        "wkvdT": np.ascontiguousarray(args["L0_attn_Wkv_down"].T),
        "wqupT": np.ascontiguousarray(args["L0_attn_Wq_up"].T),
        "wkupT": np.ascontiguousarray(args["L0_attn_Wk_up"].T),
        "wvupT": np.ascontiguousarray(args["L0_attn_Wv_up"].T),
        "woT": np.ascontiguousarray(args["L0_attn_Wo"].T),
        "bqd": np.ascontiguousarray(args["L0_attn_bq_down"].reshape(4, P).T),
        "bkvd": np.ascontiguousarray(args["L0_attn_bkv_down"].reshape(4, P).T),
        "bqup": np.ascontiguousarray(args["L0_attn_bq_up"].reshape(8, P).T),
        "bkup": np.ascontiguousarray(args["L0_attn_bk_up"].reshape(8, P).T),
        "bv": np.ascontiguousarray(args["L0_attn_bv_up"].reshape(1, H)),
        "bo": np.ascontiguousarray(args["L0_attn_bo"].reshape(1, H)),
        "rmsw": np.ascontiguousarray(args["L0_ffn_rms_weight"].reshape(16, P).T),
        "router": np.ascontiguousarray(args["L0_ffn_router_w"]),
        "gate_r": np.ascontiguousarray(
            args["L0_ffn_w_gate"].reshape(NE, 16, P, 8, P).transpose(0, 3, 2, 1, 4)),
        "up_r": np.ascontiguousarray(
            args["L0_ffn_w_up"].reshape(NE, 16, P, 8, P).transpose(0, 3, 2, 1, 4)),
        "down_w": np.ascontiguousarray(args["L0_ffn_w_down"]),
        "swg_r": np.ascontiguousarray(
            args["L0_ffn_sw_gate"].reshape(16, P, 8, P).transpose(2, 1, 0, 3)),
        "swu_r": np.ascontiguousarray(
            args["L0_ffn_sw_up"].reshape(16, P, 8, P).transpose(2, 1, 0, 3)),
        "swd_w": np.ascontiguousarray(args["L0_ffn_sw_down"]),
    }
    esel = np.zeros((8, NE * P), dtype=np.float32)
    for e in range(NE):
        esel[e, e * P:(e + 1) * P] = 1.0
    com["esel"] = esel
    for k, v in com.items():
        com[k] = np.ascontiguousarray(v, dtype=np.float32)
    in_maps = []
    for c in range(8):
        b, qh = c // 2, c % 2
        m = dict(com)
        m["x_full"] = np.ascontiguousarray(x[b])
        m["x_own"] = np.ascontiguousarray(x[b, qh * NTOK:(qh + 1) * NTOK])
        m["pos0"] = np.full((P, 1), float(qh * NTOK), dtype=np.float32)
        in_maps.append(m)
    return in_maps


_NC_CACHE = [None]


def kernel(**inputs):
    in_maps = _host_inputs(inputs)
    if _NC_CACHE[0] is None:
        _NC_CACHE[0] = KB().build()
    nc = _NC_CACHE[0]
    res = run_bass_kernel_spmd(nc, in_maps, core_ids=list(range(8)))
    if _TAPS:
        global _LAST_TAPS
        _LAST_TAPS = [{k[4:]: np.asarray(v) for k, v in r.items() if k.startswith("tap_")}
                      for r in res.results]
    out = np.empty((4, S, H), dtype=np.float32)
    for c in range(8):
        b, qh = c // 2, c % 2
        out[b, qh * NTOK:(qh + 1) * NTOK] = np.asarray(res.results[c]["out"])
    return out
